# version 10
# baseline (speedup 1.0000x reference)
import numpy as np
import concourse.bass as bass
import concourse.mybir as mybir
from concourse.bass_utils import run_bass_kernel_spmd
from contextlib import ExitStack

F32 = mybir.dt.float32
BF16 = mybir.dt.bfloat16
I32 = mybir.dt.int32
AF = mybir.ActivationFunctionType
ALU = mybir.AluOpType
AX = mybir.AxisListType.X

S = 4096
C = 1024
NH = 16
DH = 64
NT = S // 512

NRF = 256
MS = 256
GAMMAS = (0.5, 1.0, 2.0)
NLM = 64

DEPTH = 6
WINDOW = 256
DI = 2730
DIP = 2816
NJC = DIP // 128
QK_SCALE = 8.0
NEG = -1.0e30
LN16 = float(np.log(16.0))
TWO_PI = float(2.0 * np.pi)
SQ2M = float(np.sqrt(2.0 / MS))


def host_constants():
    cs = {}
    cs["eye"] = np.eye(128, dtype=np.float32)

    shuf = np.zeros((128, 128), np.float32)
    for blk in range(2):
        for m in range(64):
            shuf[blk * 64 + (m + 32) % 64, blk * 64 + m] = 1.0
    cs["shuf"] = shuf

    bd = np.zeros((128, 128), np.float32)
    for c in range(8):
        for p in range(128):
            bd[p, 16 * c + 2 * c + p // 64] = 1.0
    cs["bdh16"] = bd

    s2 = np.zeros((16, 8 * 128), np.float32)
    for c in range(8):
        for p in range(128):
            s2[2 * c + p // 64, 128 * c + p] = 1.0
    cs["sel2h"] = s2

    s16 = np.zeros((16, 16 * 128), np.float32)
    for h in range(NH):
        s16[h, 128 * h:128 * (h + 1)] = 1.0
    cs["sel16b"] = s16

    r = np.arange(128)
    j = np.arange(384)[None, :]
    d = r[:, None]
    cs["mq_full"] = np.where((j >= d) & (j <= d + 256), 0.0, NEG).astype(np.float32)
    j2 = np.arange(256)[None, :]
    cs["mq_s1"] = np.where((j2 >= 128 + d - 256) & (j2 <= 128 + d), 0.0, NEG).astype(np.float32)
    j1 = np.arange(128)[None, :]
    cs["mq_s0"] = np.where(j1 <= d, 0.0, NEG).astype(np.float32)

    cs["ones1x"] = np.ones((1, 128), np.float32)
    return cs


def host_tables():
    tb = {}
    inv = 1.0 / 10000.0 ** (np.arange(0, DH, 2, dtype=np.float64) / DH)
    ang = np.arange(S, dtype=np.float64)[None, :] * inv[:, None]
    cos64 = np.cos(np.concatenate([ang, ang], 0))
    sin64 = np.sin(np.concatenate([ang, ang], 0))
    sgn = np.concatenate([-np.ones(32), np.ones(32)])[:, None]
    tb["pr_cos"] = np.tile(cos64, (2, 1)).astype(np.float32)
    tb["pr_sin"] = np.tile(sin64 * sgn, (2, 1)).astype(np.float32)

    half = np.arange(0, DH, 2, dtype=np.float64)
    invf = 1.0 / 10000.0 ** (half / DH)
    t = np.arange(S, dtype=np.float64)
    fr = t[None, :] * invf[:, None]
    fr64 = np.concatenate([fr, fr], 0)
    base = ((half + 0.4 * DH) / (1.4 * DH))[:, None]
    power = ((t - S // 2) / (WINDOW // 2))[None, :]
    scale = np.concatenate([base ** power, base ** power], 0)
    cosf = np.cos(fr64); sinf = np.sin(fr64)
    tb["fx_cq"] = np.tile(cosf * scale, (2, 1)).astype(np.float32)
    tb["fx_sq"] = np.tile(sinf * scale * sgn, (2, 1)).astype(np.float32)
    tb["fx_ck"] = np.tile(cosf / scale, (2, 1)).astype(np.float32)
    tb["fx_sk"] = np.tile(sinf / scale * sgn, (2, 1)).astype(np.float32)
    return tb


CAST_CONSTS = {"shuf", "bdh16", "ones1x", "sel16b", "sel2h",
               "mq_full", "mq_s1", "mq_s0"}


class Pl:
    ENGINES = ("sync", "gpsimd", "vector", "scalar", "tensor")

    ENGINE_SEMS = {"v": "vector", "s": "scalar", "pe": "tensor", "g": "gpsimd"}

    def __init__(self, nc, st):
        self.nc = nc
        self.st = st
        self.stream = {e: [] for e in self.ENGINES}
        self.sems = {}
        self.counts = {}
        self.saved = {}
        self.chan_k = {}
        self.waited = {e: {} for e in self.ENGINES}

    def sem(self, name):
        if name not in self.sems:
            self.sems[name] = self.st.enter_context(self.nc.semaphore(name))
            self.counts[name] = 0
        return name

    def op(self, eng, fn, wait=(), inc=None, inc_by=1):
        flat = []

        def _fl(t):
            if t is None:
                return
            if isinstance(t, (list, tuple)) and (len(t) == 0 or not isinstance(t[0], str)):
                for x in t:
                    _fl(x)
            else:
                flat.append(t)
        for t in wait:
            _fl(t)
        best = {}
        for s, v in flat:
            if v > best.get(s, 0):
                best[s] = v
        wed = self.waited[eng]
        out = []
        for s, v in best.items():
            if eng == "tensor" and s == "pe":
                continue
            if wed.get(s, 0) >= v:
                continue
            wed[s] = v
            out.append((s, v))
        wait = tuple(out)
        for s, _ in wait:
            self.sem(s)
        sems = self.sems
        tok = None
        if inc is not None:
            self.sem(inc)
            self.counts[inc] += inc_by
            tok = (inc, self.counts[inc])

        def run(e, fn=fn, wait=wait, tok=tok, inc_by=inc_by):
            for s, v in wait:
                if v > 0:
                    e.wait_ge(sems[s], v)
            ins = fn(e)
            if tok is not None and ins is not None:
                ins.then_inc(sems[tok[0]], inc_by)
        self.stream[eng].append(run)
        return tok

    def dma(self, eng, out, in_, chan, wait=(), slots=4, **kw):
        k = self.chan_k.setdefault(chan, 0)
        self.chan_k[chan] = k + 1
        sem = f"{chan}.{k % slots}"
        self.sem(sem)
        prior = self.counts[sem]
        w = list(wait) + ([(sem, prior)] if prior > 0 else [])
        return self.op(eng, lambda e: e.dma_start(out=out, in_=in_, **kw), w, sem, 16)

    def chan_toks(self, chan, slots=4):
        return [self.tok(f"{chan}.{i}") for i in range(slots)
                if f"{chan}.{i}" in self.counts]

    def tok(self, sem):
        self.sem(sem)
        return (sem, self.counts[sem])

    def save(self, key, tok):
        self.saved[key] = tok

    def get(self, key):
        return self.saved.get(key)

    def buse(self, key, tok):
        self.saved["bk_" + key] = tok

    def bwait(self, key):
        return self.saved.get("bk_" + key)

    def final_drain(self):
        toks = [(s, c) for s, c in self.counts.items() if c > 0]
        sems = self.sems

        def run(e, toks=toks):
            for s, v in toks:
                e.wait_ge(sems[s], v)
        self.stream["sync"].append(run)

    def emit(self, block):
        for eng in self.ENGINES:
            ops = self.stream[eng]
            if not ops:
                continue
            dec = getattr(block, eng)

            def section(e, ops=ops):
                for f in ops:
                    f(e)
            section.__name__ = eng
            dec(section)


def em_mm_group(e, bank_ap, pairs):
    n = len(pairs)
    last = None
    for q, (l, r) in enumerate(pairs):
        last = e.matmul(bank_ap, l, r, start=(q == 0), stop=(q == n - 1))
    return last


def build_program(dbg=()):
    nc = bass.Bass("TRN2", target_bir_lowering=False)
    din = {}

    def inp(name, shape):
        din[name] = nc.dram_tensor(name, list(shape), F32, kind="ExternalInput").ap()
        return din[name]

    x_h = inp("x", (S, C))
    inp("conv_k", (3, 3, 256, C)); inp("conv_b", (3, C))
    inp("rff_w", (3, 3, C, MS)); inp("rff_b", (3, 3, MS))
    inp("proj_w", (3, 3 * MS, C)); inp("proj_b", (3, C))
    inp("omega", (DH, NRF)); inp("landmarks", (NLM, DH))
    inp("out_w", (C, C)); inp("out_b", (C,))
    inp("Wqkv", (DEPTH, C, 3 * C)); inp("q_scale", (DEPTH, DH)); inp("k_scale", (DEPTH, DH))
    inp("Wo", (DEPTH, C, C)); inp("ln_g", (DEPTH, C)); inp("ln_b", (DEPTH, C))
    inp("W1", (DEPTH, C, 2 * DI)); inp("b1", (DEPTH, 2 * DI))
    inp("W2", (DEPTH, DI, C)); inp("b2", (DEPTH, C))
    HC = host_constants()
    for k, v in HC.items():
        inp("c_" + k, v.shape)
    HT = host_tables()
    for k, v in HT.items():
        din["t_" + k] = nc.dram_tensor("t_" + k, list(v.shape), BF16,
                                       kind="ExternalInput").ap()

    y_h = nc.dram_tensor("y", [S, C], F32, kind="ExternalOutput").ap()

    dbg_out = {}

    def tap(name, shape):
        if name in dbg:
            dbg_out[name] = nc.dram_tensor("dbg_" + name, list(shape), F32,
                                           kind="ExternalOutput").ap()
            return dbg_out[name]
        return None

    xn_cm_d = nc.dram_tensor("xn_cm_d", [8, 128, S], F32, kind="Internal").ap()
    xf_a = nc.dram_tensor("xf_a", [8, 128, S], F32, kind="Internal").ap()
    xf_b = nc.dram_tensor("xf_b", [8, 128, S], F32, kind="Internal").ap()
    acc_d = nc.dram_tensor("acc_d", [S, C], F32, kind="Internal").ap()
    bounce = nc.dram_tensor("bounce", [4, 16384], F32, kind="Internal").ap()
    w2bf_d = nc.dram_tensor("w2bf_d", [DEPTH, DI, C], BF16, kind="Internal").ap()
    w1gbf_d = nc.dram_tensor("w1gbf_d", [DEPTH, C, DI], BF16, kind="Internal").ap()

    with ExitStack() as st:
        pl = Pl(nc, st)

        def mksb(stk, side=None):
            def sb(name, shape, dt=F32):
                kw = {"side": side} if side else {}
                return stk.enter_context(nc.sbuf_tensor(name, shape, dt, **kw))
            return sb
        sb = mksb(st)

        def mkpsum(name, shape, dt=F32):
            return st.enter_context(nc.psum_tensor(name, shape, dt))

        CS = {}
        for k, v in HC.items():
            CS[k] = sb("cs_" + k, list(v.shape), BF16 if k in CAST_CONSTS else F32)
        const_pool = sb("const_pool", [128, 8], F32)
        for i, val in enumerate([-LN16, 1e-6, 1e-12, 0.5, 2.0, -1.0]):
            nc.const_aps.register(F32, val, const_pool[:, i:i + 1])
            pl.op("gpsimd", lambda e, i=i, val=val: e.memset(const_pool[:, i:i + 1], val), inc="g")
        for k in HC:
            if k in CAST_CONSTS:
                pl.dma("gpsimd", CS[k][:], din["c_" + k], "d_cg")
            else:
                pl.dma("sync", CS[k][:], din["c_" + k], "d_cs")
        consts_tok = pl.chan_toks("d_cg") + pl.chan_toks("d_cs") + [pl.tok("g")]

        PS = [mkpsum(f"PS{i}", [128, 512]) for i in range(6)]
        PK = [mkpsum(f"PK{i}", [128, 512]) for i in range(2)]

        with ExitStack() as st0:
            sb0 = mksb(st0)
            p0_x = [sb0(f"p0_x{i}", [128, C]) for i in range(3)]
            p0_stat = [sb0(f"p0_st{i}", [128, 6]) for i in range(3)]
            p0_sq = sb0("p0_sq", [128, C], BF16)
            p0_xn = [sb0(f"p0_xn{i}", [128, C]) for i in range(2)]
            p0_cm = [sb0(f"p0_cm{i}", [128, 128]) for i in range(4)]

            n_tok = S // 128
            for tt in range(n_tok):
                bx = p0_x[tt % 3]
                stt_ = p0_stat[tt % 3]
                ld = pl.dma("sync", bx[:], x_h[tt * 128:(tt + 1) * 128, :], "d_x",
                            wait=[pl.get(f"p0x_done_{tt - 3}")])
                t1 = pl.op("vector", lambda e, bx=bx, stt_=stt_: e.reduce_sum(
                    out=stt_[:, 0:1], in_=bx[:], axis=AX),
                    wait=[ld, pl.get(f"p0st_done_{tt - 3}")], inc="v")
                t2 = pl.op("scalar", lambda e, bx=bx, stt_=stt_: e.activation(
                    out=p0_sq[:], in_=bx[:], func=AF.Square, accum_out=stt_[:, 1:2]),
                    wait=[ld, t1], inc="s")
                t3 = pl.op("vector", lambda e, stt_=stt_: e.tensor_scalar(
                    out=stt_[:, 2:3], in0=stt_[:, 0:1], scalar1=1.0 / C, scalar2=None,
                    op0=ALU.mult), wait=[t1], inc="v")
                t4 = pl.op("vector", lambda e, stt_=stt_: e.tensor_scalar(
                    out=stt_[:, 3:4], in0=stt_[:, 2:3], scalar1=stt_[:, 2:3], scalar2=None,
                    op0=ALU.mult), wait=[t3], inc="v")
                t5 = pl.op("vector", lambda e, stt_=stt_: e.scalar_tensor_tensor(
                    out=stt_[:, 3:4], in0=stt_[:, 1:2], scalar=1.0 / C, in1=stt_[:, 3:4],
                    op0=ALU.mult, op1=ALU.subtract), wait=[t2, t4], inc="v")
                t6 = pl.op("vector", lambda e, stt_=stt_: e.tensor_scalar(
                    out=stt_[:, 3:4], in0=stt_[:, 3:4], scalar1=1.0, scalar2=1e-5,
                    op0=ALU.mult, op1=ALU.add), wait=[t5], inc="v")
                t7 = pl.op("vector", lambda e, stt_=stt_: e.reciprocal(
                    out=stt_[:, 4:5], in_=stt_[:, 3:4]), wait=[t6], inc="v")
                t8 = pl.op("scalar", lambda e, stt_=stt_: e.activation(
                    out=stt_[:, 4:5], in_=stt_[:, 4:5], func=AF.Sqrt), wait=[t7], inc="s")
                t9 = pl.op("vector", lambda e, stt_=stt_: e.tensor_scalar(
                    out=stt_[:, 5:6], in0=stt_[:, 2:3], scalar1=stt_[:, 4:5], scalar2=-1.0,
                    op0=ALU.mult, op1=ALU.mult), wait=[t8], inc="v")
                bxn = p0_xn[tt % 2]
                t10 = pl.op("vector", lambda e, bx=bx, stt_=stt_, bxn=bxn: e.tensor_scalar(
                    out=bxn[:], in0=bx[:], scalar1=stt_[:, 4:5], scalar2=stt_[:, 5:6],
                    op0=ALU.mult, op1=ALU.add),
                    wait=[t9, pl.get(f"p0xn_done_{tt - 2}")], inc="v")
                pl.save(f"p0x_done_{tt}", t10)
                pl.save(f"p0st_done_{tt}", t10)
                trs = []
                for c in range(8):
                    bk = f"PS{c % 4}"
                    bank = PS[c % 4]
                    tr = pl.op("tensor", lambda e, bxn=bxn, c=c, bank=bank: e.transpose(
                        out=bank[:, 0:128], in_=bxn[:, c * 128:(c + 1) * 128],
                        identity=CS["eye"][:]),
                        wait=[t10, pl.bwait(bk)] + consts_tok, inc="pe")
                    stg = p0_cm[c % 4]
                    e2 = pl.op("scalar", lambda e, bank=bank, stg=stg: e.copy(
                        out=stg[:], in_=bank[:, 0:128]),
                        wait=[tr, pl.get(f"p0cmd_{tt}_{c - 4}"), pl.get(f"p0cmd_{tt - 1}_{c + 4}")],
                        inc="s")
                    pl.buse(bk, e2)
                    d2 = pl.dma("sync", xn_cm_d[c, :, tt * 128:(tt + 1) * 128], stg[:], "d_xn",
                                wait=[e2], slots=8)
                    pl.save(f"p0cmd_{tt}_{c}", d2)
                    trs.append(tr)
                pl.save(f"p0xn_done_{tt}", trs[-1])
            p0_done = [pl.tok("v"), pl.tok("s"), pl.tok("pe")] + pl.chan_toks("d_xn", 8)

        t_xn = tap("xn_cm", (8, 128, S))
        if t_xn is not None:
            pl.dma("sync", t_xn, xn_cm_d, "d_dbg", wait=p0_done)

        with ExitStack() as stp:
            sbp = mksb(stp)

            slotA = {"cw": sbp("cwA", [128, 6 * 1024], BF16),
                     "rw": sbp("rwA", [128, 8 * 768], BF16),
                     "pw": sbp("pwA", [128, 6 * 1024], BF16)}
            st1 = ExitStack()
            sb1 = mksb(st1, side="right")
            slotB = {"cw": sb1("cwB", [128, 6 * 1024], BF16),
                     "rw": sb1("rwB", [128, 8 * 768], BF16),
                     "pw": sb1("pwB", [128, 6 * 1024], BF16)}

            def load_proj_weights(slot, i, wait):
                tks = []
                for t3 in range(3):
                    for u in range(2):
                        tks.append(pl.dma("gpsimd", slot["cw"][:, (t3 * 2 + u) * 1024:(t3 * 2 + u + 1) * 1024],
                                          din["conv_k"][i, t3, u * 128:(u + 1) * 128, :], "d_w",
                                          wait=wait, slots=6))
                for kc in range(8):
                    for g in range(3):
                        tks.append(pl.dma("gpsimd", slot["rw"][:, kc * 768 + g * 256: kc * 768 + (g + 1) * 256],
                                          din["rff_w"][i, g, kc * 128:(kc + 1) * 128, :], "d_w",
                                          wait=wait, slots=6))
                pwt = []
                for fc in range(6):
                    pwt.append(pl.dma("gpsimd", slot["pw"][:, fc * 1024:(fc + 1) * 1024],
                                      din["proj_w"][i, fc * 128:(fc + 1) * 128, :], "d_w",
                                      wait=wait, slots=6))
                tks += pwt
                tks.append(pl.op("vector", lambda e, slot=slot: e.tensor_scalar(
                    out=slot["pw"][:], in0=slot["pw"][:], scalar1=SQ2M, scalar2=None,
                    op0=ALU.mult), wait=pwt, inc="v"))
                return tks

            wA_tok = load_proj_weights(slotA, 1, p0_done)
            wB_tok = load_proj_weights(slotB, 2, p0_done)

            ombf = sbp("ombf", [128, NRF], BF16)
            om_tok1 = pl.dma("gpsimd", ombf[0:64, :], din["omega"], "d_w", wait=p0_done)
            om_tok2 = pl.dma("gpsimd", ombf[64:128, :], din["omega"], "d_w", wait=p0_done)
            om_tok = [om_tok1, om_tok2]

            def load_colvec(tag, src2d, ncol, wait=()):
                stage = sbp(f"lcv_s_{tag}", [ncol, 128])
                out = sbp(f"lcv_{tag}", [128, ncol])
                d = pl.dma("sync", stage[:], src2d, "d_lcv", wait=list(wait) + p0_done)
                tr = pl.op("tensor", lambda e: e.transpose(
                    out=PS[5][:, 0:ncol], in_=stage[:], identity=CS["eye"][0:ncol, 0:ncol]),
                    wait=[d, pl.bwait("PS5")] + consts_tok, inc="pe")
                ev = pl.op("vector", lambda e: e.tensor_copy(out=out[:], in_=PS[5][:, 0:ncol]),
                           wait=[tr], inc="v")
                pl.buse("PS5", ev)
                return out, ev

            cb_col = {}; pb_col = {}; rb_col = {}
            for i in (0, 1, 2):
                cb_col[i] = load_colvec(f"cb{i}", din["conv_b"][i, :].rearrange("(a p) -> a p", p=128), 8)
                pb_col[i] = load_colvec(f"pb{i}", din["proj_b"][i, :].rearrange("(a p) -> a p", p=128), 8)
                col, tokc = load_colvec(f"rb{i}", din["rff_b"][i].rearrange("g (a p) -> (g a) p", p=128), 6)
                tr = pl.op("vector", lambda e, col=col: e.tensor_scalar(
                    out=col[:], in0=col[:], scalar1=1.0 / TWO_PI, scalar2=0.25,
                    op0=ALU.mult, op1=ALU.add), wait=[tokc], inc="v")
                rb_col[i] = (col, tr)
            outb_col = load_colvec("outb", din["out_b"].rearrange("(a p) -> a p", p=128), 8)

            lam_tok_sb = sbp("lam_tok", [NLM, DH])
            lam2 = sbp("lam2", [128, 128], BF16)
            lamn = sbp("lamn", [128, 1])
            lsq = sbp("lsq", [NLM, DH], BF16)
            lam_d = pl.dma("sync", lam_tok_sb[:], din["landmarks"], "d_lcv", wait=p0_done)
            g0 = pl.op("gpsimd", lambda e: e.memset(lam2[:], 0.0), wait=p0_done, inc="g")
            lt = pl.op("tensor", lambda e: e.transpose(
                out=PS[5][0:64, 0:64], in_=lam_tok_sb[:], identity=CS["eye"][0:64, 0:64]),
                wait=[lam_d, pl.bwait("PS5")] + consts_tok, inc="pe")
            l1 = pl.op("vector", lambda e: e.tensor_scalar(
                out=lam2[0:64, 0:64], in0=PS[5][0:64, 0:64], scalar1=2.0 / 64.0, scalar2=None,
                op0=ALU.mult), wait=[lt, g0], inc="v")
            l2 = pl.op("vector", lambda e: e.tensor_scalar(
                out=lam2[64:128, 64:128], in0=PS[5][0:64, 0:64], scalar1=2.0 / 64.0, scalar2=None,
                op0=ALU.mult), wait=[l1], inc="v")
            pl.buse("PS5", l2)
            l3 = pl.op("scalar", lambda e: e.activation(out=lsq[:], in_=lam_tok_sb[:], func=AF.Square),
                       wait=[lam_d], inc="s")
            l4 = pl.op("vector", lambda e: e.reduce_sum(out=lamn[0:64, 0:1], in_=lsq[:], axis=AX),
                       wait=[l3], inc="v")
            l5 = pl.op("vector", lambda e: e.tensor_scalar(
                out=lamn[0:64, 0:1], in0=lamn[0:64, 0:1], scalar1=-1.0 / 64.0, scalar2=None,
                op0=ALU.mult), wait=[l4], inc="v")
            l6 = pl.op("vector", lambda e: e.tensor_copy(out=lamn[64:128, 0:1], in_=lamn[0:64, 0:1]),
                       wait=[l5], inc="v")
            lam_ready = [l2, l6]

            xn_t = [sbp(f"xn_t{i}", [128, 8 * 514], BF16) for i in range(2)]
            h_bf = sbp("h_bf", [128, 4096], BF16)
            f_bf = sbp("f_bf", [128, 3072], BF16)
            proj_sb = sbp("proj_sb", [128, 4096], BF16)
            r_bf = sbp("r_bf", [128, 4096], BF16)
            sq_bf = sbp("sq_bf", [128, 2 * 512], BF16)
            rr16 = sbp("rr16", [16, 512], BF16)
            n16 = sbp("n16", [16, 512])
            n16q = sbp("n16q", [16, 512], BF16)
            kncols = sbp("kncols", [128, 64])
            kp_bf = sbp("kp_bf", [128, 512], BF16)
            qp_bf = sbp("qp_bf", [128, 2 * 1024], BF16)
            kvT_bf = sbp("kvT_bf", [128, 16 * 130], BF16)
            zt_sb = sbp("zt_sb", [128, 2 * 512])
            zr_sb = sbp("zr_sb", [128, 2 * 512])
            zi_sb = sbp("zi_sb", [128, 2 * 512], I32)
            rt1 = sbp("rt1", [128, 2 * 512], BF16)
            rt2 = sbp("rt2", [128, 2 * 512], BF16)
            tabc = [sbp("tabc0", [128, 512], BF16)]
            tabs = [sbp("tabs0", [128, 512], BF16)]
            pbrow = sbp("pbrow", [1, 1024], BF16)
            pbrow_tok = pl.dma("gpsimd", pbrow[:], din["proj_b"][2:3, :], "d_w", wait=p0_done)

            v_aug = sb1("v_aug", [128, 4160], BF16)
            kv_acc = sb1("kv_acc", [65, 16 * 256])

            va_ones = pl.op("gpsimd", lambda e: e.memset(
                v_aug[:].rearrange("p (h d) -> p h d", d=65)[:, :, 64:65], 1.0), inc="g")

            def emit_precast(n):
                pass

            def load_xn_tile(tile):
                buf = xn_t[tile % 2]
                prev = [pl.get(f"xn_used_{tile - 2}")]
                toks = []
                for u in range(8):
                    lo = tile * 512 - 1
                    ncols = 514
                    dst0 = 514 * u
                    if tile == 0:
                        toks.append(pl.op("gpsimd", lambda e, dst0=dst0, buf=buf: e.memset(
                            buf[:, dst0:dst0 + 1], 0.0), wait=prev, inc="g"))
                        lo = 0; ncols = 513; dst0 += 1
                    if tile == NT - 1:
                        toks.append(pl.op("gpsimd", lambda e, u=u, buf=buf: e.memset(
                            buf[:, 514 * u + 513:514 * u + 514], 0.0), wait=prev, inc="g"))
                        ncols -= 1
                    toks.append(pl.dma("gpsimd", buf[:, dst0:dst0 + ncols],
                                       xn_cm_d[u, :, lo:lo + ncols], "d_xt",
                                       wait=prev + p0_done, slots=4))
                return buf, toks

            def load_tab(tile, cos_name, sin_name):
                s0 = tile * 512
                tc, ts_ = tabc[0], tabs[0]
                tw = [pl.get(f"tab_used_{tile - 1}")] + (p0_done if tile == 0 else [])
                d1 = pl.dma("sync", tc[:], din[cos_name][:, s0:s0 + 512], "d_tab",
                            wait=tw, slots=2)
                d2 = pl.dma("sync", ts_[:], din[sin_name][:, s0:s0 + 512], "d_tab",
                            wait=tw, slots=2)
                return tc, ts_, [d1, d2]

            def conv_rff(tag, slot, i, xnbuf, xn_toks, w_tok):
                hk_toks = []
                for c in range(8):
                    bk = f"PS{c % 2}"
                    bank = PS[c % 2]

                    def mmf(e, c=c, slot=slot, xnbuf=xnbuf, bank=bank):
                        g = c // 2
                        prs = []
                        for t3 in range(3):
                            for uu in range(2):
                                prs.append((slot["cw"][:, (t3 * 2 + uu) * 1024 + c * 128:
                                                       (t3 * 2 + uu) * 1024 + (c + 1) * 128],
                                            xnbuf[:, 514 * (2 * g + uu) + t3:
                                                  514 * (2 * g + uu) + t3 + 512]))
                        return em_mm_group(e, bank[:], prs)
                    mt = pl.op("tensor", mmf, wait=[pl.bwait(bk)] + list(w_tok) + xn_toks, inc="pe")
                    ev = pl.op("vector", lambda e, c=c, bank=bank, i=i: e.tensor_scalar(
                        out=h_bf[:, c * 512:(c + 1) * 512], in0=bank[:], scalar1=1.0,
                        scalar2=cb_col[i][0][:, c:c + 1], op0=ALU.mult, op1=ALU.add),
                        wait=[mt, cb_col[i][1], pl.get("h_user")], inc="v")
                    pl.buse(bk, ev)
                    hk_toks.append(ev)
                f_toks = []
                last_rff_mm = []
                for g3 in range(3):
                    gg = float(np.sqrt(2.0 * GAMMAS[g3]) / TWO_PI)
                    for mc in range(2):
                        u = g3 * 2 + mc
                        bk = f"PS{2 + u % 2}"
                        bank = PS[2 + u % 2]

                        def mmf(e, g3=g3, mc=mc, slot=slot, bank=bank):
                            prs = [(slot["rw"][:, kc * 768 + g3 * 256 + mc * 128:
                                               kc * 768 + g3 * 256 + (mc + 1) * 128],
                                    h_bf[:, kc * 512:(kc + 1) * 512]) for kc in range(8)]
                            return em_mm_group(e, bank[:], prs)
                        mt = pl.op("tensor", mmf, wait=hk_toks + [pl.bwait(bk)], inc="pe")
                        last_rff_mm.append(mt)
                        zp = u % 2
                        zt = zt_sb[:, 512 * zp:512 * (zp + 1)]
                        zr = zr_sb[:, 512 * zp:512 * (zp + 1)]
                        zi = zi_sb[:, 512 * zp:512 * (zp + 1)]
                        c1 = pl.op("vector", lambda e, bank=bank, gg=gg, u=u, i=i, zt=zt:
                                   e.tensor_scalar(
                                       out=zt, in0=bank[:], scalar1=gg,
                                       scalar2=rb_col[i][0][:, u:u + 1],
                                       op0=ALU.mult, op1=ALU.add),
                                   wait=[mt, rb_col[i][1], pl.get(f"zt_user_{zp}")], inc="v")
                        pl.buse(bk, c1)
                        c2 = pl.op("gpsimd", lambda e, zt=zt, zi=zi: e.tensor_copy(
                            out=zi, in_=zt), wait=[c1, pl.get(f"zi_user_{zp}")], inc="g")
                        c3 = pl.op("gpsimd", lambda e, zi=zi, zr=zr: e.tensor_copy(
                            out=zr, in_=zi), wait=[c2, pl.get(f"zr_user_{zp}")], inc="g")
                        pl.save(f"zi_user_{zp}", c3)
                        c4 = pl.op("vector", lambda e, zt=zt, zr=zr: e.tensor_sub(zr, zt, zr),
                                   wait=[c3], inc="v")
                        c4b = pl.op("vector", lambda e, zt=zt, zr=zr: e.scalar_tensor_tensor(
                            out=zt, in0=zr, scalar=0.5, in1=zr,
                            op0=ALU.is_gt, op1=ALU.subtract), wait=[c4], inc="v")
                        c4c = pl.op("vector", lambda e, zt=zt, zr=zr: e.scalar_tensor_tensor(
                            out=zt, in0=zr, scalar=-0.5, in1=zt,
                            op0=ALU.is_lt, op1=ALU.subtract), wait=[c4b], inc="v")
                        pl.save(f"zr_user_{zp}", c4c)
                        c5 = pl.op("scalar", lambda e, u=u, zt=zt: e.activation(
                            out=f_bf[:, u * 512:(u + 1) * 512], in_=zt, func=AF.Sin,
                            scale=TWO_PI), wait=[c4c, pl.get(f"f_user_{u}")], inc="s")
                        pl.save(f"zt_user_{zp}", c5)
                        f_toks.append(c5)
                pl.save("h_user", last_rff_mm[-1])
                return f_toks

            def proj_cm(tag, slot, i, f_toks, dst):
                d_toks = []
                for c in range(8):
                    bk = f"PS{c % 2}"
                    bank = PS[c % 2]

                    def mmf(e, c=c, slot=slot, bank=bank):
                        prs = [(slot["pw"][:, fc * 1024 + c * 128: fc * 1024 + (c + 1) * 128],
                                f_bf[:, fc * 512:(fc + 1) * 512]) for fc in range(6)]
                        return em_mm_group(e, bank[:], prs)
                    mt = pl.op("tensor", mmf, wait=f_toks + [pl.bwait(bk)], inc="pe")
                    ev = pl.op("vector", lambda e, c=c, bank=bank, dst=dst, i=i: e.tensor_scalar(
                        out=dst[:, c * 512:(c + 1) * 512], in0=bank[:], scalar1=1.0,
                        scalar2=pb_col[i][0][:, c:c + 1], op0=ALU.mult, op1=ALU.add),
                        wait=[mt, pb_col[i][1], pl.get(f"proj_user_{c}")], inc="v")
                    pl.buse(bk, ev)
                    d_toks.append(ev)
                for u in range(6):
                    pl.save(f"f_user_{u}", pl.tok("pe"))
                return d_toks

            def rope_cm(tag, tile, src, dst, src_toks, tc, ts_, tab_toks):
                out_toks = []
                for c in range(8):
                    bk = f"PS{4 + c % 2}"
                    bank = PS[4 + c % 2]
                    par = c % 2
                    r1 = rt1[:, 512 * par:512 * (par + 1)]
                    r2 = rt2[:, 512 * par:512 * (par + 1)]
                    pm = pl.op("tensor", lambda e, c=c, src=src, bank=bank: e.matmul(
                        bank[:], CS["shuf"][:], src[:, c * 512:(c + 1) * 512],
                        start=True, stop=True),
                        wait=[src_toks[c], pl.bwait(bk)] + consts_tok, inc="pe")
                    m1 = pl.op("vector", lambda e, c=c, src=src, tc=tc, r1=r1: e.tensor_mul(
                        r1, src[:, c * 512:(c + 1) * 512], tc[:]),
                        wait=[src_toks[c], pl.get(f"rt1_user_{par}")] + tab_toks, inc="v")
                    m2 = pl.op("vector", lambda e, ts_=ts_, bank=bank, r2=r2: e.tensor_mul(
                        r2, bank[:], ts_[:]),
                        wait=[pm, pl.get(f"rt2_user_{par}")] + tab_toks, inc="v")
                    pl.buse(bk, m2)
                    m3 = pl.op("vector", lambda e, c=c, dst=dst, r1=r1, r2=r2: e.tensor_add(
                        dst[:, c * 512:(c + 1) * 512], r1, r2),
                        wait=[m1, m2, pl.get(f"rope_user_{c}")], inc="v")
                    pl.save(f"rt1_user_{par}", m3)
                    pl.save(f"rt2_user_{par}", m3)
                    pl.save(f"{tag}_srcuse_{c}", [pm, m1])
                    out_toks.append(m3)
                return out_toks

            for tile in range(NT):
                emit_precast(18)
                xnbuf, xn_toks = load_xn_tile(tile)
                tc, ts_, tab_toks = load_tab(tile, "t_pr_cos", "t_pr_sin")

                fv_toks = conv_rff("v", slotB, 2, xnbuf, xn_toks, wB_tok[-1:])
                v_toks = []
                for jj in range(4):
                    for n in range(2):
                        bk = f"PS{n}"
                        bank = PS[n]

                        def mmf(e, jj=jj, n=n, bank=bank):
                            prs = [(f_bf[:, fc * 512 + jj * 128: fc * 512 + jj * 128 + 128],
                                    slotB["pw"][:, fc * 1024 + n * 512: fc * 1024 + (n + 1) * 512])
                                   for fc in range(6)]
                            for q, (l, r) in enumerate(prs):
                                e.matmul(bank[:], l, r, start=(q == 0), stop=False)
                            return e.matmul(bank[:], CS["ones1x"][:],
                                            pbrow[:, n * 512:(n + 1) * 512],
                                            start=False, stop=True)
                        mt = pl.op("tensor", mmf,
                                   wait=fv_toks + [pl.bwait(bk), pbrow_tok] + consts_tok, inc="pe")
                        ev = pl.op("scalar", lambda e, jj=jj, n=n, bank=bank: e.copy(
                            out=v_aug[:, 1040 * jj:1040 * (jj + 1)].rearrange(
                                "p (h d) -> p h d", d=65)[:, 8 * n:8 * (n + 1), 0:64],
                            in_=bank[:].rearrange("p (h d) -> p h d", d=64)),
                            wait=[mt, va_ones, pl.get(f"vaug_user_{jj}_{n}")], inc="s")
                        pl.buse(bk, ev)
                        v_toks.append(ev)
                        pl.save(f"v_ready_{jj}_{n}", ev)
                for u in range(6):
                    pl.save(f"f_user_{u}", pl.tok("pe"))

                fk_toks = conv_rff("k", slotA, 1, xnbuf, xn_toks, wA_tok[-1:])
                kd_toks = proj_cm("k", slotA, 1, fk_toks, proj_sb)
                pl.save(f"xn_used_{tile}", [kd_toks[-1], pl.tok("pe")])
                kr_toks = rope_cm("k", tile, proj_sb, r_bf, kd_toks, tc, ts_, tab_toks)
                pl.save(f"tab_used_{tile}", pl.tok("v"))

                nA_mm = []
                for c in range(8):
                    spar = c % 2
                    sq = pl.op("scalar", lambda e, c=c, spar=spar: e.activation(
                        out=sq_bf[:, 512 * spar:512 * (spar + 1)],
                        in_=r_bf[:, c * 512:(c + 1) * 512], func=AF.Square),
                        wait=[kr_toks[c], pl.get(f"sq_user_{spar}")], inc="s")
                    mm = pl.op("tensor", lambda e, c=c, spar=spar: e.matmul(
                        PS[5][0:16, :], CS["bdh16"][:, 16 * c:16 * (c + 1)],
                        sq_bf[:, 512 * spar:512 * (spar + 1)],
                        start=(c == 0), stop=(c == 7)),
                        wait=[sq] + ([pl.bwait("PS5")] if c == 0 else []), inc="pe")
                    pl.save(f"sq_user_{spar}", mm)
                    nA_mm.append(mm)
                rrev = pl.op("scalar", lambda e: e.activation(
                    out=rr16[:], in_=PS[5][0:16, :], func=AF.Copy, scale=-1.0 / 64.0),
                    wait=[nA_mm[-1], pl.get("rr16_user")], inc="s")
                pl.buse("PS5", rrev)

                kny_toks = []
                rep_toks = []
                for c in range(8):
                    bk = f"PS{2 + c % 2}"
                    bank = PS[2 + c % 2]
                    mm1 = pl.op("tensor", lambda e, c=c, bank=bank: e.matmul(
                        bank[:], lam2[:], r_bf[:, c * 512:(c + 1) * 512], start=True, stop=False),
                        wait=[kr_toks[c], pl.bwait(bk)] + lam_ready, inc="pe")
                    mm2 = pl.op("tensor", lambda e, c=c, bank=bank: e.matmul(
                        bank[:], CS["sel2h"][:, 128 * c:128 * (c + 1)], rr16[:],
                        start=False, stop=True), wait=[rrev], inc="pe")
                    rep_toks.append(mm2)
                    ex = pl.op("scalar", lambda e, c=c, bank=bank: e.activation(
                        out=proj_sb[:, c * 512:(c + 1) * 512], in_=bank[:], func=AF.Exp,
                        bias=lamn[:, 0:1]),
                        wait=[mm2] + pl.get(f"k_srcuse_{c}") + [pl.get(f"proj_user_{c}")], inc="s")
                    pl.buse(bk, ex)
                    pl.save(f"proj_user_{c}", ex)
                    kny_toks.append(ex)
                pl.save("rr16_user", rep_toks[-1])

                nB_mm = []
                for c in range(8):
                    spar = c % 2
                    sq = pl.op("scalar", lambda e, c=c, spar=spar: e.activation(
                        out=sq_bf[:, 512 * spar:512 * (spar + 1)],
                        in_=proj_sb[:, c * 512:(c + 1) * 512], func=AF.Square),
                        wait=[kny_toks[c], pl.get(f"sq_user_{spar}")], inc="s")
                    mm = pl.op("tensor", lambda e, c=c, spar=spar: e.matmul(
                        PS[5][0:16, :], CS["bdh16"][:, 16 * c:16 * (c + 1)],
                        sq_bf[:, 512 * spar:512 * (spar + 1)],
                        start=(c == 0), stop=(c == 7)),
                        wait=[sq] + ([pl.bwait("PS5")] if c == 0 else []), inc="pe")
                    pl.save(f"sq_user_{spar}", mm)
                    nB_mm.append(mm)
                nev = pl.op("scalar", lambda e: e.activation(
                    out=n16[:], in_=PS[5][0:16, :], func=AF.Copy, scale=-0.5, bias=-LN16),
                    wait=[nB_mm[-1], pl.get("n16_user")], inc="s")
                pl.buse("PS5", nev)
                kc_toks = []
                for jj in range(4):
                    tr = pl.op("tensor", lambda e, jj=jj: e.transpose(
                        out=PS[5][:, 256 + jj * 16:256 + (jj + 1) * 16],
                        in_=n16[:, jj * 128:(jj + 1) * 128],
                        identity=CS["eye"][0:16, 0:16]),
                        wait=[nev, pl.bwait("PS5")], inc="pe")
                    ev = pl.op("vector", lambda e, jj=jj: e.tensor_copy(
                        out=kncols[:, jj * 16:(jj + 1) * 16],
                        in_=PS[5][:, 256 + jj * 16:256 + (jj + 1) * 16]),
                        wait=[tr, pl.get(f"kncols_user_{jj}")], inc="v")
                    pl.buse("PS5", ev)
                    kc_toks.append(ev)
                pl.save("n16_user", kc_toks[-1])

                for h in range(NH):
                    c, hh = h // 2, h % 2
                    for jp in range(2):
                        bk = f"PS{jp}"
                        bank = PS[jp]
                        mmk = []
                        for jo in range(2):
                            jj = jp * 2 + jo
                            mmk.append(pl.op("tensor", lambda e, c=c, hh=hh, jj=jj, jo=jo, bank=bank: e.matmul(
                                bank[:, jo * 256:(jo + 1) * 256],
                                proj_sb[64 * hh:64 * (hh + 1), c * 512 + jj * 128: c * 512 + (jj + 1) * 128],
                                ombf[64 * hh:64 * (hh + 1), :], start=(jo == 0), stop=(jo == 1)),
                                wait=[kny_toks[c], om_tok] + ([pl.bwait(bk)] if jo == 0 else []),
                                inc="pe"))
                        exs = []
                        for jo in range(2):
                            jj = jp * 2 + jo
                            exs.append(pl.op("scalar", lambda e, h=h, jj=jj, jo=jo, bank=bank: e.activation(
                                out=kp_bf[:, jo * 256:(jo + 1) * 256],
                                in_=bank[:, jo * 256:(jo + 1) * 256], func=AF.Exp,
                                bias=kncols[:, jj * 16 + h:jj * 16 + h + 1]),
                                wait=mmk + [kc_toks[jj], pl.get("kp_user")], inc="s"))
                        pl.buse(bk, exs[-1])
                        bkv = f"PK{h % 2}"
                        bankkv = PK[h % 2]

                        def kvf(e, h=h, jp=jp, tile=tile, bankkv=bankkv):
                            last = None
                            for jo in range(2):
                                jj = jp * 2 + jo
                                last = e.matmul(
                                    bankkv[0:65, 0:256],
                                    v_aug[:, 1040 * jj + 65 * h: 1040 * jj + 65 * (h + 1)],
                                    kp_bf[:, jo * 256:(jo + 1) * 256],
                                    start=(jp == 0 and jo == 0), stop=(jp == 1 and jo == 1))
                            return last
                        kvm = pl.op("tensor", kvf,
                                    wait=exs + [pl.get(f"v_ready_{jp * 2}_{h // 8}"),
                                                pl.get(f"v_ready_{jp * 2 + 1}_{h // 8}")]
                                    + ([pl.bwait(bkv)] if jp == 0 else []), inc="pe")
                        pl.save("kp_user", kvm)
                        if jp == 1:
                            if tile == 0:
                                kva = pl.op("vector", lambda e, h=h, bankkv=bankkv: e.tensor_copy(
                                    out=kv_acc[:, 256 * h:256 * (h + 1)], in_=bankkv[0:65, 0:256]),
                                    wait=[kvm], inc="v")
                            else:
                                kva = pl.op("vector", lambda e, h=h, bankkv=bankkv: e.tensor_add(
                                    kv_acc[:, 256 * h:256 * (h + 1)],
                                    kv_acc[:, 256 * h:256 * (h + 1)], bankkv[0:65, 0:256]),
                                    wait=[kvm], inc="v")
                            pl.buse(bkv, kva)
                petok = pl.tok("pe")
                stok = pl.tok("s")
                for jj in range(4):
                    for n in range(2):
                        pl.save(f"vaug_user_{jj}_{n}", petok)
                for c in range(8):
                    pl.save(f"proj_user_{c}", petok)
                    pl.save(f"rope_user_{c}", [petok, stok])
                for jj in range(4):
                    pl.save(f"kncols_user_{jj}", stok)

            kvT_toks = []
            vdone = pl.tok("v")
            for h in range(NH):
                for ec in range(2):
                    tr = pl.op("tensor", lambda e, h=h, ec=ec: e.transpose(
                        out=PS[5][:, 0:65],
                        in_=kv_acc[0:65, 256 * h + 128 * ec: 256 * h + 128 * (ec + 1)],
                        identity=CS["eye"][0:65, 0:65]),
                        wait=[vdone, pl.bwait("PS5")], inc="pe")
                    ev = pl.op("vector", lambda e, h=h, ec=ec: e.tensor_copy(
                        out=kvT_bf[:, 130 * h + 65 * ec: 130 * h + 65 * (ec + 1)],
                        in_=PS[5][:, 0:65]), wait=[tr], inc="v")
                    pl.buse("PS5", ev)
                    kvT_toks.append(ev)

            t_kv = tap("kv", (65, 16 * 256))
            if t_kv is not None:
                pl.dma("sync", t_kv, kv_acc[:], "d_dbg", wait=[vdone])
            t_kny = tap("kny", (128, 4096))
            if t_kny is not None:
                pl.dma("gpsimd", t_kny, proj_sb[:], "d_dbg2", wait=[pl.tok("s"), pl.tok("pe")])

            st1.close()
            st2 = ExitStack()
            sb2 = mksb(st2, side="right")
            num_sb = sb2("num_sb", [128, 4 * 512])
            den_one = [sb2(f"den_one{i}", [1, 512]) for i in range(2)]
            den16 = sb2("den16", [16, 512])
            den16b = sb2("den16b", [16, 512], BF16)
            invrep = sb2("invrep", [128, 512])
            attn_bf = sb2("attn_bf", [128, 4096], BF16)
            outw_sb = sb2("outw_sb", [128, 8 * 1024], BF16)
            ao_sb = sb2("ao_sb", [128, 4096])
            x_ld = [sb2("x_ld0", [128, C])]
            accst = [sb2("accst0", [128, C])]
            p1_all = [pl.tok("pe"), pl.tok("v"), pl.tok("s")]
            for key in ["xld_user", "accst_user", "den16_user", "invrep_user",
                        "denone_user_0", "denone_user_1"]:
                pl.save(key, p1_all)
            for c in range(8):
                pl.save(f"attn_user_{c}", p1_all)
            for cw_ in range(4):
                pl.save(f"numsb_user_{cw_}_0", p1_all)
                pl.save(f"numsb_user_{cw_}_1", p1_all)
            wQ_tok = load_proj_weights(slotA, 0, p1_all)
            outw_toks = [pl.dma("gpsimd", outw_sb[:, ic * 1024:(ic + 1) * 1024],
                                din["out_w"][ic * 128:(ic + 1) * 128, :], "d_w",
                                wait=p1_all, slots=6) for ic in range(8)]
            p1_pe = pl.tok("pe")
            for c in range(8):
                pl.save(f"proj_user_{c}", p1_pe)
            pl.save("xn_used_-2", pl.get(f"xn_used_{NT - 2}"))
            pl.save("xn_used_-1", pl.get(f"xn_used_{NT - 1}"))
            pl.save("tab_used_-1", pl.get(f"tab_used_{NT - 1}"))

            for tile in range(NT):
                emit_precast(18)
                xnbuf, xn_toks = load_xn_tile(tile)
                tc, ts_, tab_toks = load_tab(tile, "t_pr_cos", "t_pr_sin")
                fq_toks = conv_rff("q", slotA, 0, xnbuf, xn_toks, wQ_tok[-1:])
                qd_toks = proj_cm("q", slotA, 0, fq_toks, proj_sb)
                pl.save(f"xn_used_{tile}", [qd_toks[-1], pl.tok("pe")])
                qr_toks = rope_cm("q", tile, proj_sb, r_bf, qd_toks, tc, ts_, tab_toks)
                pl.save(f"tab_used_{tile}", pl.tok("v"))
                for c in range(8):
                    pl.save(f"proj_user_{c}", pl.get(f"q_srcuse_{c}"))

                nQ_mm = []
                for c in range(8):
                    spar = c % 2
                    sq = pl.op("scalar", lambda e, c=c, spar=spar: e.activation(
                        out=sq_bf[:, 512 * spar:512 * (spar + 1)],
                        in_=r_bf[:, c * 512:(c + 1) * 512], func=AF.Square),
                        wait=[qr_toks[c], pl.get(f"sq_user_{spar}")], inc="s")
                    mm = pl.op("tensor", lambda e, c=c, spar=spar: e.matmul(
                        PS[5][0:16, :], CS["bdh16"][:, 16 * c:16 * (c + 1)],
                        sq_bf[:, 512 * spar:512 * (spar + 1)],
                        start=(c == 0), stop=(c == 7)),
                        wait=[sq] + ([pl.bwait("PS5")] if c == 0 else []), inc="pe")
                    pl.save(f"sq_user_{spar}", mm)
                    nQ_mm.append(mm)
                nqe = pl.op("scalar", lambda e: e.activation(
                    out=n16q[:], in_=PS[5][0:16, :], func=AF.Copy, scale=-0.5, bias=-LN16),
                    wait=[nQ_mm[-1], pl.get("n16q_user")], inc="s")
                pl.buse("PS5", nqe)

                n16_users = []
                for wave in range(2):
                    hlist = list(range(8 * wave, 8 * wave + 8))
                    fav_toks = {}

                    def emit_fav(h):
                        c, hh = h // 2, h % 2
                        qoff = 1024 * (h % 2)
                        mcs = []
                        for mc in range(2):
                            bk = f"PS{2 + mc}"
                            bank = PS[2 + mc]
                            m1 = pl.op("tensor", lambda e, c=c, hh=hh, mc=mc, bank=bank: e.matmul(
                                bank[:], ombf[64 * hh:64 * (hh + 1), 128 * mc:128 * (mc + 1)],
                                r_bf[64 * hh:64 * (hh + 1), c * 512:(c + 1) * 512],
                                start=True, stop=False),
                                wait=[qr_toks[c], om_tok, pl.bwait(bk)], inc="pe")
                            m2 = pl.op("tensor", lambda e, h=h, bank=bank: e.matmul(
                                bank[:], CS["sel16b"][:, 128 * h:128 * (h + 1)], n16q[:],
                                start=False, stop=True), wait=[nqe], inc="pe")
                            n16_users.append(m2)
                            ex = pl.op("scalar", lambda e, mc=mc, bank=bank, qoff=qoff:
                                       e.activation(
                                           out=qp_bf[:, qoff + mc * 512:qoff + (mc + 1) * 512],
                                           in_=bank[:], func=AF.Exp),
                                       wait=[m2, pl.get(f"qp_user_{h % 2}")], inc="s")
                            pl.buse(bk, ex)
                            mcs.append(ex)
                        fav_toks[h] = mcs

                    def emit_num(h):
                        c, hh = h // 2, h % 2
                        hs = h - 8 * wave
                        qoff = 1024 * (h % 2)
                        bkv = f"PK{h % 2}"
                        bankkv = PK[h % 2]

                        def numf(e, h=h, bankkv=bankkv, qoff=qoff):
                            last = None
                            for ec in range(2):
                                last = e.matmul(bankkv[0:65, :],
                                                kvT_bf[:, 130 * h + 65 * ec:130 * h + 65 * (ec + 1)],
                                                qp_bf[:, qoff + ec * 512:qoff + (ec + 1) * 512],
                                                start=(ec == 0), stop=(ec == 1))
                            return last
                        nm = pl.op("tensor", numf,
                                   wait=fav_toks[h] + [kvT_toks[-1], pl.bwait(bkv)], inc="pe")
                        pl.save(f"qp_user_{h % 2}", nm)
                        do = den_one[h % 2]
                        dv = pl.op("scalar", lambda e, do=do, bankkv=bankkv: e.copy(
                            out=do[:], in_=bankkv[64:65, :]),
                            wait=[nm, pl.get(f"denone_user_{h % 2}")], inc="s")
                        dnd = pl.dma("sync", bounce[(2 * tile + wave) % 4, 512 * hs:512 * (hs + 1)][None, :],
                                     do[:], "d_bn", wait=[dv], slots=4)
                        pl.save(f"denone_user_{h % 2}", dnd)
                        cw_ = (c - 4 * wave)
                        nv = pl.op("vector", lambda e, hh=hh, cw_=cw_, bankkv=bankkv: e.tensor_copy(
                            out=num_sb[64 * hh:64 * (hh + 1), 512 * cw_:512 * (cw_ + 1)],
                            in_=bankkv[0:64, :]),
                            wait=[dv, pl.get(f"numsb_user_{cw_}_{hh}")], inc="v")
                        pl.buse(bkv, nv)

                    for idx in range(9):
                        if idx < 8:
                            emit_fav(hlist[idx])
                        if idx >= 1:
                            emit_num(hlist[idx - 1])

                    brow = (2 * tile + wave) % 4
                    dd2 = pl.dma("sync", den16[0:8, :], bounce[brow, 0:4096].rearrange(
                        "(p f) -> p f", p=8), "d_bn",
                        wait=pl.chan_toks("d_bn", 4) + [pl.get("den16_user")], slots=2)
                    iv1 = pl.op("vector", lambda e: e.tensor_scalar(
                        out=den16[0:8, :], in0=den16[0:8, :], scalar1=1.0, scalar2=1e-6,
                        op0=ALU.mult, op1=ALU.add), wait=[dd2], inc="v")
                    iv1b = pl.op("vector", lambda e: e.reciprocal(
                        out=den16[0:8, :], in_=den16[0:8, :]), wait=[iv1], inc="v")
                    iv2 = pl.op("scalar", lambda e: e.activation(
                        out=den16b[0:8, :], in_=den16[0:8, :], func=AF.Copy),
                        wait=[iv1b], inc="s")

                    att_toks = []
                    for c in range(4 * wave, 4 * wave + 4):
                        rm = pl.op("tensor", lambda e, c=c, wave=wave: e.matmul(
                            PS[4][:], CS["sel2h"][0:8, 128 * (c - 4 * wave):128 * (c - 4 * wave + 1)],
                            den16b[0:8, :], start=True, stop=True),
                            wait=[iv2, pl.bwait("PS4")], inc="pe")
                        re_ = pl.op("scalar", lambda e: e.copy(out=invrep[:], in_=PS[4][:]),
                                    wait=[rm, pl.get("invrep_user")], inc="s")
                        pl.buse("PS4", re_)
                        cw_ = (c - 4 * wave)
                        tt = pl.op("vector", lambda e, c=c, cw_=cw_: e.tensor_mul(
                            attn_bf[:, c * 512:(c + 1) * 512],
                            num_sb[:, 512 * cw_:512 * (cw_ + 1)],
                            invrep[:]),
                            wait=[re_, pl.get(f"attn_user_{c}")], inc="v")
                        pl.save(f"numsb_user_{cw_}_0", tt)
                        pl.save(f"numsb_user_{cw_}_1", tt)
                        att_toks.append(tt)
                        pl.save("invrep_user", att_toks[-1])
                    pl.save("den16_user", pl.tok("v"))
                pl.save("n16q_user", n16_users[-1])
                att_toks = [pl.tok("v")]

                ao_toks = []
                for oc in range(8):
                    bk = f"PS{oc % 2}"
                    bank = PS[oc % 2]

                    def mmf(e, oc=oc, bank=bank):
                        prs = [(outw_sb[:, ic * 1024 + oc * 128: ic * 1024 + (oc + 1) * 128],
                                attn_bf[:, ic * 512:(ic + 1) * 512]) for ic in range(8)]
                        return em_mm_group(e, bank[:], prs)
                    mt = pl.op("tensor", mmf, wait=att_toks + outw_toks + [pl.bwait(bk)], inc="pe")
                    ev = pl.op("vector", lambda e, oc=oc, bank=bank: e.tensor_scalar(
                        out=ao_sb[:, oc * 512:(oc + 1) * 512], in0=bank[:], scalar1=1.0,
                        scalar2=outb_col[0][:, oc:oc + 1], op0=ALU.mult, op1=ALU.add),
                        wait=[mt, outb_col[1], pl.get(f"ao_user_{oc}")], inc="v")
                    pl.buse(bk, ev)
                    ao_toks.append(ev)
                tr_last = {}
                for jj in range(4):
                    xl = x_ld[0]
                    ac = accst[0]
                    lx = pl.dma("sync", xl[:], x_h[tile * 512 + jj * 128: tile * 512 + (jj + 1) * 128, :],
                                "d_x2", wait=[pl.get("xld_user")], slots=2)
                    adds = []
                    for oc in range(8):
                        tbk = f"PS{4 + oc % 2}"
                        tbank = PS[4 + oc % 2]
                        tr = pl.op("tensor", lambda e, jj=jj, oc=oc, tbank=tbank: e.transpose(
                            out=tbank[:, 128:256],
                            in_=ao_sb[:, oc * 512 + jj * 128: oc * 512 + (jj + 1) * 128],
                            identity=CS["eye"][:]),
                            wait=[ao_toks[oc], pl.bwait(tbk)], inc="pe")
                        tr_last[oc] = tr
                        ad = pl.op("vector", lambda e, oc=oc, xl=xl, ac=ac, tbank=tbank: e.tensor_add(
                            ac[:, oc * 128:(oc + 1) * 128], tbank[:, 128:256],
                            xl[:, oc * 128:(oc + 1) * 128]),
                            wait=[tr, lx, pl.get("accst_user")], inc="v")
                        pl.buse(tbk, ad)
                        adds.append(ad)
                    pl.save("xld_user", adds[-1])
                    da = pl.dma("sync", acc_d[tile * 512 + jj * 128: tile * 512 + (jj + 1) * 128, :],
                                ac[:], "d_acc", wait=adds, slots=4)
                    pl.save("accst_user", da)
                for oc in range(8):
                    pl.save(f"ao_user_{oc}", tr_last[oc])
                petok = pl.tok("pe")
                stok = pl.tok("s")
                for c in range(8):
                    pl.save(f"attn_user_{c}", petok)
                    pl.save(f"rope_user_{c}", [petok, stok])

            st2.close()
            perf_done = [pl.tok("v"), pl.tok("s"), pl.tok("pe")] + pl.chan_toks("d_acc", 4)

            t_acc = tap("acc", (S, C))
            if t_acc is not None:
                pl.dma("sync", t_acc, acc_d, "d_dbg", wait=perf_done)


        with ExitStack() as stf:
            sbf = mksb(stf)
            xf_in = [sbf(f"xf_in{i}", [128, 8 * 514]) for i in range(2)]
            xs_bf = sbf("xs_bf", [128, 8 * 512], BF16)
            sqf_bf = sbf("sqf_bf", [128, 2 * 512], BF16)
            ftab = [sbf(f"ftab{i}", [128, 512], BF16) for i in range(4)]
            xf_st = [sbf(f"xf_st{i}", [128, 512]) for i in range(2)]
            frt1 = sbf("frt1", [128, 512])
            frt2 = sbf("frt2", [128, 512])
            frp1 = sbf("frp1", [128, 2 * 512], BF16)
            frp2 = sbf("frp2", [128, 2 * 512], BF16)
            eye_bf = sbf("eye_bf", [128, 128], BF16)
            onescol_bf = sbf("onescol_bf", [128, 1], BF16)
            statA = sbf("statA", [1, 512])
            statR = sbf("statR", [1, 512])
            statR_bf = sbf("statR_bf", [1, 512], BF16)
            statB2 = sbf("statB2", [2, 512], BF16)
            growf = sbf("growf", [2, C])
            growf_bf = sbf("growf_bf", [2, C], BF16)
            selstage = sbf("selstage", [16, 1024])
            ec = pl.op("vector", lambda e: e.tensor_copy(out=eye_bf[:], in_=CS["eye"][:]),
                       wait=consts_tok + perf_done, inc="v")
            go1 = pl.op("gpsimd", lambda e: e.memset(onescol_bf[:], 1.0), wait=perf_done, inc="g")
            go2 = pl.op("gpsimd", lambda e: e.memset(statB2[:], -1.0), wait=perf_done, inc="g")
            ffn_setup = [ec, go1, go2]

            def sh0(c):
                return 514 * c + (1 if c < 4 else 0)

            def load_xf_tile(buf, src_d, tile, prev_users):
                toks = []
                for u in range(8):
                    lo = tile * 512 - 1
                    ncols = 514
                    dst0 = 514 * u
                    if tile == 0:
                        toks.append(pl.op("gpsimd", lambda e, dst0=dst0, buf=buf: e.memset(
                            buf[:, dst0:dst0 + 1], 0.0), wait=prev_users, inc="g"))
                        lo = 0; ncols = 513; dst0 += 1
                    if tile == NT - 1:
                        ncols -= 1
                    toks.append(pl.dma("sync", buf[:, dst0:dst0 + ncols],
                                       src_d[u, :, lo:lo + ncols], "d_xf",
                                       wait=prev_users, slots=4))
                return toks

            def colvec_ragged(tag, vec1d, n, wait=(), sba=None):
                sba = sba or sbf
                ncol = (n + 127) // 128
                nfull = n // 128
                tail = n - nfull * 128
                stage = sba(f"fcv_s_{tag}", [ncol, 128])
                out = sba(f"fcv_{tag}", [128, ncol])
                toks = []
                if tail:
                    toks.append(pl.op("gpsimd", lambda e: e.memset(stage[:], 0.0),
                                      wait=wait, inc="g"))
                toks.append(pl.dma("sync", stage[0:nfull, :],
                                   vec1d[0:nfull * 128].rearrange("(a p) -> a p", p=128),
                                   "d_lcv", wait=list(wait) + toks))
                if tail:
                    toks.append(pl.dma("sync", stage[nfull:nfull + 1, 0:tail],
                                       vec1d[nfull * 128:n][None, :], "d_lcv",
                                       wait=list(wait) + toks[:1]))
                tr = pl.op("tensor", lambda e: e.transpose(
                    out=PS[5][:, 0:ncol], in_=stage[:], identity=CS["eye"][0:ncol, 0:ncol]),
                    wait=toks + [pl.bwait("PS5")], inc="pe")
                ev = pl.op("vector", lambda e: e.tensor_copy(out=out[:], in_=PS[5][:, 0:ncol]),
                           wait=[tr], inc="v")
                pl.buse("PS5", ev)
                return out, ev

            prev_done = perf_done + ffn_setup
            src_d = xn_cm_d
            dst_d = xf_a

            import os as _os
            NLAY = int(_os.environ.get("FFN_LAYERS", DEPTH))
            for li in range(NLAY):
                stA = ExitStack()
                sbA = mksb(stA, side="right")
                wqkv = sbA(f"wqkv{li}", [128, 8 * 3072], BF16)
                wo = sbA(f"wo{li}", [128, 8 * 1024], BF16)
                q_bf = sbA(f"q_bf{li}", [128, 8 * 512], BF16)
                kr_cur = sbA(f"fkrc{li}", [128, 8 * 512], BF16)
                kr_prev = sbA(f"fkrp{li}", [128, 8 * 256], BF16)
                va_cur = sbA(f"fvac{li}", [128, 4 * 1040], BF16)
                va_prev = sbA(f"fvap{li}", [128, 2 * 1040], BF16)
                attn_cm = sbA(f"fattn_cm{li}", [128, 8 * 512], BF16)
                probsT = sbA(f"fprobsT{li}", [128, 2 * 384], BF16)
                probs3 = sbA(f"fprobs3{li}", [128, 3 * 384], BF16)
                maxng = sbA(f"fmaxng{li}", [128, 16])
                dencol = sbA(f"fdencol{li}", [128, 16])
                invden = sbA(f"finvden{li}", [128, 16])
                diag_bf = sbA(f"fdiag{li}", [128, 2 * 128], BF16)
                fn16q = sbA(f"ffn16q{li}", [16, 512], BF16)
                fn16k = sbA(f"ffn16k{li}", [16, 512], BF16)
                fnsq = sbA(f"ffnsq{li}", [16, 512])
                qsel_bf = sbA(f"fqselb{li}", [16, 8 * 128], BF16)
                ksel_bf = sbA(f"fkselb{li}", [16, 8 * 128], BF16)

                wq_toks = [pl.dma("gpsimd", wqkv[:, ic * 3072:(ic + 1) * 3072],
                                  din["Wqkv"][li, ic * 128:(ic + 1) * 128, :], "d_w",
                                  wait=prev_done, slots=6) for ic in range(8)]
                wo_toks = [pl.dma("gpsimd", wo[:, ic * 1024:(ic + 1) * 1024],
                                  din["Wo"][li, ic * 128:(ic + 1) * 128, :], "d_w",
                                  wait=prev_done, slots=6) for ic in range(8)]
                va_init = [pl.op("gpsimd", lambda e, vb=vb: e.memset(
                    vb[:].rearrange("p (h d) -> p h d", d=65)[:, :, 64:65], 1.0),
                    wait=prev_done, inc="g") for vb in (va_cur, va_prev)]
                selcast = []
                for scname, dst, scale in ((
                        "q_scale", qsel_bf, QK_SCALE), ("k_scale", ksel_bf, 1.0)):
                    selprev = prev_done + \
                        ([pl.get("selstage_u")] if pl.get("selstage_u") else [])
                    sz = pl.op("gpsimd", lambda e: e.memset(selstage[:], 0.0),
                               wait=selprev, inc="g")
                    selt = []
                    for c in range(8):
                        for r in range(2):
                            selt.append(pl.dma(
                                "sync", selstage[2 * c + r:2 * c + r + 1,
                                                 128 * c + 64 * r:128 * c + 64 * r + 64],
                                din[scname][li:li + 1, :], "d_lcv", wait=[sz]))
                    ct = pl.op("vector", lambda e, dst=dst, scale=scale: e.tensor_scalar(
                        out=dst[:], in0=selstage[:], scalar1=scale, scalar2=None,
                        op0=ALU.mult), wait=selt, inc="v")
                    pl.save("selstage_u", ct)
                    selcast.append(ct)

                pl.save("ftile_prev", prev_done)
                pl.save("ftile_prev2", prev_done)
                pl.save("fsq_user_0", None); pl.save("fsq_user_1", None)
                pl.save("fnu_q", prev_done); pl.save("fnu_k", prev_done)
                for par in range(2):
                    pl.save(f"diag_u{par}", prev_done)
                    pl.save(f"pT_u{par}", prev_done)
                for r3 in range(3):
                    pl.save(f"probs_u{r3}", prev_done)
                for h in range(NH):
                    pl.save(f"mx_u_{h}", prev_done)
                    pl.save(f"dc_u_{h}", prev_done)
                    pl.save(f"iv_u_{h}", prev_done)
                for c in range(8):
                    pl.save(f"fattn_u_{c}", prev_done)

                for tile in range(NT):
                    s0 = tile * 512
                    tprev = pl.get("ftile_prev")
                    tprev2 = pl.get("ftile_prev2")
                    xfb = xf_in[tile % 2]
                    xf_toks = load_xf_tile(xfb, src_d, tile, [tprev2])
                    tb_toks = []
                    for ti, nm in enumerate(["t_fx_cq", "t_fx_sq", "t_fx_ck", "t_fx_sk"]):
                        tb_toks.append(pl.dma("sync", ftab[ti][:], din[nm][:, s0:s0 + 512],
                                              "d_tab", wait=[tprev], slots=4))
                    xs_toks = []
                    for c in range(8):
                        xs_toks.append(pl.op("vector", lambda e, c=c, xfb=xfb: e.tensor_copy(
                            out=xs_bf[:, 512 * c:512 * (c + 1)],
                            in_=xfb[:, sh0(c):sh0(c) + 512]),
                            wait=xf_toks + [pl.get("xsbf_u")], inc="v"))
                    qs_toks = []; ks_toks = []; vv_toks = []
                    krb = kr_cur
                    vab = va_cur
                    for c in range(8):
                        bk = f"PS{c % 2}"
                        bank = PS[c % 2]

                        def mmq(e, c=c, bank=bank):
                            prs = [(wqkv[:, ic * 3072 + c * 128: ic * 3072 + (c + 1) * 128],
                                    xs_bf[:, 512 * ic:512 * (ic + 1)]) for ic in range(8)]
                            return em_mm_group(e, bank[:], prs)
                        mt = pl.op("tensor", mmq, wait=xs_toks + wq_toks + [pl.bwait(bk)], inc="pe")
                        ev = pl.op("vector", lambda e, c=c, bank=bank: e.tensor_copy(
                            out=q_bf[:, 512 * c:512 * (c + 1)], in_=bank[:]),
                            wait=[mt, tprev], inc="v")
                        pl.buse(bk, ev)
                        qs_toks.append(ev)
                    for c in range(8):
                        bk = f"PS{c % 2}"
                        bank = PS[c % 2]

                        def mmk(e, c=c, bank=bank):
                            prs = [(wqkv[:, ic * 3072 + 1024 + c * 128:
                                         ic * 3072 + 1024 + (c + 1) * 128],
                                    xs_bf[:, 512 * ic:512 * (ic + 1)]) for ic in range(8)]
                            return em_mm_group(e, bank[:], prs)
                        mt = pl.op("tensor", mmk, wait=xs_toks + wq_toks + [pl.bwait(bk)], inc="pe")
                        ev = pl.op("vector", lambda e, c=c, bank=bank, krb=krb: e.tensor_copy(
                            out=krb[:, 512 * c:512 * (c + 1)], in_=bank[:]),
                            wait=[mt, tprev], inc="v")
                        pl.buse(bk, ev)
                        ks_toks.append(ev)
                    for sl in range(4):
                        for n in range(2):
                            bk = f"PS{2 + n}"
                            bank = PS[2 + n]

                            def mmv(e, sl=sl, n=n, bank=bank):
                                prs = [(xs_bf[:, 512 * ic + 128 * sl: 512 * ic + 128 * (sl + 1)],
                                        wqkv[:, ic * 3072 + 2048 + n * 512:
                                             ic * 3072 + 2048 + (n + 1) * 512])
                                       for ic in range(8)]
                                return em_mm_group(e, bank[:], prs)
                            mt = pl.op("tensor", mmv, wait=xs_toks + wq_toks + [pl.bwait(bk)],
                                       inc="pe")
                            ev = pl.op("scalar", lambda e, sl=sl, n=n, bank=bank, vab=vab: e.copy(
                                out=vab[:, 1040 * sl:1040 * (sl + 1)].rearrange(
                                    "p (h d) -> p h d", d=65)[:, 8 * n:8 * (n + 1), 0:64],
                                in_=bank[:].rearrange("p (h d) -> p h d", d=64)),
                                wait=[mt, tprev] + va_init, inc="s")
                            pl.buse(bk, ev)
                            vv_toks.append(ev)

                    pl.save("xsbf_u", pl.tok("pe"))

                    def l2pipe(buf, selmat_bf, src_toks, nbk, fn16b, tag):
                        nbank = PS[4] if nbk == "PS4" else PS[5]
                        nmm = []
                        for c in range(8):
                            par = c % 2
                            sq = pl.op("scalar", lambda e, c=c, buf=buf, par=par: e.activation(
                                out=sqf_bf[:, 512 * par:512 * (par + 1)],
                                in_=buf[:, 512 * c:512 * (c + 1)], func=AF.Square),
                                wait=[src_toks[c], pl.get(f"fsq_user_{par}")], inc="s")
                            mm = pl.op("tensor", lambda e, c=c, par=par, nbank=nbank: e.matmul(
                                nbank[0:16, :], CS["bdh16"][:, 16 * c:16 * (c + 1)],
                                sqf_bf[:, 512 * par:512 * (par + 1)],
                                start=(c == 0), stop=(c == 7)),
                                wait=[sq] + ([pl.bwait(nbk)] if c == 0 else []), inc="pe")
                            pl.save(f"fsq_user_{par}", mm)
                            nmm.append(mm)
                        e0 = pl.op("scalar", lambda e, nbank=nbank: e.activation(
                            out=fnsq[:], in_=nbank[0:16, :], func=AF.Sqrt),
                            wait=[nmm[-1], pl.get(f"fnu_{tag}"), pl.get("fnsq_u")], inc="s")
                        pl.buse(nbk, e0)
                        e0b = pl.op("vector", lambda e: e.tensor_scalar(
                            out=fnsq[:], in0=fnsq[:], scalar1=1e-12, scalar2=None,
                            op0=ALU.max), wait=[e0], inc="v")
                        e0c = pl.op("vector", lambda e: e.reciprocal(
                            out=fnsq[:], in_=fnsq[:]), wait=[e0b], inc="v")
                        e1 = pl.op("scalar", lambda e, fn16b=fn16b: e.activation(
                            out=fn16b[:], in_=fnsq[:], func=AF.Copy), wait=[e0c], inc="s")
                        pl.save("fnsq_u", e1)
                        outs = []
                        last_rm = None
                        for c in range(8):
                            bbk = f"PK{c % 2}"
                            bbank = PK[c % 2]
                            rm = pl.op("tensor", lambda e, c=c, selmat_bf=selmat_bf,
                                       fn16b=fn16b, bbank=bbank: e.matmul(
                                bbank[:, 0:512], selmat_bf[:, 128 * c:128 * (c + 1)], fn16b[:],
                                start=True, stop=True),
                                wait=[e1, pl.bwait(bbk)] + selcast, inc="pe")
                            last_rm = rm
                            tt = pl.op("vector", lambda e, c=c, buf=buf, bbank=bbank: e.tensor_mul(
                                buf[:, 512 * c:512 * (c + 1)], buf[:, 512 * c:512 * (c + 1)],
                                bbank[:, 0:512]), wait=[rm], inc="v")
                            pl.buse(bbk, tt)
                            outs.append(tt)
                        pl.save(f"fnu_{tag}", last_rm)
                        return outs
                    qn_toks = l2pipe(q_bf, qsel_bf, qs_toks, "PS4", fn16q, "q")
                    kn_toks = l2pipe(krb, ksel_bf, ks_toks, "PS5", fn16k, "k")

                    def ropef(buf, src_toks, tci, tsi):
                        outs = []
                        for c in range(8):
                            par = c % 2
                            bk = f"PS{4 + par}"
                            bank = PS[4 + par]
                            r1 = frp1[:, 512 * par:512 * (par + 1)]
                            r2 = frp2[:, 512 * par:512 * (par + 1)]
                            pm = pl.op("tensor", lambda e, c=c, buf=buf, bank=bank: e.matmul(
                                bank[:], CS["shuf"][:], buf[:, 512 * c:512 * (c + 1)],
                                start=True, stop=True),
                                wait=[src_toks[c], pl.bwait(bk)] + consts_tok, inc="pe")
                            m1 = pl.op("vector", lambda e, c=c, buf=buf, tci=tci, r1=r1: e.tensor_mul(
                                r1, buf[:, 512 * c:512 * (c + 1)], ftab[tci][:]),
                                wait=[src_toks[c], pl.get(f"frt1_u{par}")] + tb_toks, inc="v")
                            m2 = pl.op("vector", lambda e, tsi=tsi, bank=bank, r2=r2: e.tensor_mul(
                                r2, bank[:], ftab[tsi][:]),
                                wait=[pm, pl.get(f"frt2_u{par}")] + tb_toks, inc="v")
                            pl.buse(bk, m2)
                            m3 = pl.op("vector", lambda e, c=c, buf=buf, r1=r1, r2=r2: e.tensor_add(
                                buf[:, 512 * c:512 * (c + 1)], r1, r2),
                                wait=[m1, m2, tprev], inc="v")
                            pl.save(f"frt1_u{par}", m3)
                            pl.save(f"frt2_u{par}", m3)
                            outs.append(m3)
                        return outs
                    qr_toks = ropef(q_bf, qn_toks, 0, 1)
                    krr_toks = ropef(krb, kn_toks, 2, 3)

                    def kloc(gc):
                        if gc // 4 == tile:
                            return kr_cur, 128 * (gc % 4)
                        return kr_prev, 128 * ((gc % 4) - 2)

                    def vloc(gc):
                        if gc // 4 == tile:
                            return va_cur, 1040 * (gc % 4)
                        return va_prev, 1040 * ((gc % 4) - 2)

                    prev_k = pl.get("krr_prev") or []
                    prev_v = pl.get("vv_prev") or []

                    NSLOT = 4 * NH
                    sim_done = [None] * NSLOT
                    tmr_done = [None] * NSLOT
                    exp_done = [None] * NSLOT
                    diag_done = [None] * NSLOT
                    trdiv_done = [None] * NSLOT
                    evac_done = [None] * NSLOT
                    av_done = [None] * NSLOT
                    oc_toks = []

                    def slmeta(i):
                        sl, h = i // NH, i % NH
                        g = 4 * tile + sl
                        wchunks = [gc for gc in (g - 2, g - 1, g) if gc >= 0]
                        return sl, h, g, wchunks

                    def emit_sims(i):
                        sl, h, g, wchunks = slmeta(i)
                        c, hh = h // 2, h % 2
                        bk = f"PS{i % 2}"
                        bank = PS[i % 2]
                        runs = []
                        for gc in wchunks:
                            tb, col = kloc(gc)
                            blk = 256 if tb is kr_prev else 512
                            base = blk * c + col
                            if runs and runs[-1][0] is tb and runs[-1][1] + runs[-1][2] == base:
                                runs[-1] = (tb, runs[-1][1], runs[-1][2] + 128)
                            else:
                                runs.append((tb, base, 128))
                        nw = len(wchunks) * 128
                        toks = []
                        colpos = 0
                        for (tb, base, wdt) in runs:
                            toks.append(pl.op("tensor",
                                lambda e, tb=tb, base=base, wdt=wdt, c=c, hh=hh, sl=sl,
                                       bank=bank, colpos=colpos: e.matmul(
                                    bank[:, colpos:colpos + wdt],
                                    q_bf[64 * hh:64 * (hh + 1),
                                          512 * c + 128 * sl:512 * c + 128 * (sl + 1)],
                                    tb[64 * hh:64 * (hh + 1), base:base + wdt],
                                    start=(colpos == 0), stop=False),
                                wait=[qr_toks[c], krr_toks[c],
                                      pl.bwait(bk) if colpos == 0 else None] + prev_k,
                                inc="pe"))
                            colpos += wdt
                        mname = "mq_full" if g >= 2 else ("mq_s1" if g == 1 else "mq_s0")
                        toks.append(pl.op("tensor",
                            lambda e, bank=bank, mname=mname, nw=nw: e.matmul(
                                bank[:, 0:nw], eye_bf[:], CS[mname][:, 0:nw],
                                start=False, stop=True),
                            wait=[ec] + consts_tok, inc="pe"))
                        sim_done[i] = toks

                    def emit_tmr(i):
                        sl, h, g, wchunks = slmeta(i)
                        nw = len(wchunks) * 128
                        par = i % 2
                        bank = PS[par]
                        t = pl.op("vector", lambda e, bank=bank, nw=nw, h=h: e.reduce_max(
                            out=maxng[:, h:h + 1], in_=bank[:, 0:nw], axis=AX, negate=True),
                            wait=sim_done[i] + [pl.get(f"mx_u_{h}")], inc="v")
                        tmr_done[i] = t

                    def emit_exp(i):
                        sl, h, g, wchunks = slmeta(i)
                        nw = len(wchunks) * 128
                        par = i % 2
                        p3 = i % 3
                        bank = PS[par]
                        t = pl.op("scalar",
                                  lambda e, nw=nw, p3=p3, h=h, bank=bank: e.activation(
                                      out=probs3[:, 384 * p3:384 * p3 + nw],
                                      in_=bank[:, 0:nw],
                                      func=AF.Exp, bias=maxng[:, h:h + 1],
                                      accum_out=dencol[:, h:h + 1]),
                                  wait=[tmr_done[i], pl.get(f"probs_u{p3}"),
                                        pl.get(f"dc_u_{h}")], inc="s")
                        pl.buse(f"PS{par}", t)
                        pl.save(f"mx_u_{h}", t)
                        exp_done[i] = t

                    def emit_recip_diag(i):
                        sl, h, g, wchunks = slmeta(i)
                        par = i % 2
                        r = pl.op("vector", lambda e, h=h: e.reciprocal(
                            out=invden[:, h:h + 1], in_=dencol[:, h:h + 1]),
                            wait=[exp_done[i], pl.get(f"iv_u_{h}")], inc="v")
                        pl.save(f"dc_u_{h}", r)
                        d = pl.op("gpsimd", lambda e, par=par, h=h: e.tensor_scalar(
                            out=diag_bf[:, 128 * par:128 * (par + 1)], in0=eye_bf[:],
                            scalar1=invden[:, h:h + 1], scalar2=None, op0=ALU.mult),
                            wait=[r, pl.get(f"diag_u{par}"), ec], inc="g")
                        pl.save(f"iv_u_{h}", d)
                        diag_done[i] = d

                    def emit_trdiv(i):
                        sl, h, g, wchunks = slmeta(i)
                        nch = len(wchunks)
                        par = i % 2
                        p3 = i % 3
                        bk = f"PS{2 + par}"
                        bank = PS[2 + par]
                        toks = []
                        for q3 in range(nch):
                            toks.append(pl.op("tensor",
                                lambda e, q3=q3, p3=p3, par=par, bank=bank: e.matmul(
                                    bank[:, 128 * q3:128 * (q3 + 1)],
                                    probs3[:, 384 * p3 + 128 * q3:384 * p3 + 128 * (q3 + 1)],
                                    diag_bf[:, 128 * par:128 * (par + 1)],
                                    start=True, stop=True),
                                wait=[exp_done[i], diag_done[i]] +
                                     ([pl.bwait(bk)] if q3 == 0 else []), inc="pe"))
                        pl.save(f"probs_u{p3}", toks)
                        pl.save(f"diag_u{par}", toks[-1])
                        trdiv_done[i] = toks

                    def emit_evac(i):
                        sl, h, g, wchunks = slmeta(i)
                        nw = len(wchunks) * 128
                        par = i % 2
                        t = pl.op("scalar", lambda e, par=par, nw=nw: e.copy(
                            out=probsT[:, 384 * par:384 * par + nw],
                            in_=PS[2 + par][:, 0:nw]),
                            wait=trdiv_done[i] + [pl.get(f"pT_u{par}")], inc="s")
                        pl.buse(f"PS{2 + par}", t)
                        evac_done[i] = t

                    def emit_av(i):
                        sl, h, g, wchunks = slmeta(i)
                        par = i % 2
                        bkv = f"PK{par}"
                        bankkv = PK[par]
                        vlocs = tuple(vloc(gc) for gc in wchunks)

                        def avf(e, h=h, vlocs=vlocs, bankkv=bankkv, par=par):
                            last = None
                            for q3, (vb, voff) in enumerate(vlocs):
                                last = e.matmul(
                                    bankkv[0:65, 0:128],
                                    vb[:, voff + 65 * h: voff + 65 * (h + 1)],
                                    probsT[:, 384 * par + 128 * q3:384 * par + 128 * (q3 + 1)],
                                    start=(q3 == 0), stop=(q3 == len(vlocs) - 1))
                            return last
                        t = pl.op("tensor", avf,
                                  wait=[evac_done[i], pl.bwait(bkv)] + vv_toks + prev_v,
                                  inc="pe")
                        pl.save(f"pT_u{par}", t)
                        av_done[i] = t

                    def emit_outcopy(i):
                        sl, h, g, wchunks = slmeta(i)
                        c, hh = h // 2, h % 2
                        par = i % 2
                        t = pl.op("vector", lambda e, c=c, hh=hh, sl=sl, par=par: e.tensor_copy(
                            out=attn_cm[64 * hh:64 * (hh + 1),
                                        512 * c + 128 * sl:512 * c + 128 * (sl + 1)],
                            in_=PK[par][0:64, 0:128]),
                            wait=[av_done[i], pl.get(f"fattn_u_{c}")], inc="v")
                        pl.buse(f"PK{par}", t)
                        oc_toks.append(t)

                    for i_ in range(NSLOT + 4):
                        if i_ < NSLOT:
                            emit_sims(i_)
                            emit_tmr(i_)
                            emit_exp(i_)
                        if 1 <= i_ <= NSLOT:
                            emit_recip_diag(i_ - 1)
                        if 2 <= i_ < NSLOT + 2:
                            emit_trdiv(i_ - 2)
                            emit_evac(i_ - 2)
                        if 3 <= i_ < NSLOT + 3:
                            emit_av(i_ - 3)
                        if 4 <= i_:
                            emit_outcopy(i_ - 4)

                    slpe = pl.tok("pe")
                    cp_toks = []
                    for c in range(8):
                        cp_toks.append(pl.op("vector", lambda e, c=c: e.tensor_copy(
                            out=kr_prev[:, 256 * c:256 * (c + 1)],
                            in_=kr_cur[:, 512 * c + 256:512 * c + 512]),
                            wait=[slpe], inc="v"))
                    for b2_ in range(2):
                        cp_toks.append(pl.op("vector", lambda e, b2_=b2_: e.tensor_copy(
                            out=va_prev[:, 1040 * b2_:1040 * (b2_ + 1)],
                            in_=va_cur[:, 1040 * (2 + b2_):1040 * (3 + b2_)]),
                            wait=[slpe], inc="v"))
                    pl.save("krr_prev", cp_toks)
                    pl.save("vv_prev", cp_toks)

                    for oc in range(8):
                        bk = f"PS{oc % 2}"
                        bank = PS[oc % 2]

                        def mmo(e, oc=oc, bank=bank):
                            prs = [(wo[:, ic * 1024 + oc * 128: ic * 1024 + (oc + 1) * 128],
                                    attn_cm[:, 512 * ic:512 * (ic + 1)]) for ic in range(8)]
                            return em_mm_group(e, bank[:], prs)
                        mt = pl.op("tensor", mmo, wait=oc_toks + wo_toks + [pl.bwait(bk)], inc="pe")
                        stg = xf_st[oc % 2]
                        ad = pl.op("vector", lambda e, oc=oc, bank=bank, stg=stg, xfb=xfb:
                                   e.tensor_add(stg[:], bank[:],
                                                xfb[:, 514 * oc + 1:514 * oc + 513]),
                                   wait=[mt, pl.get(f"xfst_u{oc % 2}")], inc="v")
                        pl.buse(bk, ad)
                        dw = pl.dma("sync", dst_d[oc, :, s0:s0 + 512], stg[:], "d_xfw",
                                    wait=[ad], slots=4)
                        pl.save(f"xfst_u{oc % 2}", dw)
                    wo_pe = pl.tok("pe")
                    for c in range(8):
                        pl.save(f"fattn_u_{c}", wo_pe)
                    pl.save("ftile_prev2", pl.get("ftile_prev"))
                    pl.save("ftile_prev", [pl.tok("pe"), pl.tok("v"), pl.tok("s")])

                sub_done = [pl.tok("pe"), pl.tok("v"), pl.tok("s")] + \
                    pl.chan_toks("d_xfw", 4)
                stA.close()
                prev_done = sub_done
                src_d = dst_d
                dst_d = xf_b if dst_d is xf_a else xf_a
                t_att = tap(f"fattn{li}", (8, 128, S))
                if t_att is not None:
                    pl.dma("sync", t_att, src_d, "d_dbg", wait=prev_done)

                stF = ExitStack()
                sbF = mksb(stF, side="right")
                w1sb = sbF(f"w1sb{li}", [128, 8 * 2816], BF16)
                w1gp = [sbF(f"w1gp{li}_{i}", [128, 8 * 256], BF16) for i in range(2)]
                w1gs = sbF(f"w1gs{li}", [128, 8 * 128], BF16)
                w1gr = sbF(f"w1gr{li}", [128, 8 * 128], BF16)
                w2p = [sbF(f"w2p{li}_{i}", [128, 2 * 1024], BF16) for i in range(3)]
                w2s = sbF(f"w2s{li}", [128, 1024], BF16)
                w2r = sbF(f"w2r{li}", [128, 1024], BF16)
                t_bf = [sbF(f"t_bf{li}_{i}", [128, 512], BF16) for i in range(NJC)]
                e_sb = sbF(f"e_sb{li}", [128, 512])
                xn2_bf = sbF(f"xn2_bf{li}", [128, 8 * 512], BF16)
                w1g_src = din["W1"][li].rearrange("(a p) n -> p a n", p=128)

                w1_toks = []
                for ic in range(8):
                    base = ic * 2816
                    w1_toks.append(pl.op("gpsimd", lambda e, base=base: e.memset(
                        w1sb[:, base + 2730:base + 2816], 0.0), wait=prev_done, inc="g"))
                    w1_toks.append(pl.dma("gpsimd", w1sb[:, base:base + 2730],
                                          din["W1"][li, ic * 128:(ic + 1) * 128, 0:2730],
                                          "d_w", wait=prev_done, slots=6))
                w1gz = pl.op("gpsimd", lambda e: e.memset(w1gr[:], 0.0),
                            wait=prev_done, inc="g")
                b1a_col, b1a_tok = colvec_ragged(f"b1a{li}", din["b1"][li, 0:DI], DI,
                                                 wait=prev_done, sba=sbF)
                b1g_col, b1g_tok = colvec_ragged(f"b1g{li}", din["b1"][li, DI:2 * DI], DI,
                                                 wait=prev_done, sba=sbF)
                b2_col, b2_tok = colvec_ragged(f"b2_{li}", din["b2"][li, :], C, wait=prev_done, sba=sbF)
                grow_toks = [pl.dma("sync", growf[0:1, :], din["ln_g"][li:li + 1, :], "d_lcv",
                                    wait=prev_done),
                             pl.dma("sync", growf[1:2, :], din["ln_b"][li:li + 1, :], "d_lcv",
                                    wait=prev_done)]
                grow_toks = [pl.op("vector", lambda e: e.tensor_copy(
                    out=growf_bf[:], in_=growf[:]), wait=grow_toks + prev_done, inc="v")]

                w2z = pl.op("gpsimd", lambda e: e.memset(w2r[:], 0.0),
                            wait=prev_done, inc="g")

                pl.save("ftile_prev", prev_done)
                pl.save("ftile_prev2", prev_done)
                pl.save("fsq_user_0", None); pl.save("fsq_user_1", None)
                for i3 in range(3):
                    pl.save(f"w2p_u_{i3}", prev_done)
                pl.save("w2s_u", prev_done)
                pl.save("w2r_u", [prev_done, w2z])
                for i2 in range(2):
                    pl.save(f"w1gp_u_{i2}", prev_done)
                pl.save("w1gs_u", prev_done)
                pl.save("w1gr_u", [prev_done, w1gz])

                for tile in range(NT):
                    s0 = tile * 512
                    tprev = pl.get("ftile_prev")
                    tprev2 = pl.get("ftile_prev2")
                    xfb = xf_in[tile % 2]
                    xf_toks = load_xf_tile(xfb, src_d, tile, [tprev2])
                    xs_toks = []
                    for c in range(8):
                        xs_toks.append(pl.op("vector", lambda e, c=c, xfb=xfb: e.tensor_copy(
                            out=xs_bf[:, 512 * c:512 * (c + 1)],
                            in_=xfb[:, sh0(c):sh0(c) + 512]),
                            wait=xf_toks + [pl.get("xsbf_u")], inc="v"))
                    smm = []
                    for c in range(8):
                        smm.append(pl.op("tensor", lambda e, c=c: e.matmul(
                            PS[5][0:1, :], onescol_bf[:], xs_bf[:, 512 * c:512 * (c + 1)],
                            start=(c == 0), stop=(c == 7)),
                            wait=[xs_toks[c]] + ([pl.bwait("PS5")] if c == 0 else []) +
                                 ffn_setup, inc="pe"))
                    s1 = pl.op("scalar", lambda e: e.copy(out=statA[:], in_=PS[5][0:1, :]),
                               wait=[smm[-1], tprev], inc="s")
                    pl.buse("PS5", s1)
                    sq2 = []
                    for c in range(8):
                        par = c % 2
                        sq = pl.op("scalar", lambda e, c=c, par=par: e.activation(
                            out=sqf_bf[:, 512 * par:512 * (par + 1)],
                            in_=xs_bf[:, 512 * c:512 * (c + 1)], func=AF.Square),
                            wait=[xs_toks[c], pl.get(f"fsq_user_{par}")], inc="s")
                        mm = pl.op("tensor", lambda e, c=c, par=par: e.matmul(
                            PS[5][0:1, :], onescol_bf[:],
                            sqf_bf[:, 512 * par:512 * (par + 1)],
                            start=(c == 0), stop=(c == 7)),
                            wait=[sq] + ([s1] if c == 0 else []), inc="pe")
                        pl.save(f"fsq_user_{par}", mm)
                        sq2.append(mm)
                    s2 = pl.op("scalar", lambda e: e.copy(out=statR[:], in_=PS[5][0:1, :]),
                               wait=[sq2[-1], tprev], inc="s")
                    pl.buse("PS5", s2)
                    pl.save("xsbf_u", pl.tok("pe"))
                    v1 = pl.op("vector", lambda e: e.tensor_scalar(
                        out=statA[:], in0=statA[:], scalar1=1.0 / C, scalar2=None,
                        op0=ALU.mult), wait=[s1, s2, tprev], inc="v")
                    v3b = pl.op("vector", lambda e: e.tensor_mul(
                        frt1[0:1, 0:512], statA[:], statA[:]), wait=[v1], inc="v")
                    v4 = pl.op("vector", lambda e: e.scalar_tensor_tensor(
                        out=statR[:], in0=statR[:], scalar=1.0 / C, in1=frt1[0:1, 0:512],
                        op0=ALU.mult, op1=ALU.subtract), wait=[v3b], inc="v")
                    v5 = pl.op("vector", lambda e: e.tensor_scalar(
                        out=statR[:], in0=statR[:], scalar1=1.0, scalar2=1e-5,
                        op0=ALU.mult, op1=ALU.add), wait=[v4], inc="v")
                    v6 = pl.op("vector", lambda e: e.reciprocal(
                        out=statR[:], in_=statR[:]), wait=[v5], inc="v")
                    v7 = pl.op("scalar", lambda e: e.activation(
                        out=statR_bf[:], in_=statR[:], func=AF.Sqrt),
                        wait=[v6, tprev], inc="s")
                    v8 = pl.op("vector", lambda e: e.tensor_mul(
                        statB2[0:1, :], statA[:], statR_bf[:]), wait=[v7, tprev], inc="v")
                    xn_toks2 = []
                    for c in range(8):
                        mA = pl.op("tensor", lambda e, c=c: e.matmul(
                            PS[2][:], growf_bf[0:1, 128 * c:128 * (c + 1)], statR_bf[:],
                            start=True, stop=True),
                            wait=[v7, pl.bwait("PS2")] + grow_toks, inc="pe")
                        mB = pl.op("tensor", lambda e, c=c: e.matmul(
                            PS[3][:], growf_bf[:, 128 * c:128 * (c + 1)], statB2[:],
                            start=True, stop=True),
                            wait=[v8, pl.bwait("PS3")] + grow_toks + ffn_setup, inc="pe")
                        t1_ = pl.op("vector", lambda e, c=c, xfb=xfb: e.tensor_mul(
                            frt2[:], xfb[:, sh0(c):sh0(c) + 512], PS[2][:]),
                            wait=[mA, pl.get("frt2_u")] + xf_toks, inc="v")
                        pl.buse("PS2", t1_)
                        t2_ = pl.op("vector", lambda e, c=c: e.tensor_sub(
                            xn2_bf[:, 512 * c:512 * (c + 1)], frt2[:], PS[3][:]),
                            wait=[t1_, mB, tprev], inc="v")
                        pl.buse("PS3", t2_)
                        pl.save("frt2_u", t2_)
                        xn_toks2.append(t2_)

                    t_toks = []
                    w1pieces = [("p", pi) for pi in range(10)] + [("s", 0), ("r", 0)]
                    for pc_ in w1pieces:
                        if pc_[0] == "p":
                            pi = pc_[1]
                            gbuf = w1gp[pi % 2]
                            gkey = f"w1gp_u_{pi % 2}"
                            cw = 256
                            gl = pl.dma("gpsimd",
                                        gbuf[:].rearrange("p (a c) -> p a c", c=256),
                                        w1g_src[:, :, DI + 256 * pi: DI + 256 * (pi + 1)],
                                        "d_w2", wait=[pl.get(gkey)], slots=5)
                            chunks = [(2 * pi, 0), (2 * pi + 1, 128)]
                        elif pc_[0] == "s":
                            gbuf = w1gs
                            gkey = "w1gs_u"
                            cw = 128
                            gl = pl.dma("gpsimd",
                                        gbuf[:].rearrange("p (a c) -> p a c", c=128),
                                        w1g_src[:, :, DI + 2560: DI + 2688],
                                        "d_w2", wait=[pl.get(gkey)], slots=5)
                            chunks = [(20, 0)]
                        else:
                            gbuf = w1gr
                            gkey = "w1gr_u"
                            cw = 128
                            gl = pl.dma("gpsimd",
                                        gbuf[:].rearrange("p (a c) -> p a c", c=128)[:, :, 0:42],
                                        w1g_src[:, :, DI + 2688: DI + 2730],
                                        "d_w2", wait=[pl.get(gkey)], slots=5)
                            chunks = [(21, 0)]
                        glast = []
                        for (jc, goff) in chunks:
                            bkA = f"PS{jc % 2}"
                            bankA = PS[jc % 2]

                            def mma(e, jc=jc, bankA=bankA):
                                prs = [(w1sb[:, ic * 2816 + 128 * jc: ic * 2816 + 128 * (jc + 1)],
                                        xn2_bf[:, 512 * ic:512 * (ic + 1)]) for ic in range(8)]
                                return em_mm_group(e, bankA[:], prs)
                            ma = pl.op("tensor", mma, wait=xn_toks2 + w1_toks + [pl.bwait(bkA)],
                                       inc="pe")
                            bkB = f"PS{2 + jc % 2}"
                            bankB = PS[2 + jc % 2]

                            def mmg(e, gbuf=gbuf, goff=goff, cw=cw, bankB=bankB):
                                prs = [(gbuf[:, cw * ic + goff: cw * ic + goff + 128],
                                        xn2_bf[:, 512 * ic:512 * (ic + 1)]) for ic in range(8)]
                                return em_mm_group(e, bankB[:], prs)
                            mg = pl.op("tensor", mmg, wait=xn_toks2 + [gl, pl.bwait(bkB)],
                                       inc="pe")
                            glast.append(mg)
                            ge = pl.op("scalar", lambda e, jc=jc, bankB=bankB: e.activation(
                                out=e_sb[:], in_=bankB[:], func=AF.Gelu,
                                bias=b1g_col[:, jc:jc + 1]),
                                wait=[mg, b1g_tok, pl.get("fe_u")], inc="s")
                            pl.buse(bkB, ge)
                            tb_ = pl.op("vector", lambda e, jc=jc, bankA=bankA:
                                        e.scalar_tensor_tensor(
                                            out=t_bf[jc][:], in0=bankA[:],
                                            scalar=b1a_col[:, jc:jc + 1],
                                            in1=e_sb[:], op0=ALU.add, op1=ALU.mult),
                                        wait=[ma, ge, b1a_tok, tprev], inc="v")
                            pl.buse(bkA, tb_)
                            pl.save("fe_u", tb_)
                            t_toks.append(tb_)
                        pl.save(gkey, glast)

                    accbanks = [("PS4", PS[4]), ("PS5", PS[5]), ("PS2", PS[2]), ("PS3", PS[3])]
                    w2pieces = [("p", pi) for pi in range(10)] + [("s", 0), ("r", 0)]
                    for grp in range(2):
                        grp_mms = {oo: [] for oo in range(4)}
                        for pc_ in w2pieces:
                            if pc_[0] == "p":
                                pi = pc_[1]
                                wt = w2p[pi % 3]
                                wkey = f"w2p_u_{pi % 3}"
                                wl = pl.dma("gpsimd",
                                            wt[:].rearrange("p (a n) -> p a n", n=1024),
                                            din["W2"][li, 256 * pi:256 * (pi + 1), :]
                                            .rearrange("(a p) n -> p a n", p=128),
                                            "d_w2", wait=[pl.get(wkey)], slots=5)
                                chunks = [(2 * pi, 0), (2 * pi + 1, 1024)]
                            elif pc_[0] == "s":
                                wt = w2s
                                wkey = "w2s_u"
                                if grp == 0:
                                    wl = pl.dma("gpsimd", wt[:],
                                                din["W2"][li, 2560:2688, :], "d_w2",
                                                wait=[pl.get(wkey)], slots=5)
                                else:
                                    wl = pl.get(wkey)
                                chunks = [(20, 0)]
                            else:
                                wt = w2r
                                wkey = "w2r_u"
                                if grp == 0:
                                    wl = pl.dma("gpsimd", wt[0:42, :],
                                                din["W2"][li, 2688:2730, :], "d_w2",
                                                wait=[pl.get(wkey)], slots=5)
                                else:
                                    wl = pl.get(wkey)
                                chunks = [(21, 0)]
                            mlast = []
                            for (jc, woff) in chunks:
                                for oo in range(4):
                                    oc = 4 * grp + oo
                                    bkn, bb = accbanks[oo]
                                    mm = pl.op("tensor",
                                               lambda e, jc=jc, oc=oc, wt=wt, woff=woff,
                                               bb=bb: e.matmul(
                                                   bb[:],
                                                   wt[:, woff + 128 * oc:woff + 128 * (oc + 1)],
                                                   t_bf[jc][:],
                                                   start=(jc == 0), stop=(jc == NJC - 1)),
                                               wait=[wl, t_toks[jc]] +
                                                    ([pl.bwait(bkn)] if jc == 0 else []),
                                               inc="pe")
                                    grp_mms[oo].append(mm)
                                    mlast.append(mm)
                            pl.save(wkey, mlast)
                        for oo in range(4):
                            oc = 4 * grp + oo
                            bkn, bb = accbanks[oo]
                            stg = xf_st[oo % 2]
                            ad = pl.op("vector", lambda e, oc=oc, bb=bb, stg=stg, xfb=xfb:
                                       e.scalar_tensor_tensor(
                                           out=stg[:], in0=bb[:], scalar=b2_col[:, oc:oc + 1],
                                           in1=xfb[:, 514 * oc + 1:514 * oc + 513],
                                           op0=ALU.add, op1=ALU.add),
                                       wait=[grp_mms[oo][-1], b2_tok,
                                             pl.get(f"fxfst_u{oo % 2}")], inc="v")
                            pl.buse(bkn, ad)
                            dw = pl.dma("sync", dst_d[oc, :, s0:s0 + 512], stg[:], "d_xfw",
                                        wait=[ad], slots=4)
                            pl.save(f"fxfst_u{oo % 2}", dw)
                    pl.save(f"f2xf_user_{tile}", xs_toks[-1])
                    pl.save("ftile_prev2", pl.get("ftile_prev"))
                    pl.save("ftile_prev", [pl.tok("pe"), pl.tok("v"), pl.tok("s")])

                sub_done = [pl.tok("pe"), pl.tok("v"), pl.tok("s")] + pl.chan_toks("d_xfw", 4)
                stF.close()
                prev_done = sub_done
                src_d = dst_d
                dst_d = xf_b if dst_d is xf_a else xf_a
                t_ffn = tap(f"fffn{li}", (8, 128, S))
                if t_ffn is not None:
                    pl.dma("sync", t_ffn, src_d, "d_dbg", wait=prev_done)

            fin_acc = sbf("fin_acc", [128, C])
            fin_cm = sbf("fin_cm", [128, C])
            fin_out = sbf("fin_out", [128, C])
            for tt in range(S // 128):
                la = pl.dma("sync", fin_acc[:], acc_d[tt * 128:(tt + 1) * 128, :], "d_fx",
                            wait=prev_done + [pl.get("finacc_u")])
                lc = [pl.dma("sync", fin_cm[:, c * 128:(c + 1) * 128],
                             src_d[c, :, tt * 128:(tt + 1) * 128], "d_fx2",
                             wait=prev_done + [pl.get("fincm_u")], slots=4) for c in range(8)]
                adds = []
                for c in range(8):
                    tr = pl.op("tensor", lambda e, c=c: e.transpose(
                        out=PS[4][:, 0:128], in_=fin_cm[:, c * 128:(c + 1) * 128],
                        identity=CS["eye"][:]),
                        wait=lc + [pl.bwait("PS4")] + prev_done, inc="pe")
                    ad = pl.op("vector", lambda e, c=c: e.tensor_add(
                        fin_out[:, c * 128:(c + 1) * 128], PS[4][:, 0:128],
                        fin_acc[:, c * 128:(c + 1) * 128]),
                        wait=[tr, la, pl.get("finout_u")], inc="v")
                    pl.buse("PS4", ad)
                    adds.append(ad)
                pl.save("finacc_u", adds[-1])
                pl.save("fincm_u", adds[-1])
                dy = pl.dma("sync", y_h[tt * 128:(tt + 1) * 128, :], fin_out[:], "d_y",
                            wait=adds, slots=4)
                pl.save("finout_u", dy)

        pl.final_drain()
        block = st.enter_context(nc.Block())
        pl.emit(block)
    return nc, dbg_out


def shard_inputs(inputs, core):
    HC = host_constants()
    HT = host_tables()
    m = {k: np.ascontiguousarray(v) for k, v in {
        "x": inputs["L0_attn_x"][core],
        "conv_k": inputs["L0_attn_conv_k"], "conv_b": inputs["L0_attn_conv_b"],
        "rff_w": inputs["L0_attn_rff_w"], "rff_b": inputs["L0_attn_rff_b"],
        "proj_w": inputs["L0_attn_proj_w"], "proj_b": inputs["L0_attn_proj_b"],
        "omega": inputs["L0_attn_omega"], "landmarks": inputs["L0_attn_landmarks"],
        "out_w": inputs["L0_attn_out_w"], "out_b": inputs["L0_attn_out_b"],
        "Wqkv": inputs["L0_ffn_Wqkv"], "q_scale": inputs["L0_ffn_q_scale"],
        "k_scale": inputs["L0_ffn_k_scale"], "Wo": inputs["L0_ffn_Wo"],
        "ln_g": inputs["L0_ffn_ln_g"], "ln_b": inputs["L0_ffn_ln_b"],
        "W1": inputs["L0_ffn_W1"], "b1": inputs["L0_ffn_b1"],
        "W2": inputs["L0_ffn_W2"], "b2": inputs["L0_ffn_b2"],
    }.items()}
    for k, v in HC.items():
        m["c_" + k] = v
    import ml_dtypes
    for k, v in HT.items():
        m["t_" + k] = v.astype(ml_dtypes.bfloat16)
    return m


def kernel(**inputs):
    nc, _ = build_program()
    in_maps = [shard_inputs(inputs, core) for core in range(8)]
    res = run_bass_kernel_spmd(nc, in_maps, core_ids=list(range(8)))
    return np.stack([np.asarray(r["y"]) for r in res.results], axis=0)


# version 11
# speedup vs baseline: 1.0874x; 1.0874x over previous
import numpy as np
import concourse.bass as bass
import concourse.mybir as mybir
from concourse.bass_utils import run_bass_kernel_spmd
from contextlib import ExitStack

F32 = mybir.dt.float32
BF16 = mybir.dt.bfloat16
I32 = mybir.dt.int32
AF = mybir.ActivationFunctionType
ALU = mybir.AluOpType
AX = mybir.AxisListType.X

S = 4096
C = 1024
NH = 16
DH = 64
NT = S // 512

NRF = 256
MS = 256
GAMMAS = (0.5, 1.0, 2.0)
NLM = 64

DEPTH = 6
WINDOW = 256
DI = 2730
DIP = 2816
NJC = DIP // 128
QK_SCALE = 8.0
NEG = -1.0e30
LN16 = float(np.log(16.0))
TWO_PI = float(2.0 * np.pi)
SQ2M = float(np.sqrt(2.0 / MS))


def host_constants():
    cs = {}
    cs["eye"] = np.eye(128, dtype=np.float32)

    shuf = np.zeros((128, 128), np.float32)
    for blk in range(2):
        for m in range(64):
            shuf[blk * 64 + (m + 32) % 64, blk * 64 + m] = 1.0
    cs["shuf"] = shuf

    bd = np.zeros((128, 128), np.float32)
    for c in range(8):
        for p in range(128):
            bd[p, 16 * c + 2 * c + p // 64] = 1.0
    cs["bdh16"] = bd

    s2 = np.zeros((16, 8 * 128), np.float32)
    for c in range(8):
        for p in range(128):
            s2[2 * c + p // 64, 128 * c + p] = 1.0
    cs["sel2h"] = s2

    s16 = np.zeros((16, 16 * 128), np.float32)
    for h in range(NH):
        s16[h, 128 * h:128 * (h + 1)] = 1.0
    cs["sel16b"] = s16

    r = np.arange(128)
    j = np.arange(384)[None, :]
    d = r[:, None]
    cs["mq_full"] = np.where((j >= d) & (j <= d + 256), 0.0, NEG).astype(np.float32)
    j2 = np.arange(256)[None, :]
    cs["mq_s1"] = np.where((j2 >= 128 + d - 256) & (j2 <= 128 + d), 0.0, NEG).astype(np.float32)
    j1 = np.arange(128)[None, :]
    cs["mq_s0"] = np.where(j1 <= d, 0.0, NEG).astype(np.float32)

    cs["ones1x"] = np.ones((1, 128), np.float32)
    return cs


def host_tables():
    tb = {}
    inv = 1.0 / 10000.0 ** (np.arange(0, DH, 2, dtype=np.float64) / DH)
    ang = np.arange(S, dtype=np.float64)[None, :] * inv[:, None]
    cos64 = np.cos(np.concatenate([ang, ang], 0))
    sin64 = np.sin(np.concatenate([ang, ang], 0))
    sgn = np.concatenate([-np.ones(32), np.ones(32)])[:, None]
    tb["pr_cos"] = np.tile(cos64, (2, 1)).astype(np.float32)
    tb["pr_sin"] = np.tile(sin64 * sgn, (2, 1)).astype(np.float32)

    half = np.arange(0, DH, 2, dtype=np.float64)
    invf = 1.0 / 10000.0 ** (half / DH)
    t = np.arange(S, dtype=np.float64)
    fr = t[None, :] * invf[:, None]
    fr64 = np.concatenate([fr, fr], 0)
    base = ((half + 0.4 * DH) / (1.4 * DH))[:, None]
    power = ((t - S // 2) / (WINDOW // 2))[None, :]
    scale = np.concatenate([base ** power, base ** power], 0)
    cosf = np.cos(fr64); sinf = np.sin(fr64)
    tb["fx_cq"] = np.tile(cosf * scale, (2, 1)).astype(np.float32)
    tb["fx_sq"] = np.tile(sinf * scale * sgn, (2, 1)).astype(np.float32)
    tb["fx_ck"] = np.tile(cosf / scale, (2, 1)).astype(np.float32)
    tb["fx_sk"] = np.tile(sinf / scale * sgn, (2, 1)).astype(np.float32)
    return tb


CAST_CONSTS = {"shuf", "bdh16", "ones1x", "sel16b", "sel2h",
               "mq_full", "mq_s1", "mq_s0"}


class Pl:
    ENGINES = ("sync", "gpsimd", "vector", "scalar", "tensor")

    ENGINE_SEMS = {"v": "vector", "s": "scalar", "pe": "tensor", "g": "gpsimd"}

    def __init__(self, nc, st):
        self.nc = nc
        self.st = st
        self.stream = {e: [] for e in self.ENGINES}
        self.sems = {}
        self.counts = {}
        self.saved = {}
        self.chan_k = {}
        self.waited = {e: {} for e in self.ENGINES}

    def sem(self, name):
        if name not in self.sems:
            self.sems[name] = self.st.enter_context(self.nc.semaphore(name))
            self.counts[name] = 0
        return name

    def op(self, eng, fn, wait=(), inc=None, inc_by=1):
        flat = []

        def _fl(t):
            if t is None:
                return
            if isinstance(t, (list, tuple)) and (len(t) == 0 or not isinstance(t[0], str)):
                for x in t:
                    _fl(x)
            else:
                flat.append(t)
        for t in wait:
            _fl(t)
        best = {}
        for s, v in flat:
            if v > best.get(s, 0):
                best[s] = v
        wed = self.waited[eng]
        out = []
        for s, v in best.items():
            if eng == "tensor" and s == "pe":
                continue
            if wed.get(s, 0) >= v:
                continue
            wed[s] = v
            out.append((s, v))
        wait = tuple(out)
        for s, _ in wait:
            self.sem(s)
        sems = self.sems
        tok = None
        if inc is not None:
            self.sem(inc)
            self.counts[inc] += inc_by
            tok = (inc, self.counts[inc])

        def run(e, fn=fn, wait=wait, tok=tok, inc_by=inc_by):
            for s, v in wait:
                if v > 0:
                    e.wait_ge(sems[s], v)
            ins = fn(e)
            if tok is not None and ins is not None:
                ins.then_inc(sems[tok[0]], inc_by)
        self.stream[eng].append(run)
        return tok

    def dma(self, eng, out, in_, chan, wait=(), slots=4, **kw):
        k = self.chan_k.setdefault(chan, 0)
        self.chan_k[chan] = k + 1
        sem = f"{chan}.{k % slots}"
        self.sem(sem)
        prior = self.counts[sem]
        w = list(wait) + ([(sem, prior)] if prior > 0 else [])
        return self.op(eng, lambda e: e.dma_start(out=out, in_=in_, **kw), w, sem, 16)

    def chan_toks(self, chan, slots=4):
        return [self.tok(f"{chan}.{i}") for i in range(slots)
                if f"{chan}.{i}" in self.counts]

    def tok(self, sem):
        self.sem(sem)
        return (sem, self.counts[sem])

    def save(self, key, tok):
        self.saved[key] = tok

    def get(self, key):
        return self.saved.get(key)

    def buse(self, key, tok):
        self.saved["bk_" + key] = tok

    def bwait(self, key):
        return self.saved.get("bk_" + key)

    def final_drain(self):
        toks = [(s, c) for s, c in self.counts.items() if c > 0]
        sems = self.sems

        def run(e, toks=toks):
            for s, v in toks:
                e.wait_ge(sems[s], v)
        self.stream["sync"].append(run)

    def emit(self, block):
        for eng in self.ENGINES:
            ops = self.stream[eng]
            if not ops:
                continue
            dec = getattr(block, eng)

            def section(e, ops=ops):
                for f in ops:
                    f(e)
            section.__name__ = eng
            dec(section)


def em_mm_group(e, bank_ap, pairs):
    n = len(pairs)
    last = None
    for q, (l, r) in enumerate(pairs):
        last = e.matmul(bank_ap, l, r, start=(q == 0), stop=(q == n - 1))
    return last


def build_program(dbg=()):
    nc = bass.Bass("TRN2", target_bir_lowering=False)
    din = {}

    def inp(name, shape):
        din[name] = nc.dram_tensor(name, list(shape), F32, kind="ExternalInput").ap()
        return din[name]

    x_h = inp("x", (S, C))
    inp("conv_k", (3, 3, 256, C)); inp("conv_b", (3, C))
    inp("rff_w", (3, 3, C, MS)); inp("rff_b", (3, 3, MS))
    inp("proj_w", (3, 3 * MS, C)); inp("proj_b", (3, C))
    inp("omega", (DH, NRF)); inp("landmarks", (NLM, DH))
    inp("out_w", (C, C)); inp("out_b", (C,))
    inp("Wqkv", (DEPTH, C, 3 * C)); inp("q_scale", (DEPTH, DH)); inp("k_scale", (DEPTH, DH))
    inp("Wo", (DEPTH, C, C)); inp("ln_g", (DEPTH, C)); inp("ln_b", (DEPTH, C))
    inp("W1", (DEPTH, C, 2 * DI)); inp("b1", (DEPTH, 2 * DI))
    inp("W2", (DEPTH, DI, C)); inp("b2", (DEPTH, C))
    HC = host_constants()
    for k, v in HC.items():
        inp("c_" + k, v.shape)
    HT = host_tables()
    for k, v in HT.items():
        din["t_" + k] = nc.dram_tensor("t_" + k, list(v.shape), BF16,
                                       kind="ExternalInput").ap()

    y_h = nc.dram_tensor("y", [S, C], F32, kind="ExternalOutput").ap()

    dbg_out = {}

    def tap(name, shape):
        if name in dbg:
            dbg_out[name] = nc.dram_tensor("dbg_" + name, list(shape), F32,
                                           kind="ExternalOutput").ap()
            return dbg_out[name]
        return None

    xn_cm_d = nc.dram_tensor("xn_cm_d", [8, 128, S], F32, kind="Internal").ap()
    xf_a = nc.dram_tensor("xf_a", [8, 128, S], F32, kind="Internal").ap()
    xf_b = nc.dram_tensor("xf_b", [8, 128, S], F32, kind="Internal").ap()
    acc_d = nc.dram_tensor("acc_d", [S, C], F32, kind="Internal").ap()
    bounce = nc.dram_tensor("bounce", [4, 16384], F32, kind="Internal").ap()
    w2bf_d = nc.dram_tensor("w2bf_d", [DEPTH, DI, C], BF16, kind="Internal").ap()
    w1gbf_d = nc.dram_tensor("w1gbf_d", [DEPTH, C, DI], BF16, kind="Internal").ap()

    with ExitStack() as st:
        pl = Pl(nc, st)

        def mksb(stk, side=None):
            def sb(name, shape, dt=F32):
                kw = {"side": side} if side else {}
                return stk.enter_context(nc.sbuf_tensor(name, shape, dt, **kw))
            return sb
        sb = mksb(st)

        def mkpsum(name, shape, dt=F32):
            return st.enter_context(nc.psum_tensor(name, shape, dt))

        CS = {}
        for k, v in HC.items():
            CS[k] = sb("cs_" + k, list(v.shape), BF16 if k in CAST_CONSTS else F32)
        const_pool = sb("const_pool", [128, 8], F32)
        for i, val in enumerate([-LN16, 1e-6, 1e-12, 0.5, 2.0, -1.0]):
            nc.const_aps.register(F32, val, const_pool[:, i:i + 1])
            pl.op("gpsimd", lambda e, i=i, val=val: e.memset(const_pool[:, i:i + 1], val), inc="g")
        for k in HC:
            if k in CAST_CONSTS:
                pl.dma("gpsimd", CS[k][:], din["c_" + k], "d_cg")
            else:
                pl.dma("sync", CS[k][:], din["c_" + k], "d_cs")
        consts_tok = pl.chan_toks("d_cg") + pl.chan_toks("d_cs") + [pl.tok("g")]

        PS = [mkpsum(f"PS{i}", [128, 512]) for i in range(6)]
        PK = [mkpsum(f"PK{i}", [128, 512]) for i in range(2)]

        with ExitStack() as st0:
            sb0 = mksb(st0)
            p0_x = [sb0(f"p0_x{i}", [128, C]) for i in range(3)]
            p0_stat = [sb0(f"p0_st{i}", [128, 6]) for i in range(3)]
            p0_sq = sb0("p0_sq", [128, C], BF16)
            p0_xn = [sb0(f"p0_xn{i}", [128, C]) for i in range(2)]
            p0_cm = [sb0(f"p0_cm{i}", [128, 128]) for i in range(4)]

            n_tok = S // 128
            for tt in range(n_tok):
                bx = p0_x[tt % 3]
                stt_ = p0_stat[tt % 3]
                ld = pl.dma("sync", bx[:], x_h[tt * 128:(tt + 1) * 128, :], "d_x",
                            wait=[pl.get(f"p0x_done_{tt - 3}")])
                t1 = pl.op("vector", lambda e, bx=bx, stt_=stt_: e.reduce_sum(
                    out=stt_[:, 0:1], in_=bx[:], axis=AX),
                    wait=[ld, pl.get(f"p0st_done_{tt - 3}")], inc="v")
                t2 = pl.op("scalar", lambda e, bx=bx, stt_=stt_: e.activation(
                    out=p0_sq[:], in_=bx[:], func=AF.Square, accum_out=stt_[:, 1:2]),
                    wait=[ld, t1], inc="s")
                t3 = pl.op("vector", lambda e, stt_=stt_: e.tensor_scalar(
                    out=stt_[:, 2:3], in0=stt_[:, 0:1], scalar1=1.0 / C, scalar2=None,
                    op0=ALU.mult), wait=[t1], inc="v")
                t4 = pl.op("vector", lambda e, stt_=stt_: e.tensor_scalar(
                    out=stt_[:, 3:4], in0=stt_[:, 2:3], scalar1=stt_[:, 2:3], scalar2=None,
                    op0=ALU.mult), wait=[t3], inc="v")
                t5 = pl.op("vector", lambda e, stt_=stt_: e.scalar_tensor_tensor(
                    out=stt_[:, 3:4], in0=stt_[:, 1:2], scalar=1.0 / C, in1=stt_[:, 3:4],
                    op0=ALU.mult, op1=ALU.subtract), wait=[t2, t4], inc="v")
                t6 = pl.op("vector", lambda e, stt_=stt_: e.tensor_scalar(
                    out=stt_[:, 3:4], in0=stt_[:, 3:4], scalar1=1.0, scalar2=1e-5,
                    op0=ALU.mult, op1=ALU.add), wait=[t5], inc="v")
                t7 = pl.op("vector", lambda e, stt_=stt_: e.reciprocal(
                    out=stt_[:, 4:5], in_=stt_[:, 3:4]), wait=[t6], inc="v")
                t8 = pl.op("scalar", lambda e, stt_=stt_: e.activation(
                    out=stt_[:, 4:5], in_=stt_[:, 4:5], func=AF.Sqrt), wait=[t7], inc="s")
                t9 = pl.op("vector", lambda e, stt_=stt_: e.tensor_scalar(
                    out=stt_[:, 5:6], in0=stt_[:, 2:3], scalar1=stt_[:, 4:5], scalar2=-1.0,
                    op0=ALU.mult, op1=ALU.mult), wait=[t8], inc="v")
                bxn = p0_xn[tt % 2]
                t10 = pl.op("vector", lambda e, bx=bx, stt_=stt_, bxn=bxn: e.tensor_scalar(
                    out=bxn[:], in0=bx[:], scalar1=stt_[:, 4:5], scalar2=stt_[:, 5:6],
                    op0=ALU.mult, op1=ALU.add),
                    wait=[t9, pl.get(f"p0xn_done_{tt - 2}")], inc="v")
                pl.save(f"p0x_done_{tt}", t10)
                pl.save(f"p0st_done_{tt}", t10)
                trs = []
                for c in range(8):
                    bk = f"PS{c % 4}"
                    bank = PS[c % 4]
                    tr = pl.op("tensor", lambda e, bxn=bxn, c=c, bank=bank: e.transpose(
                        out=bank[:, 0:128], in_=bxn[:, c * 128:(c + 1) * 128],
                        identity=CS["eye"][:]),
                        wait=[t10, pl.bwait(bk)] + consts_tok, inc="pe")
                    stg = p0_cm[c % 4]
                    e2 = pl.op("scalar", lambda e, bank=bank, stg=stg: e.copy(
                        out=stg[:], in_=bank[:, 0:128]),
                        wait=[tr, pl.get(f"p0cmd_{tt}_{c - 4}"), pl.get(f"p0cmd_{tt - 1}_{c + 4}")],
                        inc="s")
                    pl.buse(bk, e2)
                    d2 = pl.dma("sync", xn_cm_d[c, :, tt * 128:(tt + 1) * 128], stg[:], "d_xn",
                                wait=[e2], slots=8)
                    pl.save(f"p0cmd_{tt}_{c}", d2)
                    trs.append(tr)
                pl.save(f"p0xn_done_{tt}", trs[-1])
            p0_done = [pl.tok("v"), pl.tok("s"), pl.tok("pe")] + pl.chan_toks("d_xn", 8)

        t_xn = tap("xn_cm", (8, 128, S))
        if t_xn is not None:
            pl.dma("sync", t_xn, xn_cm_d, "d_dbg", wait=p0_done)

        with ExitStack() as stp:
            sbp = mksb(stp)

            slotA = {"cw": sbp("cwA", [128, 6 * 1024], BF16),
                     "rw": sbp("rwA", [128, 8 * 768], BF16),
                     "pw": sbp("pwA", [128, 6 * 1024], BF16)}
            st1 = ExitStack()
            sb1 = mksb(st1, side="right")
            slotB = {"cw": sb1("cwB", [128, 6 * 1024], BF16),
                     "rw": sb1("rwB", [128, 8 * 768], BF16),
                     "pw": sb1("pwB", [128, 6 * 1024], BF16)}

            def load_proj_weights(slot, i, wait):
                tks = []
                for t3 in range(3):
                    for u in range(2):
                        tks.append(pl.dma("gpsimd", slot["cw"][:, (t3 * 2 + u) * 1024:(t3 * 2 + u + 1) * 1024],
                                          din["conv_k"][i, t3, u * 128:(u + 1) * 128, :], "d_w",
                                          wait=wait, slots=6))
                for kc in range(8):
                    for g in range(3):
                        tks.append(pl.dma("gpsimd", slot["rw"][:, kc * 768 + g * 256: kc * 768 + (g + 1) * 256],
                                          din["rff_w"][i, g, kc * 128:(kc + 1) * 128, :], "d_w",
                                          wait=wait, slots=6))
                pwt = []
                for fc in range(6):
                    pwt.append(pl.dma("gpsimd", slot["pw"][:, fc * 1024:(fc + 1) * 1024],
                                      din["proj_w"][i, fc * 128:(fc + 1) * 128, :], "d_w",
                                      wait=wait, slots=6))
                tks += pwt
                tks.append(pl.op("vector", lambda e, slot=slot: e.tensor_scalar(
                    out=slot["pw"][:], in0=slot["pw"][:], scalar1=SQ2M, scalar2=None,
                    op0=ALU.mult), wait=pwt, inc="v"))
                return tks

            wA_tok = load_proj_weights(slotA, 1, p0_done)
            wB_tok = load_proj_weights(slotB, 2, p0_done)

            ombf = sbp("ombf", [128, NRF], BF16)
            om_tok1 = pl.dma("gpsimd", ombf[0:64, :], din["omega"], "d_w", wait=p0_done)
            om_tok2 = pl.dma("gpsimd", ombf[64:128, :], din["omega"], "d_w", wait=p0_done)
            om_tok = [om_tok1, om_tok2]

            def load_colvec(tag, src2d, ncol, wait=()):
                stage = sbp(f"lcv_s_{tag}", [ncol, 128])
                out = sbp(f"lcv_{tag}", [128, ncol])
                d = pl.dma("sync", stage[:], src2d, "d_lcv", wait=list(wait) + p0_done)
                tr = pl.op("tensor", lambda e: e.transpose(
                    out=PS[5][:, 0:ncol], in_=stage[:], identity=CS["eye"][0:ncol, 0:ncol]),
                    wait=[d, pl.bwait("PS5")] + consts_tok, inc="pe")
                ev = pl.op("vector", lambda e: e.tensor_copy(out=out[:], in_=PS[5][:, 0:ncol]),
                           wait=[tr], inc="v")
                pl.buse("PS5", ev)
                return out, ev

            cb_col = {}; pb_col = {}; rb_col = {}
            for i in (0, 1, 2):
                cb_col[i] = load_colvec(f"cb{i}", din["conv_b"][i, :].rearrange("(a p) -> a p", p=128), 8)
                pb_col[i] = load_colvec(f"pb{i}", din["proj_b"][i, :].rearrange("(a p) -> a p", p=128), 8)
                col, tokc = load_colvec(f"rb{i}", din["rff_b"][i].rearrange("g (a p) -> (g a) p", p=128), 6)
                tr = pl.op("vector", lambda e, col=col: e.tensor_scalar(
                    out=col[:], in0=col[:], scalar1=1.0 / TWO_PI, scalar2=0.25,
                    op0=ALU.mult, op1=ALU.add), wait=[tokc], inc="v")
                rb_col[i] = (col, tr)
            outb_col = load_colvec("outb", din["out_b"].rearrange("(a p) -> a p", p=128), 8)

            lam_tok_sb = sbp("lam_tok", [NLM, DH])
            lam2 = sbp("lam2", [128, 128], BF16)
            lamn = sbp("lamn", [128, 1])
            lsq = sbp("lsq", [NLM, DH], BF16)
            lam_d = pl.dma("sync", lam_tok_sb[:], din["landmarks"], "d_lcv", wait=p0_done)
            g0 = pl.op("gpsimd", lambda e: e.memset(lam2[:], 0.0), wait=p0_done, inc="g")
            lt = pl.op("tensor", lambda e: e.transpose(
                out=PS[5][0:64, 0:64], in_=lam_tok_sb[:], identity=CS["eye"][0:64, 0:64]),
                wait=[lam_d, pl.bwait("PS5")] + consts_tok, inc="pe")
            l1 = pl.op("vector", lambda e: e.tensor_scalar(
                out=lam2[0:64, 0:64], in0=PS[5][0:64, 0:64], scalar1=2.0 / 64.0, scalar2=None,
                op0=ALU.mult), wait=[lt, g0], inc="v")
            l2 = pl.op("vector", lambda e: e.tensor_scalar(
                out=lam2[64:128, 64:128], in0=PS[5][0:64, 0:64], scalar1=2.0 / 64.0, scalar2=None,
                op0=ALU.mult), wait=[l1], inc="v")
            pl.buse("PS5", l2)
            l3 = pl.op("scalar", lambda e: e.activation(out=lsq[:], in_=lam_tok_sb[:], func=AF.Square),
                       wait=[lam_d], inc="s")
            l4 = pl.op("vector", lambda e: e.reduce_sum(out=lamn[0:64, 0:1], in_=lsq[:], axis=AX),
                       wait=[l3], inc="v")
            l5 = pl.op("vector", lambda e: e.tensor_scalar(
                out=lamn[0:64, 0:1], in0=lamn[0:64, 0:1], scalar1=-1.0 / 64.0, scalar2=None,
                op0=ALU.mult), wait=[l4], inc="v")
            l6 = pl.op("vector", lambda e: e.tensor_copy(out=lamn[64:128, 0:1], in_=lamn[0:64, 0:1]),
                       wait=[l5], inc="v")
            lam_ready = [l2, l6]

            xn_t = [sbp(f"xn_t{i}", [128, 8 * 514], BF16) for i in range(2)]
            h_bf = sbp("h_bf", [128, 4096], BF16)
            f_bf = sbp("f_bf", [128, 3072], BF16)
            proj_sb = sbp("proj_sb", [128, 4096], BF16)
            r_bf = sbp("r_bf", [128, 4096], BF16)
            sq_bf = sbp("sq_bf", [128, 2 * 512], BF16)
            rr16 = sbp("rr16", [16, 512], BF16)
            n16 = sbp("n16", [16, 512])
            n16q = sbp("n16q", [16, 512], BF16)
            kncols = sbp("kncols", [128, 64])
            kp_bf = sbp("kp_bf", [128, 512], BF16)
            qp_bf = sbp("qp_bf", [128, 2 * 1024], BF16)
            kvT_bf = sbp("kvT_bf", [128, 16 * 130], BF16)
            zt_sb = sbp("zt_sb", [128, 2 * 512])
            zr_sb = sbp("zr_sb", [128, 2 * 512])
            zi_sb = sbp("zi_sb", [128, 2 * 512], I32)
            rt1 = sbp("rt1", [128, 2 * 512], BF16)
            rt2 = sbp("rt2", [128, 2 * 512], BF16)
            tabc = [sbp("tabc0", [128, 512], BF16)]
            tabs = [sbp("tabs0", [128, 512], BF16)]
            pbrow = sbp("pbrow", [1, 1024], BF16)
            pbrow_tok = pl.dma("gpsimd", pbrow[:], din["proj_b"][2:3, :], "d_w", wait=p0_done)

            v_aug = sb1("v_aug", [128, 4160], BF16)
            kv_acc = sb1("kv_acc", [65, 16 * 256])

            va_ones = pl.op("gpsimd", lambda e: e.memset(
                v_aug[:].rearrange("p (h d) -> p h d", d=65)[:, :, 64:65], 1.0), inc="g")

            def emit_precast(n):
                pass

            def load_xn_tile(tile):
                buf = xn_t[tile % 2]
                prev = [pl.get(f"xn_used_{tile - 2}")]
                toks = []
                for u in range(8):
                    lo = tile * 512 - 1
                    ncols = 514
                    dst0 = 514 * u
                    if tile == 0:
                        toks.append(pl.op("gpsimd", lambda e, dst0=dst0, buf=buf: e.memset(
                            buf[:, dst0:dst0 + 1], 0.0), wait=prev, inc="g"))
                        lo = 0; ncols = 513; dst0 += 1
                    if tile == NT - 1:
                        toks.append(pl.op("gpsimd", lambda e, u=u, buf=buf: e.memset(
                            buf[:, 514 * u + 513:514 * u + 514], 0.0), wait=prev, inc="g"))
                        ncols -= 1
                    toks.append(pl.dma("gpsimd", buf[:, dst0:dst0 + ncols],
                                       xn_cm_d[u, :, lo:lo + ncols], "d_xt",
                                       wait=prev + p0_done, slots=4))
                return buf, toks

            def load_tab(tile, cos_name, sin_name):
                s0 = tile * 512
                tc, ts_ = tabc[0], tabs[0]
                tw = [pl.get(f"tab_used_{tile - 1}")] + (p0_done if tile == 0 else [])
                d1 = pl.dma("sync", tc[:], din[cos_name][:, s0:s0 + 512], "d_tab",
                            wait=tw, slots=2)
                d2 = pl.dma("sync", ts_[:], din[sin_name][:, s0:s0 + 512], "d_tab",
                            wait=tw, slots=2)
                return tc, ts_, [d1, d2]

            def conv_rff(tag, slot, i, xnbuf, xn_toks, w_tok):
                hk_toks = []
                for c in range(8):
                    bk = f"PS{c % 2}"
                    bank = PS[c % 2]

                    def mmf(e, c=c, slot=slot, xnbuf=xnbuf, bank=bank):
                        g = c // 2
                        prs = []
                        for t3 in range(3):
                            for uu in range(2):
                                prs.append((slot["cw"][:, (t3 * 2 + uu) * 1024 + c * 128:
                                                       (t3 * 2 + uu) * 1024 + (c + 1) * 128],
                                            xnbuf[:, 514 * (2 * g + uu) + t3:
                                                  514 * (2 * g + uu) + t3 + 512]))
                        return em_mm_group(e, bank[:], prs)
                    mt = pl.op("tensor", mmf, wait=[pl.bwait(bk)] + list(w_tok) + xn_toks, inc="pe")
                    ev = pl.op("vector", lambda e, c=c, bank=bank, i=i: e.tensor_scalar(
                        out=h_bf[:, c * 512:(c + 1) * 512], in0=bank[:], scalar1=1.0,
                        scalar2=cb_col[i][0][:, c:c + 1], op0=ALU.mult, op1=ALU.add),
                        wait=[mt, cb_col[i][1], pl.get("h_user")], inc="v")
                    pl.buse(bk, ev)
                    hk_toks.append(ev)
                f_toks = []
                last_rff_mm = []
                for g3 in range(3):
                    gg = float(np.sqrt(2.0 * GAMMAS[g3]) / TWO_PI)
                    for mc in range(2):
                        u = g3 * 2 + mc
                        bk = f"PS{2 + u % 2}"
                        bank = PS[2 + u % 2]

                        def mmf(e, g3=g3, mc=mc, slot=slot, bank=bank):
                            prs = [(slot["rw"][:, kc * 768 + g3 * 256 + mc * 128:
                                               kc * 768 + g3 * 256 + (mc + 1) * 128],
                                    h_bf[:, kc * 512:(kc + 1) * 512]) for kc in range(8)]
                            return em_mm_group(e, bank[:], prs)
                        mt = pl.op("tensor", mmf, wait=hk_toks + [pl.bwait(bk)], inc="pe")
                        last_rff_mm.append(mt)
                        zp = u % 2
                        zt = zt_sb[:, 512 * zp:512 * (zp + 1)]
                        zr = zr_sb[:, 512 * zp:512 * (zp + 1)]
                        zi = zi_sb[:, 512 * zp:512 * (zp + 1)]
                        c1 = pl.op("vector", lambda e, bank=bank, gg=gg, u=u, i=i, zt=zt:
                                   e.tensor_scalar(
                                       out=zt, in0=bank[:], scalar1=gg,
                                       scalar2=rb_col[i][0][:, u:u + 1],
                                       op0=ALU.mult, op1=ALU.add),
                                   wait=[mt, rb_col[i][1], pl.get(f"zt_user_{zp}")], inc="v")
                        pl.buse(bk, c1)
                        c2 = pl.op("vector", lambda e, zt=zt, zi=zi: e.tensor_copy(
                            out=zi, in_=zt), wait=[c1, pl.get(f"zi_user_{zp}")], inc="v")
                        c3 = pl.op("vector", lambda e, zi=zi, zr=zr: e.tensor_copy(
                            out=zr, in_=zi), wait=[c2, pl.get(f"zr_user_{zp}")], inc="v")
                        pl.save(f"zi_user_{zp}", c3)
                        c4 = pl.op("vector", lambda e, zt=zt, zr=zr: e.tensor_sub(zr, zt, zr),
                                   wait=[c3], inc="v")
                        c4b = pl.op("vector", lambda e, zt=zt, zr=zr: e.scalar_tensor_tensor(
                            out=zt, in0=zr, scalar=0.5, in1=zr,
                            op0=ALU.is_gt, op1=ALU.subtract), wait=[c4], inc="v")
                        c4c = pl.op("vector", lambda e, zt=zt, zr=zr: e.scalar_tensor_tensor(
                            out=zt, in0=zr, scalar=-0.5, in1=zt,
                            op0=ALU.is_lt, op1=ALU.subtract), wait=[c4b], inc="v")
                        pl.save(f"zr_user_{zp}", c4c)
                        c5 = pl.op("scalar", lambda e, u=u, zt=zt: e.activation(
                            out=f_bf[:, u * 512:(u + 1) * 512], in_=zt, func=AF.Sin,
                            scale=TWO_PI), wait=[c4c, pl.get(f"f_user_{u}")], inc="s")
                        pl.save(f"zt_user_{zp}", c5)
                        f_toks.append(c5)
                pl.save("h_user", last_rff_mm[-1])
                return f_toks

            def proj_cm(tag, slot, i, f_toks, dst):
                d_toks = []
                for c in range(8):
                    bk = f"PS{c % 2}"
                    bank = PS[c % 2]

                    def mmf(e, c=c, slot=slot, bank=bank):
                        prs = [(slot["pw"][:, fc * 1024 + c * 128: fc * 1024 + (c + 1) * 128],
                                f_bf[:, fc * 512:(fc + 1) * 512]) for fc in range(6)]
                        return em_mm_group(e, bank[:], prs)
                    mt = pl.op("tensor", mmf, wait=f_toks + [pl.bwait(bk)], inc="pe")
                    ev = pl.op("vector", lambda e, c=c, bank=bank, dst=dst, i=i: e.tensor_scalar(
                        out=dst[:, c * 512:(c + 1) * 512], in0=bank[:], scalar1=1.0,
                        scalar2=pb_col[i][0][:, c:c + 1], op0=ALU.mult, op1=ALU.add),
                        wait=[mt, pb_col[i][1], pl.get(f"proj_user_{c}")], inc="v")
                    pl.buse(bk, ev)
                    d_toks.append(ev)
                for u in range(6):
                    pl.save(f"f_user_{u}", pl.tok("pe"))
                return d_toks

            def rope_cm(tag, tile, src, dst, src_toks, tc, ts_, tab_toks):
                out_toks = []
                for c in range(8):
                    bk = f"PS{4 + c % 2}"
                    bank = PS[4 + c % 2]
                    par = c % 2
                    r1 = rt1[:, 512 * par:512 * (par + 1)]
                    r2 = rt2[:, 512 * par:512 * (par + 1)]
                    pm = pl.op("tensor", lambda e, c=c, src=src, bank=bank: e.matmul(
                        bank[:], CS["shuf"][:], src[:, c * 512:(c + 1) * 512],
                        start=True, stop=True),
                        wait=[src_toks[c], pl.bwait(bk)] + consts_tok, inc="pe")
                    m1 = pl.op("vector", lambda e, c=c, src=src, tc=tc, r1=r1: e.tensor_mul(
                        r1, src[:, c * 512:(c + 1) * 512], tc[:]),
                        wait=[src_toks[c], pl.get(f"rt1_user_{par}")] + tab_toks, inc="v")
                    m2 = pl.op("vector", lambda e, ts_=ts_, bank=bank, r2=r2: e.tensor_mul(
                        r2, bank[:], ts_[:]),
                        wait=[pm, pl.get(f"rt2_user_{par}")] + tab_toks, inc="v")
                    pl.buse(bk, m2)
                    m3 = pl.op("vector", lambda e, c=c, dst=dst, r1=r1, r2=r2: e.tensor_add(
                        dst[:, c * 512:(c + 1) * 512], r1, r2),
                        wait=[m1, m2, pl.get(f"rope_user_{c}")], inc="v")
                    pl.save(f"rt1_user_{par}", m3)
                    pl.save(f"rt2_user_{par}", m3)
                    pl.save(f"{tag}_srcuse_{c}", [pm, m1])
                    out_toks.append(m3)
                return out_toks

            for tile in range(NT):
                emit_precast(18)
                xnbuf, xn_toks = load_xn_tile(tile)
                tc, ts_, tab_toks = load_tab(tile, "t_pr_cos", "t_pr_sin")

                fv_toks = conv_rff("v", slotB, 2, xnbuf, xn_toks, wB_tok[-1:])
                v_toks = []
                for jj in range(4):
                    for n in range(2):
                        bk = f"PS{n}"
                        bank = PS[n]

                        def mmf(e, jj=jj, n=n, bank=bank):
                            prs = [(f_bf[:, fc * 512 + jj * 128: fc * 512 + jj * 128 + 128],
                                    slotB["pw"][:, fc * 1024 + n * 512: fc * 1024 + (n + 1) * 512])
                                   for fc in range(6)]
                            for q, (l, r) in enumerate(prs):
                                e.matmul(bank[:], l, r, start=(q == 0), stop=False)
                            return e.matmul(bank[:], CS["ones1x"][:],
                                            pbrow[:, n * 512:(n + 1) * 512],
                                            start=False, stop=True)
                        mt = pl.op("tensor", mmf,
                                   wait=fv_toks + [pl.bwait(bk), pbrow_tok] + consts_tok, inc="pe")
                        ev = pl.op("scalar", lambda e, jj=jj, n=n, bank=bank: e.copy(
                            out=v_aug[:, 1040 * jj:1040 * (jj + 1)].rearrange(
                                "p (h d) -> p h d", d=65)[:, 8 * n:8 * (n + 1), 0:64],
                            in_=bank[:].rearrange("p (h d) -> p h d", d=64)),
                            wait=[mt, va_ones, pl.get(f"vaug_user_{jj}_{n}")], inc="s")
                        pl.buse(bk, ev)
                        v_toks.append(ev)
                        pl.save(f"v_ready_{jj}_{n}", ev)
                for u in range(6):
                    pl.save(f"f_user_{u}", pl.tok("pe"))

                fk_toks = conv_rff("k", slotA, 1, xnbuf, xn_toks, wA_tok[-1:])
                kd_toks = proj_cm("k", slotA, 1, fk_toks, proj_sb)
                pl.save(f"xn_used_{tile}", [kd_toks[-1], pl.tok("pe")])
                kr_toks = rope_cm("k", tile, proj_sb, r_bf, kd_toks, tc, ts_, tab_toks)
                pl.save(f"tab_used_{tile}", pl.tok("v"))

                nA_mm = []
                for c in range(8):
                    spar = c % 2
                    sq = pl.op("scalar", lambda e, c=c, spar=spar: e.activation(
                        out=sq_bf[:, 512 * spar:512 * (spar + 1)],
                        in_=r_bf[:, c * 512:(c + 1) * 512], func=AF.Square),
                        wait=[kr_toks[c], pl.get(f"sq_user_{spar}")], inc="s")
                    mm = pl.op("tensor", lambda e, c=c, spar=spar: e.matmul(
                        PS[5][0:16, :], CS["bdh16"][:, 16 * c:16 * (c + 1)],
                        sq_bf[:, 512 * spar:512 * (spar + 1)],
                        start=(c == 0), stop=(c == 7)),
                        wait=[sq] + ([pl.bwait("PS5")] if c == 0 else []), inc="pe")
                    pl.save(f"sq_user_{spar}", mm)
                    nA_mm.append(mm)
                rrev = pl.op("scalar", lambda e: e.activation(
                    out=rr16[:], in_=PS[5][0:16, :], func=AF.Copy, scale=-1.0 / 64.0),
                    wait=[nA_mm[-1], pl.get("rr16_user")], inc="s")
                pl.buse("PS5", rrev)

                kny_toks = []
                rep_toks = []
                for c in range(8):
                    bk = f"PS{2 + c % 2}"
                    bank = PS[2 + c % 2]
                    mm1 = pl.op("tensor", lambda e, c=c, bank=bank: e.matmul(
                        bank[:], lam2[:], r_bf[:, c * 512:(c + 1) * 512], start=True, stop=False),
                        wait=[kr_toks[c], pl.bwait(bk)] + lam_ready, inc="pe")
                    mm2 = pl.op("tensor", lambda e, c=c, bank=bank: e.matmul(
                        bank[:], CS["sel2h"][:, 128 * c:128 * (c + 1)], rr16[:],
                        start=False, stop=True), wait=[rrev], inc="pe")
                    rep_toks.append(mm2)
                    ex = pl.op("scalar", lambda e, c=c, bank=bank: e.activation(
                        out=proj_sb[:, c * 512:(c + 1) * 512], in_=bank[:], func=AF.Exp,
                        bias=lamn[:, 0:1]),
                        wait=[mm2] + pl.get(f"k_srcuse_{c}") + [pl.get(f"proj_user_{c}")], inc="s")
                    pl.buse(bk, ex)
                    pl.save(f"proj_user_{c}", ex)
                    kny_toks.append(ex)
                pl.save("rr16_user", rep_toks[-1])

                nB_mm = []
                for c in range(8):
                    spar = c % 2
                    sq = pl.op("scalar", lambda e, c=c, spar=spar: e.activation(
                        out=sq_bf[:, 512 * spar:512 * (spar + 1)],
                        in_=proj_sb[:, c * 512:(c + 1) * 512], func=AF.Square),
                        wait=[kny_toks[c], pl.get(f"sq_user_{spar}")], inc="s")
                    mm = pl.op("tensor", lambda e, c=c, spar=spar: e.matmul(
                        PS[5][0:16, :], CS["bdh16"][:, 16 * c:16 * (c + 1)],
                        sq_bf[:, 512 * spar:512 * (spar + 1)],
                        start=(c == 0), stop=(c == 7)),
                        wait=[sq] + ([pl.bwait("PS5")] if c == 0 else []), inc="pe")
                    pl.save(f"sq_user_{spar}", mm)
                    nB_mm.append(mm)
                nev = pl.op("scalar", lambda e: e.activation(
                    out=n16[:], in_=PS[5][0:16, :], func=AF.Copy, scale=-0.5, bias=-LN16),
                    wait=[nB_mm[-1], pl.get("n16_user")], inc="s")
                pl.buse("PS5", nev)
                kc_toks = []
                for jj in range(4):
                    tr = pl.op("tensor", lambda e, jj=jj: e.transpose(
                        out=PS[5][:, 256 + jj * 16:256 + (jj + 1) * 16],
                        in_=n16[:, jj * 128:(jj + 1) * 128],
                        identity=CS["eye"][0:16, 0:16]),
                        wait=[nev, pl.bwait("PS5")], inc="pe")
                    ev = pl.op("vector", lambda e, jj=jj: e.tensor_copy(
                        out=kncols[:, jj * 16:(jj + 1) * 16],
                        in_=PS[5][:, 256 + jj * 16:256 + (jj + 1) * 16]),
                        wait=[tr, pl.get(f"kncols_user_{jj}")], inc="v")
                    pl.buse("PS5", ev)
                    kc_toks.append(ev)
                pl.save("n16_user", kc_toks[-1])

                for h in range(NH):
                    c, hh = h // 2, h % 2
                    for jp in range(2):
                        bk = f"PS{jp}"
                        bank = PS[jp]
                        mmk = []
                        for jo in range(2):
                            jj = jp * 2 + jo
                            mmk.append(pl.op("tensor", lambda e, c=c, hh=hh, jj=jj, jo=jo, bank=bank: e.matmul(
                                bank[:, jo * 256:(jo + 1) * 256],
                                proj_sb[64 * hh:64 * (hh + 1), c * 512 + jj * 128: c * 512 + (jj + 1) * 128],
                                ombf[64 * hh:64 * (hh + 1), :], start=(jo == 0), stop=(jo == 1)),
                                wait=[kny_toks[c], om_tok] + ([pl.bwait(bk)] if jo == 0 else []),
                                inc="pe"))
                        exs = []
                        for jo in range(2):
                            jj = jp * 2 + jo
                            exs.append(pl.op("scalar", lambda e, h=h, jj=jj, jo=jo, bank=bank: e.activation(
                                out=kp_bf[:, jo * 256:(jo + 1) * 256],
                                in_=bank[:, jo * 256:(jo + 1) * 256], func=AF.Exp,
                                bias=kncols[:, jj * 16 + h:jj * 16 + h + 1]),
                                wait=mmk + [kc_toks[jj], pl.get("kp_user")], inc="s"))
                        pl.buse(bk, exs[-1])
                        bkv = f"PK{h % 2}"
                        bankkv = PK[h % 2]

                        def kvf(e, h=h, jp=jp, tile=tile, bankkv=bankkv):
                            last = None
                            for jo in range(2):
                                jj = jp * 2 + jo
                                last = e.matmul(
                                    bankkv[0:65, 0:256],
                                    v_aug[:, 1040 * jj + 65 * h: 1040 * jj + 65 * (h + 1)],
                                    kp_bf[:, jo * 256:(jo + 1) * 256],
                                    start=(jp == 0 and jo == 0), stop=(jp == 1 and jo == 1))
                            return last
                        kvm = pl.op("tensor", kvf,
                                    wait=exs + [pl.get(f"v_ready_{jp * 2}_{h // 8}"),
                                                pl.get(f"v_ready_{jp * 2 + 1}_{h // 8}")]
                                    + ([pl.bwait(bkv)] if jp == 0 else []), inc="pe")
                        pl.save("kp_user", kvm)
                        if jp == 1:
                            if tile == 0:
                                kva = pl.op("vector", lambda e, h=h, bankkv=bankkv: e.tensor_copy(
                                    out=kv_acc[:, 256 * h:256 * (h + 1)], in_=bankkv[0:65, 0:256]),
                                    wait=[kvm], inc="v")
                            else:
                                kva = pl.op("vector", lambda e, h=h, bankkv=bankkv: e.tensor_add(
                                    kv_acc[:, 256 * h:256 * (h + 1)],
                                    kv_acc[:, 256 * h:256 * (h + 1)], bankkv[0:65, 0:256]),
                                    wait=[kvm], inc="v")
                            pl.buse(bkv, kva)
                petok = pl.tok("pe")
                stok = pl.tok("s")
                for jj in range(4):
                    for n in range(2):
                        pl.save(f"vaug_user_{jj}_{n}", petok)
                for c in range(8):
                    pl.save(f"proj_user_{c}", petok)
                    pl.save(f"rope_user_{c}", [petok, stok])
                for jj in range(4):
                    pl.save(f"kncols_user_{jj}", stok)

            kvT_toks = []
            vdone = pl.tok("v")
            for h in range(NH):
                for ec in range(2):
                    tr = pl.op("tensor", lambda e, h=h, ec=ec: e.transpose(
                        out=PS[5][:, 0:65],
                        in_=kv_acc[0:65, 256 * h + 128 * ec: 256 * h + 128 * (ec + 1)],
                        identity=CS["eye"][0:65, 0:65]),
                        wait=[vdone, pl.bwait("PS5")], inc="pe")
                    ev = pl.op("vector", lambda e, h=h, ec=ec: e.tensor_copy(
                        out=kvT_bf[:, 130 * h + 65 * ec: 130 * h + 65 * (ec + 1)],
                        in_=PS[5][:, 0:65]), wait=[tr], inc="v")
                    pl.buse("PS5", ev)
                    kvT_toks.append(ev)

            t_kv = tap("kv", (65, 16 * 256))
            if t_kv is not None:
                pl.dma("sync", t_kv, kv_acc[:], "d_dbg", wait=[vdone])
            t_kny = tap("kny", (128, 4096))
            if t_kny is not None:
                pl.dma("gpsimd", t_kny, proj_sb[:], "d_dbg2", wait=[pl.tok("s"), pl.tok("pe")])

            st1.close()
            st2 = ExitStack()
            sb2 = mksb(st2, side="right")
            num_sb = sb2("num_sb", [128, 8 * 512], BF16)
            den_one = [sb2(f"den_one{i}", [1, 512]) for i in range(2)]
            den16 = sb2("den16", [16, 512])
            den16b = sb2("den16b", [16, 512], BF16)
            invrep = sb2("invrep", [128, 512])
            attn_bf = sb2("attn_bf", [128, 4096], BF16)
            outw_sb = sb2("outw_sb", [128, 8 * 1024], BF16)
            ao_sb = sb2("ao_sb", [128, 4096])
            x_ld = [sb2("x_ld0", [128, C])]
            accst = [sb2("accst0", [128, C])]
            p1_all = [pl.tok("pe"), pl.tok("v"), pl.tok("s")]
            for key in ["xld_user", "accst_user", "den16_user", "invrep_user",
                        "denone_user_0", "denone_user_1"]:
                pl.save(key, p1_all)
            for c in range(8):
                pl.save(f"attn_user_{c}", p1_all)
            for cw_ in range(8):
                pl.save(f"numsb_user_{cw_}", p1_all)
            wQ_tok = load_proj_weights(slotA, 0, p1_all)
            outw_toks = [pl.dma("gpsimd", outw_sb[:, ic * 1024:(ic + 1) * 1024],
                                din["out_w"][ic * 128:(ic + 1) * 128, :], "d_w",
                                wait=p1_all, slots=6) for ic in range(8)]
            p1_pe = pl.tok("pe")
            for c in range(8):
                pl.save(f"proj_user_{c}", p1_pe)
            pl.save("xn_used_-2", pl.get(f"xn_used_{NT - 2}"))
            pl.save("xn_used_-1", pl.get(f"xn_used_{NT - 1}"))
            pl.save("tab_used_-1", pl.get(f"tab_used_{NT - 1}"))

            for tile in range(NT):
                emit_precast(18)
                xnbuf, xn_toks = load_xn_tile(tile)
                tc, ts_, tab_toks = load_tab(tile, "t_pr_cos", "t_pr_sin")
                fq_toks = conv_rff("q", slotA, 0, xnbuf, xn_toks, wQ_tok[-1:])
                qd_toks = proj_cm("q", slotA, 0, fq_toks, proj_sb)
                pl.save(f"xn_used_{tile}", [qd_toks[-1], pl.tok("pe")])
                qr_toks = rope_cm("q", tile, proj_sb, r_bf, qd_toks, tc, ts_, tab_toks)
                pl.save(f"tab_used_{tile}", pl.tok("v"))
                for c in range(8):
                    pl.save(f"proj_user_{c}", pl.get(f"q_srcuse_{c}"))

                nQ_mm = []
                for c in range(8):
                    spar = c % 2
                    sq = pl.op("scalar", lambda e, c=c, spar=spar: e.activation(
                        out=sq_bf[:, 512 * spar:512 * (spar + 1)],
                        in_=r_bf[:, c * 512:(c + 1) * 512], func=AF.Square),
                        wait=[qr_toks[c], pl.get(f"sq_user_{spar}")], inc="s")
                    mm = pl.op("tensor", lambda e, c=c, spar=spar: e.matmul(
                        PS[5][0:16, :], CS["bdh16"][:, 16 * c:16 * (c + 1)],
                        sq_bf[:, 512 * spar:512 * (spar + 1)],
                        start=(c == 0), stop=(c == 7)),
                        wait=[sq] + ([pl.bwait("PS5")] if c == 0 else []), inc="pe")
                    pl.save(f"sq_user_{spar}", mm)
                    nQ_mm.append(mm)
                nqe = pl.op("scalar", lambda e: e.activation(
                    out=n16q[:], in_=PS[5][0:16, :], func=AF.Copy, scale=-0.5, bias=-LN16),
                    wait=[nQ_mm[-1], pl.get("n16q_user")], inc="s")
                pl.buse("PS5", nqe)

                n16_users = []
                wave_dnd = {0: [], 1: []}

                def emit_fav(wave, h):
                    c, hh = h // 2, h % 2
                    qoff = 1024 * (h % 2)
                    mcs = []
                    for mc in range(2):
                        bk = f"PS{2 + mc}"
                        bank = PS[2 + mc]
                        m1 = pl.op("tensor", lambda e, c=c, hh=hh, mc=mc, bank=bank: e.matmul(
                            bank[:], ombf[64 * hh:64 * (hh + 1), 128 * mc:128 * (mc + 1)],
                            r_bf[64 * hh:64 * (hh + 1), c * 512:(c + 1) * 512],
                            start=True, stop=False),
                            wait=[qr_toks[c], om_tok, pl.bwait(bk)], inc="pe")
                        m2 = pl.op("tensor", lambda e, h=h, bank=bank: e.matmul(
                            bank[:], CS["sel16b"][:, 128 * h:128 * (h + 1)], n16q[:],
                            start=False, stop=True), wait=[nqe], inc="pe")
                        n16_users.append(m2)
                        ex = pl.op("scalar", lambda e, mc=mc, bank=bank, qoff=qoff:
                                   e.activation(
                                       out=qp_bf[:, qoff + mc * 512:qoff + (mc + 1) * 512],
                                       in_=bank[:], func=AF.Exp),
                                   wait=[m2, pl.get(f"qp_user_{h % 2}")], inc="s")
                        pl.buse(bk, ex)
                        mcs.append(ex)
                    fav_toks[h] = mcs

                def emit_num(wave, h):
                    c, hh = h // 2, h % 2
                    hs = h - 8 * wave
                    qoff = 1024 * (h % 2)
                    bkv = f"PK{h % 2}"
                    bankkv = PK[h % 2]

                    def numf(e, h=h, bankkv=bankkv, qoff=qoff):
                        last = None
                        for ec in range(2):
                            last = e.matmul(bankkv[0:65, :],
                                            kvT_bf[:, 130 * h + 65 * ec:130 * h + 65 * (ec + 1)],
                                            qp_bf[:, qoff + ec * 512:qoff + (ec + 1) * 512],
                                            start=(ec == 0), stop=(ec == 1))
                        return last
                    nm = pl.op("tensor", numf,
                               wait=fav_toks[h] + [kvT_toks[-1], pl.bwait(bkv)], inc="pe")
                    pl.save(f"qp_user_{h % 2}", nm)
                    do = den_one[h % 2]
                    dv = pl.op("scalar", lambda e, do=do, bankkv=bankkv: e.copy(
                        out=do[:], in_=bankkv[64:65, :]),
                        wait=[nm, pl.get(f"denone_user_{h % 2}")], inc="s")
                    dnd = pl.dma("sync", bounce[(2 * tile + wave) % 4, 512 * hs:512 * (hs + 1)][None, :],
                                 do[:], "d_bn", wait=[dv], slots=4)
                    pl.save(f"denone_user_{h % 2}", dnd)
                    wave_dnd[wave].append(dnd)
                    nv = pl.op("vector", lambda e, hh=hh, c=c, bankkv=bankkv: e.tensor_copy(
                        out=num_sb[64 * hh:64 * (hh + 1), 512 * c:512 * (c + 1)],
                        in_=bankkv[0:64, :]),
                        wait=[dv, pl.get(f"numsb_user_{c}")], inc="v")
                    pl.buse(bkv, nv)

                def emit_wave_den(wave):
                    brow = (2 * tile + wave) % 4
                    dd2 = pl.dma("sync", den16[0:8, :], bounce[brow, 0:4096].rearrange(
                        "(p f) -> p f", p=8), "d_bn",
                        wait=wave_dnd[wave] + [pl.get("den16_user")], slots=2)
                    iv1 = pl.op("vector", lambda e: e.tensor_scalar(
                        out=den16[0:8, :], in0=den16[0:8, :], scalar1=1.0, scalar2=1e-6,
                        op0=ALU.mult, op1=ALU.add), wait=[dd2], inc="v")
                    iv1b = pl.op("vector", lambda e: e.reciprocal(
                        out=den16[0:8, :], in_=den16[0:8, :]), wait=[iv1], inc="v")
                    iv2 = pl.op("scalar", lambda e: e.activation(
                        out=den16b[0:8, :], in_=den16[0:8, :], func=AF.Copy),
                        wait=[iv1b], inc="s")
                    for c in range(4 * wave, 4 * wave + 4):
                        rm = pl.op("tensor", lambda e, c=c, wave=wave: e.matmul(
                            PS[4][:], CS["sel2h"][0:8, 128 * (c - 4 * wave):128 * (c - 4 * wave + 1)],
                            den16b[0:8, :], start=True, stop=True),
                            wait=[iv2, pl.bwait("PS4")], inc="pe")
                        re_ = pl.op("scalar", lambda e: e.copy(out=invrep[:], in_=PS[4][:]),
                                    wait=[rm, pl.get("invrep_user")], inc="s")
                        pl.buse("PS4", re_)
                        tt = pl.op("vector", lambda e, c=c: e.tensor_mul(
                            attn_bf[:, c * 512:(c + 1) * 512],
                            num_sb[:, 512 * c:512 * (c + 1)],
                            invrep[:]),
                            wait=[re_, pl.get(f"attn_user_{c}")], inc="v")
                        pl.save(f"numsb_user_{c}", tt)
                        pl.save("invrep_user", tt)
                    pl.save("den16_user", pl.tok("v"))

                for wave in range(2):
                    fav_toks = {}
                    hlist = list(range(8 * wave, 8 * wave + 8))
                    for idx in range(9):
                        if idx < 8:
                            emit_fav(wave, hlist[idx])
                        if idx >= 1:
                            emit_num(wave, hlist[idx - 1])
                    if wave == 1:
                        emit_wave_den(0)
                emit_wave_den(1)
                pl.save("n16q_user", n16_users[-1])
                att_toks = [pl.tok("v")]

                ao_toks = []
                for oc in range(8):
                    bk = f"PS{oc % 2}"
                    bank = PS[oc % 2]

                    def mmf(e, oc=oc, bank=bank):
                        prs = [(outw_sb[:, ic * 1024 + oc * 128: ic * 1024 + (oc + 1) * 128],
                                attn_bf[:, ic * 512:(ic + 1) * 512]) for ic in range(8)]
                        return em_mm_group(e, bank[:], prs)
                    mt = pl.op("tensor", mmf, wait=att_toks + outw_toks + [pl.bwait(bk)], inc="pe")
                    ev = pl.op("vector", lambda e, oc=oc, bank=bank: e.tensor_scalar(
                        out=ao_sb[:, oc * 512:(oc + 1) * 512], in0=bank[:], scalar1=1.0,
                        scalar2=outb_col[0][:, oc:oc + 1], op0=ALU.mult, op1=ALU.add),
                        wait=[mt, outb_col[1], pl.get(f"ao_user_{oc}")], inc="v")
                    pl.buse(bk, ev)
                    ao_toks.append(ev)
                tr_last = {}
                for jj in range(4):
                    xl = x_ld[0]
                    ac = accst[0]
                    lx = pl.dma("sync", xl[:], x_h[tile * 512 + jj * 128: tile * 512 + (jj + 1) * 128, :],
                                "d_x2", wait=[pl.get("xld_user")], slots=2)
                    adds = []
                    for oc in range(8):
                        tbk = f"PS{4 + oc % 2}"
                        tbank = PS[4 + oc % 2]
                        tr = pl.op("tensor", lambda e, jj=jj, oc=oc, tbank=tbank: e.transpose(
                            out=tbank[:, 128:256],
                            in_=ao_sb[:, oc * 512 + jj * 128: oc * 512 + (jj + 1) * 128],
                            identity=CS["eye"][:]),
                            wait=[ao_toks[oc], pl.bwait(tbk)], inc="pe")
                        tr_last[oc] = tr
                        ad = pl.op("vector", lambda e, oc=oc, xl=xl, ac=ac, tbank=tbank: e.tensor_add(
                            ac[:, oc * 128:(oc + 1) * 128], tbank[:, 128:256],
                            xl[:, oc * 128:(oc + 1) * 128]),
                            wait=[tr, lx, pl.get("accst_user")], inc="v")
                        pl.buse(tbk, ad)
                        adds.append(ad)
                    pl.save("xld_user", adds[-1])
                    da = pl.dma("sync", acc_d[tile * 512 + jj * 128: tile * 512 + (jj + 1) * 128, :],
                                ac[:], "d_acc", wait=adds, slots=4)
                    pl.save("accst_user", da)
                for oc in range(8):
                    pl.save(f"ao_user_{oc}", tr_last[oc])
                petok = pl.tok("pe")
                stok = pl.tok("s")
                for c in range(8):
                    pl.save(f"attn_user_{c}", petok)
                    pl.save(f"rope_user_{c}", [petok, stok])

            st2.close()
            perf_done = [pl.tok("v"), pl.tok("s"), pl.tok("pe")] + pl.chan_toks("d_acc", 4)

            t_acc = tap("acc", (S, C))
            if t_acc is not None:
                pl.dma("sync", t_acc, acc_d, "d_dbg", wait=perf_done)


        with ExitStack() as stf:
            sbf = mksb(stf)
            xf_in = [sbf(f"xf_in{i}", [128, 8 * 514]) for i in range(2)]
            xs_bf = sbf("xs_bf", [128, 8 * 512], BF16)
            sqf_bf = sbf("sqf_bf", [128, 2 * 512], BF16)
            ftab = [sbf(f"ftab{i}", [128, 512], BF16) for i in range(4)]
            xf_st = [sbf(f"xf_st{i}", [128, 512]) for i in range(2)]
            frt1 = sbf("frt1", [128, 512])
            frt2 = sbf("frt2", [128, 512])
            frp1 = sbf("frp1", [128, 2 * 512], BF16)
            frp2 = sbf("frp2", [128, 2 * 512], BF16)
            eye_bf = sbf("eye_bf", [128, 128], BF16)
            onescol_bf = sbf("onescol_bf", [128, 1], BF16)
            statA = sbf("statA", [1, 512])
            statR = sbf("statR", [1, 512])
            statR_bf = sbf("statR_bf", [1, 512], BF16)
            statB2 = sbf("statB2", [2, 512], BF16)
            growf = sbf("growf", [2, C])
            growf_bf = sbf("growf_bf", [2, C], BF16)
            selstage = sbf("selstage", [16, 1024])
            ec = pl.op("vector", lambda e: e.tensor_copy(out=eye_bf[:], in_=CS["eye"][:]),
                       wait=consts_tok + perf_done, inc="v")
            go1 = pl.op("gpsimd", lambda e: e.memset(onescol_bf[:], 1.0), wait=perf_done, inc="g")
            go2 = pl.op("gpsimd", lambda e: e.memset(statB2[:], -1.0), wait=perf_done, inc="g")
            ffn_setup = [ec, go1, go2]

            def sh0(c):
                return 514 * c + (1 if c < 4 else 0)

            def load_xf_tile(buf, src_d, tile, prev_users):
                toks = []
                for u in range(8):
                    lo = tile * 512 - 1
                    ncols = 514
                    dst0 = 514 * u
                    if tile == 0:
                        toks.append(pl.op("gpsimd", lambda e, dst0=dst0, buf=buf: e.memset(
                            buf[:, dst0:dst0 + 1], 0.0), wait=prev_users, inc="g"))
                        lo = 0; ncols = 513; dst0 += 1
                    if tile == NT - 1:
                        ncols -= 1
                    toks.append(pl.dma("sync", buf[:, dst0:dst0 + ncols],
                                       src_d[u, :, lo:lo + ncols], "d_xf",
                                       wait=prev_users, slots=4))
                return toks

            def colvec_ragged(tag, vec1d, n, wait=(), sba=None):
                sba = sba or sbf
                ncol = (n + 127) // 128
                nfull = n // 128
                tail = n - nfull * 128
                stage = sba(f"fcv_s_{tag}", [ncol, 128])
                out = sba(f"fcv_{tag}", [128, ncol])
                toks = []
                if tail:
                    toks.append(pl.op("gpsimd", lambda e: e.memset(stage[:], 0.0),
                                      wait=wait, inc="g"))
                toks.append(pl.dma("sync", stage[0:nfull, :],
                                   vec1d[0:nfull * 128].rearrange("(a p) -> a p", p=128),
                                   "d_lcv", wait=list(wait) + toks))
                if tail:
                    toks.append(pl.dma("sync", stage[nfull:nfull + 1, 0:tail],
                                       vec1d[nfull * 128:n][None, :], "d_lcv",
                                       wait=list(wait) + toks[:1]))
                tr = pl.op("tensor", lambda e: e.transpose(
                    out=PS[5][:, 0:ncol], in_=stage[:], identity=CS["eye"][0:ncol, 0:ncol]),
                    wait=toks + [pl.bwait("PS5")], inc="pe")
                ev = pl.op("vector", lambda e: e.tensor_copy(out=out[:], in_=PS[5][:, 0:ncol]),
                           wait=[tr], inc="v")
                pl.buse("PS5", ev)
                return out, ev

            prev_done = perf_done + ffn_setup
            src_d = xn_cm_d
            dst_d = xf_a

            import os as _os
            NLAY = int(_os.environ.get("FFN_LAYERS", DEPTH))
            for li in range(NLAY):
                stA = ExitStack()
                sbA = mksb(stA, side="right")
                wqkv = sbA(f"wqkv{li}", [128, 8 * 3072], BF16)
                wo = sbA(f"wo{li}", [128, 8 * 1024], BF16)
                q_bf = sbA(f"q_bf{li}", [128, 8 * 512], BF16)
                kr_cur = sbA(f"fkrc{li}", [128, 8 * 512], BF16)
                kr_prev = sbA(f"fkrp{li}", [128, 8 * 256], BF16)
                va_cur = sbA(f"fvac{li}", [128, 4 * 1040], BF16)
                va_prev = sbA(f"fvap{li}", [128, 2 * 1040], BF16)
                attn_cm = sbA(f"fattn_cm{li}", [128, 8 * 512], BF16)
                probsT = sbA(f"fprobsT{li}", [128, 2 * 384], BF16)
                probs3 = sbA(f"fprobs3{li}", [128, 3 * 384], BF16)
                maxng = sbA(f"fmaxng{li}", [128, 16])
                dencol = sbA(f"fdencol{li}", [128, 16])
                invden = sbA(f"finvden{li}", [128, 16])
                diag_bf = sbA(f"fdiag{li}", [128, 2 * 128], BF16)
                fn16q = sbA(f"ffn16q{li}", [16, 512], BF16)
                fn16k = sbA(f"ffn16k{li}", [16, 512], BF16)
                fnsq = sbA(f"ffnsq{li}", [16, 512])
                qsel_bf = sbA(f"fqselb{li}", [16, 8 * 128], BF16)
                ksel_bf = sbA(f"fkselb{li}", [16, 8 * 128], BF16)

                wq_toks = [pl.dma("gpsimd", wqkv[:, ic * 3072:(ic + 1) * 3072],
                                  din["Wqkv"][li, ic * 128:(ic + 1) * 128, :], "d_w",
                                  wait=prev_done, slots=6) for ic in range(8)]
                wo_toks = [pl.dma("gpsimd", wo[:, ic * 1024:(ic + 1) * 1024],
                                  din["Wo"][li, ic * 128:(ic + 1) * 128, :], "d_w",
                                  wait=prev_done, slots=6) for ic in range(8)]
                va_init = [pl.op("gpsimd", lambda e, vb=vb: e.memset(
                    vb[:].rearrange("p (h d) -> p h d", d=65)[:, :, 64:65], 1.0),
                    wait=prev_done, inc="g") for vb in (va_cur, va_prev)]
                selcast = []
                for scname, dst, scale in ((
                        "q_scale", qsel_bf, QK_SCALE), ("k_scale", ksel_bf, 1.0)):
                    selprev = prev_done + \
                        ([pl.get("selstage_u")] if pl.get("selstage_u") else [])
                    sz = pl.op("gpsimd", lambda e: e.memset(selstage[:], 0.0),
                               wait=selprev, inc="g")
                    selt = []
                    for c in range(8):
                        for r in range(2):
                            selt.append(pl.dma(
                                "sync", selstage[2 * c + r:2 * c + r + 1,
                                                 128 * c + 64 * r:128 * c + 64 * r + 64],
                                din[scname][li:li + 1, :], "d_lcv", wait=[sz]))
                    ct = pl.op("vector", lambda e, dst=dst, scale=scale: e.tensor_scalar(
                        out=dst[:], in0=selstage[:], scalar1=scale, scalar2=None,
                        op0=ALU.mult), wait=selt, inc="v")
                    pl.save("selstage_u", ct)
                    selcast.append(ct)

                pl.save("ftile_prev", prev_done)
                pl.save("ftile_prev2", prev_done)
                pl.save("fsq_user_0", None); pl.save("fsq_user_1", None)
                pl.save("fnu_q", prev_done); pl.save("fnu_k", prev_done)
                for par in range(2):
                    pl.save(f"diag_u{par}", prev_done)
                    pl.save(f"pT_u{par}", prev_done)
                for r3 in range(3):
                    pl.save(f"probs_u{r3}", prev_done)
                for h in range(NH):
                    pl.save(f"mx_u_{h}", prev_done)
                    pl.save(f"dc_u_{h}", prev_done)
                    pl.save(f"iv_u_{h}", prev_done)
                for c in range(8):
                    pl.save(f"fattn_u_{c}", prev_done)

                for tile in range(NT):
                    s0 = tile * 512
                    tprev = pl.get("ftile_prev")
                    tprev2 = pl.get("ftile_prev2")
                    xfb = xf_in[tile % 2]
                    xf_toks = load_xf_tile(xfb, src_d, tile, [tprev2])
                    tb_toks = []
                    for ti, nm in enumerate(["t_fx_cq", "t_fx_sq", "t_fx_ck", "t_fx_sk"]):
                        tb_toks.append(pl.dma("sync", ftab[ti][:], din[nm][:, s0:s0 + 512],
                                              "d_tab", wait=[tprev], slots=4))
                    xs_toks = []
                    for c in range(8):
                        xs_toks.append(pl.op("vector", lambda e, c=c, xfb=xfb: e.tensor_copy(
                            out=xs_bf[:, 512 * c:512 * (c + 1)],
                            in_=xfb[:, sh0(c):sh0(c) + 512]),
                            wait=xf_toks + [pl.get("xsbf_u")], inc="v"))
                    qs_toks = []; ks_toks = []; vv_toks = []
                    krb = kr_cur
                    vab = va_cur
                    for c in range(8):
                        bk = f"PS{c % 2}"
                        bank = PS[c % 2]

                        def mmq(e, c=c, bank=bank):
                            prs = [(wqkv[:, ic * 3072 + c * 128: ic * 3072 + (c + 1) * 128],
                                    xs_bf[:, 512 * ic:512 * (ic + 1)]) for ic in range(8)]
                            return em_mm_group(e, bank[:], prs)
                        mt = pl.op("tensor", mmq, wait=xs_toks + wq_toks + [pl.bwait(bk)], inc="pe")
                        ev = pl.op("vector", lambda e, c=c, bank=bank: e.tensor_copy(
                            out=q_bf[:, 512 * c:512 * (c + 1)], in_=bank[:]),
                            wait=[mt, tprev], inc="v")
                        pl.buse(bk, ev)
                        qs_toks.append(ev)
                    for c in range(8):
                        bk = f"PS{c % 2}"
                        bank = PS[c % 2]

                        def mmk(e, c=c, bank=bank):
                            prs = [(wqkv[:, ic * 3072 + 1024 + c * 128:
                                         ic * 3072 + 1024 + (c + 1) * 128],
                                    xs_bf[:, 512 * ic:512 * (ic + 1)]) for ic in range(8)]
                            return em_mm_group(e, bank[:], prs)
                        mt = pl.op("tensor", mmk, wait=xs_toks + wq_toks + [pl.bwait(bk)], inc="pe")
                        ev = pl.op("vector", lambda e, c=c, bank=bank, krb=krb: e.tensor_copy(
                            out=krb[:, 512 * c:512 * (c + 1)], in_=bank[:]),
                            wait=[mt, tprev], inc="v")
                        pl.buse(bk, ev)
                        ks_toks.append(ev)
                    for sl in range(4):
                        for n in range(2):
                            bk = f"PS{2 + n}"
                            bank = PS[2 + n]

                            def mmv(e, sl=sl, n=n, bank=bank):
                                prs = [(xs_bf[:, 512 * ic + 128 * sl: 512 * ic + 128 * (sl + 1)],
                                        wqkv[:, ic * 3072 + 2048 + n * 512:
                                             ic * 3072 + 2048 + (n + 1) * 512])
                                       for ic in range(8)]
                                return em_mm_group(e, bank[:], prs)
                            mt = pl.op("tensor", mmv, wait=xs_toks + wq_toks + [pl.bwait(bk)],
                                       inc="pe")
                            ev = pl.op("scalar", lambda e, sl=sl, n=n, bank=bank, vab=vab: e.copy(
                                out=vab[:, 1040 * sl:1040 * (sl + 1)].rearrange(
                                    "p (h d) -> p h d", d=65)[:, 8 * n:8 * (n + 1), 0:64],
                                in_=bank[:].rearrange("p (h d) -> p h d", d=64)),
                                wait=[mt, tprev] + va_init, inc="s")
                            pl.buse(bk, ev)
                            vv_toks.append(ev)

                    pl.save("xsbf_u", pl.tok("pe"))

                    def l2pipe(buf, selmat_bf, src_toks, nbk, fn16b, tag):
                        nbank = PS[4] if nbk == "PS4" else PS[5]
                        nmm = []
                        for c in range(8):
                            par = c % 2
                            sq = pl.op("scalar", lambda e, c=c, buf=buf, par=par: e.activation(
                                out=sqf_bf[:, 512 * par:512 * (par + 1)],
                                in_=buf[:, 512 * c:512 * (c + 1)], func=AF.Square),
                                wait=[src_toks[c], pl.get(f"fsq_user_{par}")], inc="s")
                            mm = pl.op("tensor", lambda e, c=c, par=par, nbank=nbank: e.matmul(
                                nbank[0:16, :], CS["bdh16"][:, 16 * c:16 * (c + 1)],
                                sqf_bf[:, 512 * par:512 * (par + 1)],
                                start=(c == 0), stop=(c == 7)),
                                wait=[sq] + ([pl.bwait(nbk)] if c == 0 else []), inc="pe")
                            pl.save(f"fsq_user_{par}", mm)
                            nmm.append(mm)
                        e0 = pl.op("scalar", lambda e, nbank=nbank: e.activation(
                            out=fnsq[:], in_=nbank[0:16, :], func=AF.Sqrt),
                            wait=[nmm[-1], pl.get(f"fnu_{tag}"), pl.get("fnsq_u")], inc="s")
                        pl.buse(nbk, e0)
                        e0b = pl.op("vector", lambda e: e.tensor_scalar(
                            out=fnsq[:], in0=fnsq[:], scalar1=1e-12, scalar2=None,
                            op0=ALU.max), wait=[e0], inc="v")
                        e0c = pl.op("vector", lambda e: e.reciprocal(
                            out=fnsq[:], in_=fnsq[:]), wait=[e0b], inc="v")
                        e1 = pl.op("scalar", lambda e, fn16b=fn16b: e.activation(
                            out=fn16b[:], in_=fnsq[:], func=AF.Copy), wait=[e0c], inc="s")
                        pl.save("fnsq_u", e1)
                        outs = []
                        last_rm = None
                        for c in range(8):
                            bbk = f"PK{c % 2}"
                            bbank = PK[c % 2]
                            rm = pl.op("tensor", lambda e, c=c, selmat_bf=selmat_bf,
                                       fn16b=fn16b, bbank=bbank: e.matmul(
                                bbank[:, 0:512], selmat_bf[:, 128 * c:128 * (c + 1)], fn16b[:],
                                start=True, stop=True),
                                wait=[e1, pl.bwait(bbk)] + selcast, inc="pe")
                            last_rm = rm
                            tt = pl.op("vector", lambda e, c=c, buf=buf, bbank=bbank: e.tensor_mul(
                                buf[:, 512 * c:512 * (c + 1)], buf[:, 512 * c:512 * (c + 1)],
                                bbank[:, 0:512]), wait=[rm], inc="v")
                            pl.buse(bbk, tt)
                            outs.append(tt)
                        pl.save(f"fnu_{tag}", last_rm)
                        return outs
                    qn_toks = l2pipe(q_bf, qsel_bf, qs_toks, "PS4", fn16q, "q")
                    kn_toks = l2pipe(krb, ksel_bf, ks_toks, "PS5", fn16k, "k")

                    def ropef(buf, src_toks, tci, tsi):
                        outs = []
                        for c in range(8):
                            par = c % 2
                            bk = f"PS{4 + par}"
                            bank = PS[4 + par]
                            r1 = frp1[:, 512 * par:512 * (par + 1)]
                            r2 = frp2[:, 512 * par:512 * (par + 1)]
                            pm = pl.op("tensor", lambda e, c=c, buf=buf, bank=bank: e.matmul(
                                bank[:], CS["shuf"][:], buf[:, 512 * c:512 * (c + 1)],
                                start=True, stop=True),
                                wait=[src_toks[c], pl.bwait(bk)] + consts_tok, inc="pe")
                            m1 = pl.op("vector", lambda e, c=c, buf=buf, tci=tci, r1=r1: e.tensor_mul(
                                r1, buf[:, 512 * c:512 * (c + 1)], ftab[tci][:]),
                                wait=[src_toks[c], pl.get(f"frt1_u{par}")] + tb_toks, inc="v")
                            m2 = pl.op("vector", lambda e, tsi=tsi, bank=bank, r2=r2: e.tensor_mul(
                                r2, bank[:], ftab[tsi][:]),
                                wait=[pm, pl.get(f"frt2_u{par}")] + tb_toks, inc="v")
                            pl.buse(bk, m2)
                            m3 = pl.op("vector", lambda e, c=c, buf=buf, r1=r1, r2=r2: e.tensor_add(
                                buf[:, 512 * c:512 * (c + 1)], r1, r2),
                                wait=[m1, m2, tprev], inc="v")
                            pl.save(f"frt1_u{par}", m3)
                            pl.save(f"frt2_u{par}", m3)
                            outs.append(m3)
                        return outs
                    qr_toks = ropef(q_bf, qn_toks, 0, 1)
                    krr_toks = ropef(krb, kn_toks, 2, 3)

                    def kloc(gc):
                        if gc // 4 == tile:
                            return kr_cur, 128 * (gc % 4)
                        return kr_prev, 128 * ((gc % 4) - 2)

                    def vloc(gc):
                        if gc // 4 == tile:
                            return va_cur, 1040 * (gc % 4)
                        return va_prev, 1040 * ((gc % 4) - 2)

                    prev_k = pl.get("krr_prev") or []
                    prev_v = pl.get("vv_prev") or []

                    NSLOT = 4 * NH
                    sim_done = [None] * NSLOT
                    tmr_done = [None] * NSLOT
                    exp_done = [None] * NSLOT
                    diag_done = [None] * NSLOT
                    trdiv_done = [None] * NSLOT
                    evac_done = [None] * NSLOT
                    av_done = [None] * NSLOT
                    oc_toks = []

                    def slmeta(i):
                        sl, h = i // NH, i % NH
                        g = 4 * tile + sl
                        wchunks = [gc for gc in (g - 2, g - 1, g) if gc >= 0]
                        return sl, h, g, wchunks

                    def emit_sims(i):
                        sl, h, g, wchunks = slmeta(i)
                        c, hh = h // 2, h % 2
                        bk = f"PS{i % 2}"
                        bank = PS[i % 2]
                        runs = []
                        for gc in wchunks:
                            tb, col = kloc(gc)
                            blk = 256 if tb is kr_prev else 512
                            base = blk * c + col
                            if runs and runs[-1][0] is tb and runs[-1][1] + runs[-1][2] == base:
                                runs[-1] = (tb, runs[-1][1], runs[-1][2] + 128)
                            else:
                                runs.append((tb, base, 128))
                        nw = len(wchunks) * 128
                        toks = []
                        colpos = 0
                        for (tb, base, wdt) in runs:
                            toks.append(pl.op("tensor",
                                lambda e, tb=tb, base=base, wdt=wdt, c=c, hh=hh, sl=sl,
                                       bank=bank, colpos=colpos: e.matmul(
                                    bank[:, colpos:colpos + wdt],
                                    q_bf[64 * hh:64 * (hh + 1),
                                          512 * c + 128 * sl:512 * c + 128 * (sl + 1)],
                                    tb[64 * hh:64 * (hh + 1), base:base + wdt],
                                    start=(colpos == 0), stop=False),
                                wait=[qr_toks[c], krr_toks[c],
                                      pl.bwait(bk) if colpos == 0 else None] + prev_k,
                                inc="pe"))
                            colpos += wdt
                        mname = "mq_full" if g >= 2 else ("mq_s1" if g == 1 else "mq_s0")
                        toks.append(pl.op("tensor",
                            lambda e, bank=bank, mname=mname, nw=nw: e.matmul(
                                bank[:, 0:nw], eye_bf[:], CS[mname][:, 0:nw],
                                start=False, stop=True),
                            wait=[ec] + consts_tok, inc="pe"))
                        sim_done[i] = toks

                    def emit_tmr(i):
                        sl, h, g, wchunks = slmeta(i)
                        nw = len(wchunks) * 128
                        par = i % 2
                        bank = PS[par]
                        t = pl.op("vector", lambda e, bank=bank, nw=nw, h=h: e.reduce_max(
                            out=maxng[:, h:h + 1], in_=bank[:, 0:nw], axis=AX, negate=True),
                            wait=sim_done[i] + [pl.get(f"mx_u_{h}")], inc="v")
                        tmr_done[i] = t

                    def emit_exp(i):
                        sl, h, g, wchunks = slmeta(i)
                        nw = len(wchunks) * 128
                        par = i % 2
                        p3 = i % 3
                        bank = PS[par]
                        t = pl.op("scalar",
                                  lambda e, nw=nw, p3=p3, h=h, bank=bank: e.activation(
                                      out=probs3[:, 384 * p3:384 * p3 + nw],
                                      in_=bank[:, 0:nw],
                                      func=AF.Exp, bias=maxng[:, h:h + 1],
                                      accum_out=dencol[:, h:h + 1]),
                                  wait=[tmr_done[i], pl.get(f"probs_u{p3}"),
                                        pl.get(f"dc_u_{h}")], inc="s")
                        pl.buse(f"PS{par}", t)
                        pl.save(f"mx_u_{h}", t)
                        exp_done[i] = t

                    def emit_recip_diag(i):
                        sl, h, g, wchunks = slmeta(i)
                        par = i % 2
                        r = pl.op("vector", lambda e, h=h: e.reciprocal(
                            out=invden[:, h:h + 1], in_=dencol[:, h:h + 1]),
                            wait=[exp_done[i], pl.get(f"iv_u_{h}")], inc="v")
                        pl.save(f"dc_u_{h}", r)
                        d = pl.op("vector", lambda e, par=par, h=h: e.tensor_scalar(
                            out=diag_bf[:, 128 * par:128 * (par + 1)], in0=eye_bf[:],
                            scalar1=invden[:, h:h + 1], scalar2=None, op0=ALU.mult),
                            wait=[r, pl.get(f"diag_u{par}"), ec], inc="v")
                        pl.save(f"iv_u_{h}", d)
                        diag_done[i] = d

                    def emit_trdiv(i):
                        sl, h, g, wchunks = slmeta(i)
                        nch = len(wchunks)
                        par = i % 2
                        p3 = i % 3
                        bk = f"PS{2 + par}"
                        bank = PS[2 + par]
                        toks = []
                        for q3 in range(nch):
                            toks.append(pl.op("tensor",
                                lambda e, q3=q3, p3=p3, par=par, bank=bank: e.matmul(
                                    bank[:, 128 * q3:128 * (q3 + 1)],
                                    probs3[:, 384 * p3 + 128 * q3:384 * p3 + 128 * (q3 + 1)],
                                    diag_bf[:, 128 * par:128 * (par + 1)],
                                    start=True, stop=True),
                                wait=[exp_done[i], diag_done[i]] +
                                     ([pl.bwait(bk)] if q3 == 0 else []), inc="pe"))
                        pl.save(f"probs_u{p3}", toks)
                        pl.save(f"diag_u{par}", toks[-1])
                        trdiv_done[i] = toks

                    def emit_evac(i):
                        sl, h, g, wchunks = slmeta(i)
                        nw = len(wchunks) * 128
                        par = i % 2
                        t = pl.op("scalar", lambda e, par=par, nw=nw: e.copy(
                            out=probsT[:, 384 * par:384 * par + nw],
                            in_=PS[2 + par][:, 0:nw]),
                            wait=trdiv_done[i] + [pl.get(f"pT_u{par}")], inc="s")
                        pl.buse(f"PS{2 + par}", t)
                        evac_done[i] = t

                    def emit_av(i):
                        sl, h, g, wchunks = slmeta(i)
                        par = i % 2
                        bkv = f"PK{par}"
                        bankkv = PK[par]
                        vlocs = tuple(vloc(gc) for gc in wchunks)

                        def avf(e, h=h, vlocs=vlocs, bankkv=bankkv, par=par):
                            last = None
                            for q3, (vb, voff) in enumerate(vlocs):
                                last = e.matmul(
                                    bankkv[0:65, 0:128],
                                    vb[:, voff + 65 * h: voff + 65 * (h + 1)],
                                    probsT[:, 384 * par + 128 * q3:384 * par + 128 * (q3 + 1)],
                                    start=(q3 == 0), stop=(q3 == len(vlocs) - 1))
                            return last
                        t = pl.op("tensor", avf,
                                  wait=[evac_done[i], pl.bwait(bkv)] + vv_toks + prev_v,
                                  inc="pe")
                        pl.save(f"pT_u{par}", t)
                        av_done[i] = t

                    def emit_outcopy(i):
                        sl, h, g, wchunks = slmeta(i)
                        c, hh = h // 2, h % 2
                        par = i % 2
                        t = pl.op("vector", lambda e, c=c, hh=hh, sl=sl, par=par: e.tensor_copy(
                            out=attn_cm[64 * hh:64 * (hh + 1),
                                        512 * c + 128 * sl:512 * c + 128 * (sl + 1)],
                            in_=PK[par][0:64, 0:128]),
                            wait=[av_done[i], pl.get(f"fattn_u_{c}")], inc="v")
                        pl.buse(f"PK{par}", t)
                        oc_toks.append(t)

                    for i_ in range(NSLOT + 4):
                        if i_ < NSLOT:
                            emit_sims(i_)
                            emit_tmr(i_)
                            emit_exp(i_)
                        if 1 <= i_ <= NSLOT:
                            emit_recip_diag(i_ - 1)
                        if 2 <= i_ < NSLOT + 2:
                            emit_trdiv(i_ - 2)
                            emit_evac(i_ - 2)
                        if 3 <= i_ < NSLOT + 3:
                            emit_av(i_ - 3)
                        if 4 <= i_:
                            emit_outcopy(i_ - 4)

                    slpe = pl.tok("pe")
                    cp_toks = []
                    for c in range(8):
                        cp_toks.append(pl.op("vector", lambda e, c=c: e.tensor_copy(
                            out=kr_prev[:, 256 * c:256 * (c + 1)],
                            in_=kr_cur[:, 512 * c + 256:512 * c + 512]),
                            wait=[slpe], inc="v"))
                    for b2_ in range(2):
                        cp_toks.append(pl.op("vector", lambda e, b2_=b2_: e.tensor_copy(
                            out=va_prev[:, 1040 * b2_:1040 * (b2_ + 1)],
                            in_=va_cur[:, 1040 * (2 + b2_):1040 * (3 + b2_)]),
                            wait=[slpe], inc="v"))
                    pl.save("krr_prev", cp_toks)
                    pl.save("vv_prev", cp_toks)

                    for oc in range(8):
                        bk = f"PS{oc % 2}"
                        bank = PS[oc % 2]

                        def mmo(e, oc=oc, bank=bank):
                            prs = [(wo[:, ic * 1024 + oc * 128: ic * 1024 + (oc + 1) * 128],
                                    attn_cm[:, 512 * ic:512 * (ic + 1)]) for ic in range(8)]
                            return em_mm_group(e, bank[:], prs)
                        mt = pl.op("tensor", mmo, wait=oc_toks + wo_toks + [pl.bwait(bk)], inc="pe")
                        stg = xf_st[oc % 2]
                        ad = pl.op("vector", lambda e, oc=oc, bank=bank, stg=stg, xfb=xfb:
                                   e.tensor_add(stg[:], bank[:],
                                                xfb[:, 514 * oc + 1:514 * oc + 513]),
                                   wait=[mt, pl.get(f"xfst_u{oc % 2}")], inc="v")
                        pl.buse(bk, ad)
                        dw = pl.dma("sync", dst_d[oc, :, s0:s0 + 512], stg[:], "d_xfw",
                                    wait=[ad], slots=4)
                        pl.save(f"xfst_u{oc % 2}", dw)
                    wo_pe = pl.tok("pe")
                    for c in range(8):
                        pl.save(f"fattn_u_{c}", wo_pe)
                    pl.save("ftile_prev2", pl.get("ftile_prev"))
                    pl.save("ftile_prev", [pl.tok("pe"), pl.tok("v"), pl.tok("s")])

                sub_done = [pl.tok("pe"), pl.tok("v"), pl.tok("s")] + \
                    pl.chan_toks("d_xfw", 4)
                stA.close()
                prev_done = sub_done
                src_d = dst_d
                dst_d = xf_b if dst_d is xf_a else xf_a
                t_att = tap(f"fattn{li}", (8, 128, S))
                if t_att is not None:
                    pl.dma("sync", t_att, src_d, "d_dbg", wait=prev_done)

                stF = ExitStack()
                sbF = mksb(stF, side="right")
                w1sb = sbF(f"w1sb{li}", [128, 8 * 2816], BF16)
                w1gp = [sbF(f"w1gp{li}_{i}", [128, 8 * 256], BF16) for i in range(2)]
                w1gs = sbF(f"w1gs{li}", [128, 8 * 128], BF16)
                w1gr = sbF(f"w1gr{li}", [128, 8 * 128], BF16)
                w2p = [sbF(f"w2p{li}_{i}", [128, 2 * 1024], BF16) for i in range(3)]
                w2s = sbF(f"w2s{li}", [128, 1024], BF16)
                w2r = sbF(f"w2r{li}", [128, 1024], BF16)
                t_bf = [sbF(f"t_bf{li}_{i}", [128, 512], BF16) for i in range(NJC)]
                e_sb = sbF(f"e_sb{li}", [128, 512])
                xn2_bf = sbF(f"xn2_bf{li}", [128, 8 * 512], BF16)
                w1g_src = din["W1"][li].rearrange("(a p) n -> p a n", p=128)

                w1_toks = []
                for ic in range(8):
                    base = ic * 2816
                    w1_toks.append(pl.op("gpsimd", lambda e, base=base: e.memset(
                        w1sb[:, base + 2730:base + 2816], 0.0), wait=prev_done, inc="g"))
                    w1_toks.append(pl.dma("gpsimd", w1sb[:, base:base + 2730],
                                          din["W1"][li, ic * 128:(ic + 1) * 128, 0:2730],
                                          "d_w", wait=prev_done, slots=6))
                w1gz = pl.op("gpsimd", lambda e: e.memset(w1gr[:], 0.0),
                            wait=prev_done, inc="g")
                b1a_col, b1a_tok = colvec_ragged(f"b1a{li}", din["b1"][li, 0:DI], DI,
                                                 wait=prev_done, sba=sbF)
                b1g_col, b1g_tok = colvec_ragged(f"b1g{li}", din["b1"][li, DI:2 * DI], DI,
                                                 wait=prev_done, sba=sbF)
                b2_col, b2_tok = colvec_ragged(f"b2_{li}", din["b2"][li, :], C, wait=prev_done, sba=sbF)
                grow_toks = [pl.dma("sync", growf[0:1, :], din["ln_g"][li:li + 1, :], "d_lcv",
                                    wait=prev_done),
                             pl.dma("sync", growf[1:2, :], din["ln_b"][li:li + 1, :], "d_lcv",
                                    wait=prev_done)]
                grow_toks = [pl.op("vector", lambda e: e.tensor_copy(
                    out=growf_bf[:], in_=growf[:]), wait=grow_toks + prev_done, inc="v")]

                w2z = pl.op("gpsimd", lambda e: e.memset(w2r[:], 0.0),
                            wait=prev_done, inc="g")

                pl.save("ftile_prev", prev_done)
                pl.save("ftile_prev2", prev_done)
                pl.save("fsq_user_0", None); pl.save("fsq_user_1", None)
                for i3 in range(3):
                    pl.save(f"w2p_u_{i3}", prev_done)
                pl.save("w2s_u", prev_done)
                pl.save("w2r_u", [prev_done, w2z])
                for i2 in range(2):
                    pl.save(f"w1gp_u_{i2}", prev_done)
                pl.save("w1gs_u", prev_done)
                pl.save("w1gr_u", [prev_done, w1gz])

                for tile in range(NT):
                    s0 = tile * 512
                    tprev = pl.get("ftile_prev")
                    tprev2 = pl.get("ftile_prev2")
                    xfb = xf_in[tile % 2]
                    xf_toks = load_xf_tile(xfb, src_d, tile, [tprev2])
                    xs_toks = []
                    for c in range(8):
                        xs_toks.append(pl.op("vector", lambda e, c=c, xfb=xfb: e.tensor_copy(
                            out=xs_bf[:, 512 * c:512 * (c + 1)],
                            in_=xfb[:, sh0(c):sh0(c) + 512]),
                            wait=xf_toks + [pl.get("xsbf_u")], inc="v"))
                    smm = []
                    for c in range(8):
                        smm.append(pl.op("tensor", lambda e, c=c: e.matmul(
                            PS[5][0:1, :], onescol_bf[:], xs_bf[:, 512 * c:512 * (c + 1)],
                            start=(c == 0), stop=(c == 7)),
                            wait=[xs_toks[c]] + ([pl.bwait("PS5")] if c == 0 else []) +
                                 ffn_setup, inc="pe"))
                    s1 = pl.op("scalar", lambda e: e.copy(out=statA[:], in_=PS[5][0:1, :]),
                               wait=[smm[-1], tprev], inc="s")
                    pl.buse("PS5", s1)
                    sq2 = []
                    for c in range(8):
                        par = c % 2
                        sq = pl.op("scalar", lambda e, c=c, par=par: e.activation(
                            out=sqf_bf[:, 512 * par:512 * (par + 1)],
                            in_=xs_bf[:, 512 * c:512 * (c + 1)], func=AF.Square),
                            wait=[xs_toks[c], pl.get(f"fsq_user_{par}")], inc="s")
                        mm = pl.op("tensor", lambda e, c=c, par=par: e.matmul(
                            PS[5][0:1, :], onescol_bf[:],
                            sqf_bf[:, 512 * par:512 * (par + 1)],
                            start=(c == 0), stop=(c == 7)),
                            wait=[sq] + ([s1] if c == 0 else []), inc="pe")
                        pl.save(f"fsq_user_{par}", mm)
                        sq2.append(mm)
                    s2 = pl.op("scalar", lambda e: e.copy(out=statR[:], in_=PS[5][0:1, :]),
                               wait=[sq2[-1], tprev], inc="s")
                    pl.buse("PS5", s2)
                    pl.save("xsbf_u", pl.tok("pe"))
                    v1 = pl.op("vector", lambda e: e.tensor_scalar(
                        out=statA[:], in0=statA[:], scalar1=1.0 / C, scalar2=None,
                        op0=ALU.mult), wait=[s1, s2, tprev], inc="v")
                    v3b = pl.op("vector", lambda e: e.tensor_mul(
                        frt1[0:1, 0:512], statA[:], statA[:]), wait=[v1], inc="v")
                    v4 = pl.op("vector", lambda e: e.scalar_tensor_tensor(
                        out=statR[:], in0=statR[:], scalar=1.0 / C, in1=frt1[0:1, 0:512],
                        op0=ALU.mult, op1=ALU.subtract), wait=[v3b], inc="v")
                    v5 = pl.op("vector", lambda e: e.tensor_scalar(
                        out=statR[:], in0=statR[:], scalar1=1.0, scalar2=1e-5,
                        op0=ALU.mult, op1=ALU.add), wait=[v4], inc="v")
                    v6 = pl.op("vector", lambda e: e.reciprocal(
                        out=statR[:], in_=statR[:]), wait=[v5], inc="v")
                    v7 = pl.op("scalar", lambda e: e.activation(
                        out=statR_bf[:], in_=statR[:], func=AF.Sqrt),
                        wait=[v6, tprev], inc="s")
                    v8 = pl.op("vector", lambda e: e.tensor_mul(
                        statB2[0:1, :], statA[:], statR_bf[:]), wait=[v7, tprev], inc="v")
                    xn_toks2 = []
                    for c in range(8):
                        mA = pl.op("tensor", lambda e, c=c: e.matmul(
                            PS[2][:], growf_bf[0:1, 128 * c:128 * (c + 1)], statR_bf[:],
                            start=True, stop=True),
                            wait=[v7, pl.bwait("PS2")] + grow_toks, inc="pe")
                        mB = pl.op("tensor", lambda e, c=c: e.matmul(
                            PS[3][:], growf_bf[:, 128 * c:128 * (c + 1)], statB2[:],
                            start=True, stop=True),
                            wait=[v8, pl.bwait("PS3")] + grow_toks + ffn_setup, inc="pe")
                        t1_ = pl.op("vector", lambda e, c=c, xfb=xfb: e.tensor_mul(
                            frt2[:], xfb[:, sh0(c):sh0(c) + 512], PS[2][:]),
                            wait=[mA, pl.get("frt2_u")] + xf_toks, inc="v")
                        pl.buse("PS2", t1_)
                        t2_ = pl.op("vector", lambda e, c=c: e.tensor_sub(
                            xn2_bf[:, 512 * c:512 * (c + 1)], frt2[:], PS[3][:]),
                            wait=[t1_, mB, tprev], inc="v")
                        pl.buse("PS3", t2_)
                        pl.save("frt2_u", t2_)
                        xn_toks2.append(t2_)

                    t_toks = []
                    w1pieces = [("p", pi) for pi in range(10)] + [("s", 0), ("r", 0)]
                    for pc_ in w1pieces:
                        if pc_[0] == "p":
                            pi = pc_[1]
                            gbuf = w1gp[pi % 2]
                            gkey = f"w1gp_u_{pi % 2}"
                            cw = 256
                            gl = pl.dma("gpsimd",
                                        gbuf[:].rearrange("p (a c) -> p a c", c=256),
                                        w1g_src[:, :, DI + 256 * pi: DI + 256 * (pi + 1)],
                                        "d_w2", wait=[pl.get(gkey)], slots=5)
                            chunks = [(2 * pi, 0), (2 * pi + 1, 128)]
                        elif pc_[0] == "s":
                            gbuf = w1gs
                            gkey = "w1gs_u"
                            cw = 128
                            gl = pl.dma("gpsimd",
                                        gbuf[:].rearrange("p (a c) -> p a c", c=128),
                                        w1g_src[:, :, DI + 2560: DI + 2688],
                                        "d_w2", wait=[pl.get(gkey)], slots=5)
                            chunks = [(20, 0)]
                        else:
                            gbuf = w1gr
                            gkey = "w1gr_u"
                            cw = 128
                            gl = pl.dma("gpsimd",
                                        gbuf[:].rearrange("p (a c) -> p a c", c=128)[:, :, 0:42],
                                        w1g_src[:, :, DI + 2688: DI + 2730],
                                        "d_w2", wait=[pl.get(gkey)], slots=5)
                            chunks = [(21, 0)]
                        glast = []
                        for (jc, goff) in chunks:
                            bkA = f"PS{jc % 2}"
                            bankA = PS[jc % 2]

                            def mma(e, jc=jc, bankA=bankA):
                                prs = [(w1sb[:, ic * 2816 + 128 * jc: ic * 2816 + 128 * (jc + 1)],
                                        xn2_bf[:, 512 * ic:512 * (ic + 1)]) for ic in range(8)]
                                return em_mm_group(e, bankA[:], prs)
                            ma = pl.op("tensor", mma, wait=xn_toks2 + w1_toks + [pl.bwait(bkA)],
                                       inc="pe")
                            bkB = f"PS{2 + jc % 2}"
                            bankB = PS[2 + jc % 2]

                            def mmg(e, gbuf=gbuf, goff=goff, cw=cw, bankB=bankB):
                                prs = [(gbuf[:, cw * ic + goff: cw * ic + goff + 128],
                                        xn2_bf[:, 512 * ic:512 * (ic + 1)]) for ic in range(8)]
                                return em_mm_group(e, bankB[:], prs)
                            mg = pl.op("tensor", mmg, wait=xn_toks2 + [gl, pl.bwait(bkB)],
                                       inc="pe")
                            glast.append(mg)
                            ge = pl.op("scalar", lambda e, jc=jc, bankB=bankB: e.activation(
                                out=e_sb[:], in_=bankB[:], func=AF.Gelu,
                                bias=b1g_col[:, jc:jc + 1]),
                                wait=[mg, b1g_tok, pl.get("fe_u")], inc="s")
                            pl.buse(bkB, ge)
                            tb_ = pl.op("vector", lambda e, jc=jc, bankA=bankA:
                                        e.scalar_tensor_tensor(
                                            out=t_bf[jc][:], in0=bankA[:],
                                            scalar=b1a_col[:, jc:jc + 1],
                                            in1=e_sb[:], op0=ALU.add, op1=ALU.mult),
                                        wait=[ma, ge, b1a_tok, tprev], inc="v")
                            pl.buse(bkA, tb_)
                            pl.save("fe_u", tb_)
                            t_toks.append(tb_)
                        pl.save(gkey, glast)

                    accbanks = [("PS4", PS[4]), ("PS5", PS[5]), ("PS2", PS[2]), ("PS3", PS[3])]
                    w2pieces = [("p", pi) for pi in range(10)] + [("s", 0), ("r", 0)]
                    for grp in range(2):
                        grp_mms = {oo: [] for oo in range(4)}
                        for pc_ in w2pieces:
                            if pc_[0] == "p":
                                pi = pc_[1]
                                wt = w2p[pi % 3]
                                wkey = f"w2p_u_{pi % 3}"
                                wl = pl.dma("gpsimd",
                                            wt[:].rearrange("p (a n) -> p a n", n=1024),
                                            din["W2"][li, 256 * pi:256 * (pi + 1), :]
                                            .rearrange("(a p) n -> p a n", p=128),
                                            "d_w2", wait=[pl.get(wkey)], slots=5)
                                chunks = [(2 * pi, 0), (2 * pi + 1, 1024)]
                            elif pc_[0] == "s":
                                wt = w2s
                                wkey = "w2s_u"
                                if grp == 0:
                                    wl = pl.dma("gpsimd", wt[:],
                                                din["W2"][li, 2560:2688, :], "d_w2",
                                                wait=[pl.get(wkey)], slots=5)
                                else:
                                    wl = pl.get(wkey)
                                chunks = [(20, 0)]
                            else:
                                wt = w2r
                                wkey = "w2r_u"
                                if grp == 0:
                                    wl = pl.dma("gpsimd", wt[0:42, :],
                                                din["W2"][li, 2688:2730, :], "d_w2",
                                                wait=[pl.get(wkey)], slots=5)
                                else:
                                    wl = pl.get(wkey)
                                chunks = [(21, 0)]
                            mlast = []
                            for (jc, woff) in chunks:
                                for oo in range(4):
                                    oc = 4 * grp + oo
                                    bkn, bb = accbanks[oo]
                                    mm = pl.op("tensor",
                                               lambda e, jc=jc, oc=oc, wt=wt, woff=woff,
                                               bb=bb: e.matmul(
                                                   bb[:],
                                                   wt[:, woff + 128 * oc:woff + 128 * (oc + 1)],
                                                   t_bf[jc][:],
                                                   start=(jc == 0), stop=(jc == NJC - 1)),
                                               wait=[wl, t_toks[jc]] +
                                                    ([pl.bwait(bkn)] if jc == 0 else []),
                                               inc="pe")
                                    grp_mms[oo].append(mm)
                                    mlast.append(mm)
                            pl.save(wkey, mlast)
                        for oo in range(4):
                            oc = 4 * grp + oo
                            bkn, bb = accbanks[oo]
                            stg = xf_st[oo % 2]
                            ad = pl.op("vector", lambda e, oc=oc, bb=bb, stg=stg, xfb=xfb:
                                       e.scalar_tensor_tensor(
                                           out=stg[:], in0=bb[:], scalar=b2_col[:, oc:oc + 1],
                                           in1=xfb[:, 514 * oc + 1:514 * oc + 513],
                                           op0=ALU.add, op1=ALU.add),
                                       wait=[grp_mms[oo][-1], b2_tok,
                                             pl.get(f"fxfst_u{oo % 2}")], inc="v")
                            pl.buse(bkn, ad)
                            dw = pl.dma("sync", dst_d[oc, :, s0:s0 + 512], stg[:], "d_xfw",
                                        wait=[ad], slots=4)
                            pl.save(f"fxfst_u{oo % 2}", dw)
                    pl.save(f"f2xf_user_{tile}", xs_toks[-1])
                    pl.save("ftile_prev2", pl.get("ftile_prev"))
                    pl.save("ftile_prev", [pl.tok("pe"), pl.tok("v"), pl.tok("s")])

                sub_done = [pl.tok("pe"), pl.tok("v"), pl.tok("s")] + pl.chan_toks("d_xfw", 4)
                stF.close()
                prev_done = sub_done
                src_d = dst_d
                dst_d = xf_b if dst_d is xf_a else xf_a
                t_ffn = tap(f"fffn{li}", (8, 128, S))
                if t_ffn is not None:
                    pl.dma("sync", t_ffn, src_d, "d_dbg", wait=prev_done)

            fin_acc = sbf("fin_acc", [128, C])
            fin_cm = sbf("fin_cm", [128, C])
            fin_out = sbf("fin_out", [128, C])
            for tt in range(S // 128):
                la = pl.dma("sync", fin_acc[:], acc_d[tt * 128:(tt + 1) * 128, :], "d_fx",
                            wait=prev_done + [pl.get("finacc_u")])
                lc = [pl.dma("sync", fin_cm[:, c * 128:(c + 1) * 128],
                             src_d[c, :, tt * 128:(tt + 1) * 128], "d_fx2",
                             wait=prev_done + [pl.get("fincm_u")], slots=4) for c in range(8)]
                adds = []
                for c in range(8):
                    tr = pl.op("tensor", lambda e, c=c: e.transpose(
                        out=PS[4][:, 0:128], in_=fin_cm[:, c * 128:(c + 1) * 128],
                        identity=CS["eye"][:]),
                        wait=lc + [pl.bwait("PS4")] + prev_done, inc="pe")
                    ad = pl.op("vector", lambda e, c=c: e.tensor_add(
                        fin_out[:, c * 128:(c + 1) * 128], PS[4][:, 0:128],
                        fin_acc[:, c * 128:(c + 1) * 128]),
                        wait=[tr, la, pl.get("finout_u")], inc="v")
                    pl.buse("PS4", ad)
                    adds.append(ad)
                pl.save("finacc_u", adds[-1])
                pl.save("fincm_u", adds[-1])
                dy = pl.dma("sync", y_h[tt * 128:(tt + 1) * 128, :], fin_out[:], "d_y",
                            wait=adds, slots=4)
                pl.save("finout_u", dy)

        pl.final_drain()
        block = st.enter_context(nc.Block())
        pl.emit(block)
    return nc, dbg_out


def shard_inputs(inputs, core):
    HC = host_constants()
    HT = host_tables()
    m = {k: np.ascontiguousarray(v) for k, v in {
        "x": inputs["L0_attn_x"][core],
        "conv_k": inputs["L0_attn_conv_k"], "conv_b": inputs["L0_attn_conv_b"],
        "rff_w": inputs["L0_attn_rff_w"], "rff_b": inputs["L0_attn_rff_b"],
        "proj_w": inputs["L0_attn_proj_w"], "proj_b": inputs["L0_attn_proj_b"],
        "omega": inputs["L0_attn_omega"], "landmarks": inputs["L0_attn_landmarks"],
        "out_w": inputs["L0_attn_out_w"], "out_b": inputs["L0_attn_out_b"],
        "Wqkv": inputs["L0_ffn_Wqkv"], "q_scale": inputs["L0_ffn_q_scale"],
        "k_scale": inputs["L0_ffn_k_scale"], "Wo": inputs["L0_ffn_Wo"],
        "ln_g": inputs["L0_ffn_ln_g"], "ln_b": inputs["L0_ffn_ln_b"],
        "W1": inputs["L0_ffn_W1"], "b1": inputs["L0_ffn_b1"],
        "W2": inputs["L0_ffn_W2"], "b2": inputs["L0_ffn_b2"],
    }.items()}
    for k, v in HC.items():
        m["c_" + k] = v
    import ml_dtypes
    for k, v in HT.items():
        m["t_" + k] = v.astype(ml_dtypes.bfloat16)
    return m


def kernel(**inputs):
    nc, _ = build_program()
    in_maps = [shard_inputs(inputs, core) for core in range(8)]
    res = run_bass_kernel_spmd(nc, in_maps, core_ids=list(range(8)))
    return np.stack([np.asarray(r["y"]) for r in res.results], axis=0)


# version 12 (final)
# speedup vs baseline: 1.0897x; 1.0021x over previous
import numpy as np
import concourse.bass as bass
import concourse.mybir as mybir
from concourse.bass_utils import run_bass_kernel_spmd
from contextlib import ExitStack

F32 = mybir.dt.float32
BF16 = mybir.dt.bfloat16
I32 = mybir.dt.int32
AF = mybir.ActivationFunctionType
ALU = mybir.AluOpType
AX = mybir.AxisListType.X

S = 4096
C = 1024
NH = 16
DH = 64
NT = S // 512

NRF = 256
MS = 256
GAMMAS = (0.5, 1.0, 2.0)
NLM = 64

DEPTH = 6
WINDOW = 256
DI = 2730
DIP = 2816
NJC = DIP // 128
QK_SCALE = 8.0
NEG = -1.0e30
LN16 = float(np.log(16.0))
TWO_PI = float(2.0 * np.pi)
SQ2M = float(np.sqrt(2.0 / MS))


def host_constants():
    cs = {}
    cs["eye"] = np.eye(128, dtype=np.float32)

    shuf = np.zeros((128, 128), np.float32)
    for blk in range(2):
        for m in range(64):
            shuf[blk * 64 + (m + 32) % 64, blk * 64 + m] = 1.0
    cs["shuf"] = shuf

    bd = np.zeros((128, 128), np.float32)
    for c in range(8):
        for p in range(128):
            bd[p, 16 * c + 2 * c + p // 64] = 1.0
    cs["bdh16"] = bd

    s2 = np.zeros((16, 8 * 128), np.float32)
    for c in range(8):
        for p in range(128):
            s2[2 * c + p // 64, 128 * c + p] = 1.0
    cs["sel2h"] = s2

    s16 = np.zeros((16, 16 * 128), np.float32)
    for h in range(NH):
        s16[h, 128 * h:128 * (h + 1)] = 1.0
    cs["sel16b"] = s16

    r = np.arange(128)
    j = np.arange(384)[None, :]
    d = r[:, None]
    cs["mq_full"] = np.where((j >= d) & (j <= d + 256), 0.0, NEG).astype(np.float32)
    j2 = np.arange(256)[None, :]
    cs["mq_s1"] = np.where((j2 >= 128 + d - 256) & (j2 <= 128 + d), 0.0, NEG).astype(np.float32)
    j1 = np.arange(128)[None, :]
    cs["mq_s0"] = np.where(j1 <= d, 0.0, NEG).astype(np.float32)

    cs["ones1x"] = np.ones((1, 128), np.float32)
    return cs


def host_tables():
    tb = {}
    inv = 1.0 / 10000.0 ** (np.arange(0, DH, 2, dtype=np.float64) / DH)
    ang = np.arange(S, dtype=np.float64)[None, :] * inv[:, None]
    cos64 = np.cos(np.concatenate([ang, ang], 0))
    sin64 = np.sin(np.concatenate([ang, ang], 0))
    sgn = np.concatenate([-np.ones(32), np.ones(32)])[:, None]
    tb["pr_cos"] = np.tile(cos64, (2, 1)).astype(np.float32)
    tb["pr_sin"] = np.tile(sin64 * sgn, (2, 1)).astype(np.float32)

    half = np.arange(0, DH, 2, dtype=np.float64)
    invf = 1.0 / 10000.0 ** (half / DH)
    t = np.arange(S, dtype=np.float64)
    fr = t[None, :] * invf[:, None]
    fr64 = np.concatenate([fr, fr], 0)
    base = ((half + 0.4 * DH) / (1.4 * DH))[:, None]
    power = ((t - S // 2) / (WINDOW // 2))[None, :]
    scale = np.concatenate([base ** power, base ** power], 0)
    cosf = np.cos(fr64); sinf = np.sin(fr64)
    tb["fx_cq"] = np.tile(cosf * scale, (2, 1)).astype(np.float32)
    tb["fx_sq"] = np.tile(sinf * scale * sgn, (2, 1)).astype(np.float32)
    tb["fx_ck"] = np.tile(cosf / scale, (2, 1)).astype(np.float32)
    tb["fx_sk"] = np.tile(sinf / scale * sgn, (2, 1)).astype(np.float32)
    return tb


CAST_CONSTS = {"shuf", "bdh16", "ones1x", "sel16b", "sel2h",
               "mq_full", "mq_s1", "mq_s0"}


class Pl:
    ENGINES = ("sync", "gpsimd", "vector", "scalar", "tensor")

    ENGINE_SEMS = {"v": "vector", "s": "scalar", "pe": "tensor", "g": "gpsimd"}

    def __init__(self, nc, st):
        self.nc = nc
        self.st = st
        self.stream = {e: [] for e in self.ENGINES}
        self.sems = {}
        self.counts = {}
        self.saved = {}
        self.chan_k = {}
        self.waited = {e: {} for e in self.ENGINES}

    def sem(self, name):
        if name not in self.sems:
            self.sems[name] = self.st.enter_context(self.nc.semaphore(name))
            self.counts[name] = 0
        return name

    def op(self, eng, fn, wait=(), inc=None, inc_by=1):
        flat = []

        def _fl(t):
            if t is None:
                return
            if isinstance(t, (list, tuple)) and (len(t) == 0 or not isinstance(t[0], str)):
                for x in t:
                    _fl(x)
            else:
                flat.append(t)
        for t in wait:
            _fl(t)
        best = {}
        for s, v in flat:
            if v > best.get(s, 0):
                best[s] = v
        wed = self.waited[eng]
        out = []
        for s, v in best.items():
            if eng == "tensor" and s == "pe":
                continue
            if wed.get(s, 0) >= v:
                continue
            wed[s] = v
            out.append((s, v))
        wait = tuple(out)
        for s, _ in wait:
            self.sem(s)
        sems = self.sems
        tok = None
        if inc is not None:
            self.sem(inc)
            self.counts[inc] += inc_by
            tok = (inc, self.counts[inc])

        def run(e, fn=fn, wait=wait, tok=tok, inc_by=inc_by):
            for s, v in wait:
                if v > 0:
                    e.wait_ge(sems[s], v)
            ins = fn(e)
            if tok is not None and ins is not None:
                ins.then_inc(sems[tok[0]], inc_by)
        self.stream[eng].append(run)
        return tok

    def dma(self, eng, out, in_, chan, wait=(), slots=4, **kw):
        k = self.chan_k.setdefault(chan, 0)
        self.chan_k[chan] = k + 1
        sem = f"{chan}.{k % slots}"
        self.sem(sem)
        prior = self.counts[sem]
        w = list(wait) + ([(sem, prior)] if prior > 0 else [])
        return self.op(eng, lambda e: e.dma_start(out=out, in_=in_, **kw), w, sem, 16)

    def chan_toks(self, chan, slots=4):
        return [self.tok(f"{chan}.{i}") for i in range(slots)
                if f"{chan}.{i}" in self.counts]

    def tok(self, sem):
        self.sem(sem)
        return (sem, self.counts[sem])

    def save(self, key, tok):
        self.saved[key] = tok

    def get(self, key):
        return self.saved.get(key)

    def buse(self, key, tok):
        self.saved["bk_" + key] = tok

    def bwait(self, key):
        return self.saved.get("bk_" + key)

    def final_drain(self):
        toks = [(s, c) for s, c in self.counts.items() if c > 0]
        sems = self.sems

        def run(e, toks=toks):
            for s, v in toks:
                e.wait_ge(sems[s], v)
        self.stream["sync"].append(run)

    def emit(self, block):
        for eng in self.ENGINES:
            ops = self.stream[eng]
            if not ops:
                continue
            dec = getattr(block, eng)

            def section(e, ops=ops):
                for f in ops:
                    f(e)
            section.__name__ = eng
            dec(section)


def em_mm_group(e, bank_ap, pairs):
    n = len(pairs)
    last = None
    for q, (l, r) in enumerate(pairs):
        last = e.matmul(bank_ap, l, r, start=(q == 0), stop=(q == n - 1))
    return last


def build_program(dbg=()):
    nc = bass.Bass("TRN2", target_bir_lowering=False)
    din = {}

    def inp(name, shape):
        din[name] = nc.dram_tensor(name, list(shape), F32, kind="ExternalInput").ap()
        return din[name]

    x_h = inp("x", (S, C))
    inp("conv_k", (3, 3, 256, C)); inp("conv_b", (3, C))
    inp("rff_w", (3, 3, C, MS)); inp("rff_b", (3, 3, MS))
    inp("proj_w", (3, 3 * MS, C)); inp("proj_b", (3, C))
    inp("omega", (DH, NRF)); inp("landmarks", (NLM, DH))
    inp("out_w", (C, C)); inp("out_b", (C,))
    inp("Wqkv", (DEPTH, C, 3 * C)); inp("q_scale", (DEPTH, DH)); inp("k_scale", (DEPTH, DH))
    inp("Wo", (DEPTH, C, C)); inp("ln_g", (DEPTH, C)); inp("ln_b", (DEPTH, C))
    inp("W1", (DEPTH, C, 2 * DI)); inp("b1", (DEPTH, 2 * DI))
    inp("W2", (DEPTH, DI, C)); inp("b2", (DEPTH, C))
    HC = host_constants()
    for k, v in HC.items():
        inp("c_" + k, v.shape)
    HT = host_tables()
    for k, v in HT.items():
        din["t_" + k] = nc.dram_tensor("t_" + k, list(v.shape), BF16,
                                       kind="ExternalInput").ap()

    y_h = nc.dram_tensor("y", [S, C], F32, kind="ExternalOutput").ap()

    dbg_out = {}

    def tap(name, shape):
        if name in dbg:
            dbg_out[name] = nc.dram_tensor("dbg_" + name, list(shape), F32,
                                           kind="ExternalOutput").ap()
            return dbg_out[name]
        return None

    xn_cm_d = nc.dram_tensor("xn_cm_d", [8, 128, S], F32, kind="Internal").ap()
    xf_a = nc.dram_tensor("xf_a", [8, 128, S], F32, kind="Internal").ap()
    xf_b = nc.dram_tensor("xf_b", [8, 128, S], F32, kind="Internal").ap()
    acc_d = nc.dram_tensor("acc_d", [S, C], F32, kind="Internal").ap()
    bounce = nc.dram_tensor("bounce", [4, 16384], F32, kind="Internal").ap()
    w2bf_d = nc.dram_tensor("w2bf_d", [DEPTH, DI, C], BF16, kind="Internal").ap()
    w1gbf_d = nc.dram_tensor("w1gbf_d", [DEPTH, C, DI], BF16, kind="Internal").ap()

    with ExitStack() as st:
        pl = Pl(nc, st)

        def mksb(stk, side=None):
            def sb(name, shape, dt=F32):
                kw = {"side": side} if side else {}
                return stk.enter_context(nc.sbuf_tensor(name, shape, dt, **kw))
            return sb
        sb = mksb(st)

        def mkpsum(name, shape, dt=F32):
            return st.enter_context(nc.psum_tensor(name, shape, dt))

        CS = {}
        for k, v in HC.items():
            CS[k] = sb("cs_" + k, list(v.shape), BF16 if k in CAST_CONSTS else F32)
        const_pool = sb("const_pool", [128, 8], F32)
        for i, val in enumerate([-LN16, 1e-6, 1e-12, 0.5, 2.0, -1.0]):
            nc.const_aps.register(F32, val, const_pool[:, i:i + 1])
            pl.op("gpsimd", lambda e, i=i, val=val: e.memset(const_pool[:, i:i + 1], val), inc="g")
        for k in HC:
            if k in CAST_CONSTS:
                pl.dma("gpsimd", CS[k][:], din["c_" + k], "d_cg")
            else:
                pl.dma("sync", CS[k][:], din["c_" + k], "d_cs")
        consts_tok = pl.chan_toks("d_cg") + pl.chan_toks("d_cs") + [pl.tok("g")]

        PS = [mkpsum(f"PS{i}", [128, 512]) for i in range(6)]
        PK = [mkpsum(f"PK{i}", [128, 512]) for i in range(2)]

        with ExitStack() as st0:
            sb0 = mksb(st0)
            p0_x = [sb0(f"p0_x{i}", [128, C]) for i in range(3)]
            p0_stat = [sb0(f"p0_st{i}", [128, 6]) for i in range(3)]
            p0_sq = sb0("p0_sq", [128, C], BF16)
            p0_xn = [sb0(f"p0_xn{i}", [128, C]) for i in range(2)]
            p0_cm = [sb0(f"p0_cm{i}", [128, 128]) for i in range(4)]

            n_tok = S // 128
            for tt in range(n_tok):
                bx = p0_x[tt % 3]
                stt_ = p0_stat[tt % 3]
                ld = pl.dma("sync", bx[:], x_h[tt * 128:(tt + 1) * 128, :], "d_x",
                            wait=[pl.get(f"p0x_done_{tt - 3}")])
                t1 = pl.op("vector", lambda e, bx=bx, stt_=stt_: e.reduce_sum(
                    out=stt_[:, 0:1], in_=bx[:], axis=AX),
                    wait=[ld, pl.get(f"p0st_done_{tt - 3}")], inc="v")
                t2 = pl.op("scalar", lambda e, bx=bx, stt_=stt_: e.activation(
                    out=p0_sq[:], in_=bx[:], func=AF.Square, accum_out=stt_[:, 1:2]),
                    wait=[ld, t1], inc="s")
                t3 = pl.op("vector", lambda e, stt_=stt_: e.tensor_scalar(
                    out=stt_[:, 2:3], in0=stt_[:, 0:1], scalar1=1.0 / C, scalar2=None,
                    op0=ALU.mult), wait=[t1], inc="v")
                t4 = pl.op("vector", lambda e, stt_=stt_: e.tensor_scalar(
                    out=stt_[:, 3:4], in0=stt_[:, 2:3], scalar1=stt_[:, 2:3], scalar2=None,
                    op0=ALU.mult), wait=[t3], inc="v")
                t5 = pl.op("vector", lambda e, stt_=stt_: e.scalar_tensor_tensor(
                    out=stt_[:, 3:4], in0=stt_[:, 1:2], scalar=1.0 / C, in1=stt_[:, 3:4],
                    op0=ALU.mult, op1=ALU.subtract), wait=[t2, t4], inc="v")
                t6 = pl.op("vector", lambda e, stt_=stt_: e.tensor_scalar(
                    out=stt_[:, 3:4], in0=stt_[:, 3:4], scalar1=1.0, scalar2=1e-5,
                    op0=ALU.mult, op1=ALU.add), wait=[t5], inc="v")
                t7 = pl.op("vector", lambda e, stt_=stt_: e.reciprocal(
                    out=stt_[:, 4:5], in_=stt_[:, 3:4]), wait=[t6], inc="v")
                t8 = pl.op("scalar", lambda e, stt_=stt_: e.activation(
                    out=stt_[:, 4:5], in_=stt_[:, 4:5], func=AF.Sqrt), wait=[t7], inc="s")
                t9 = pl.op("vector", lambda e, stt_=stt_: e.tensor_scalar(
                    out=stt_[:, 5:6], in0=stt_[:, 2:3], scalar1=stt_[:, 4:5], scalar2=-1.0,
                    op0=ALU.mult, op1=ALU.mult), wait=[t8], inc="v")
                bxn = p0_xn[tt % 2]
                t10 = pl.op("vector", lambda e, bx=bx, stt_=stt_, bxn=bxn: e.tensor_scalar(
                    out=bxn[:], in0=bx[:], scalar1=stt_[:, 4:5], scalar2=stt_[:, 5:6],
                    op0=ALU.mult, op1=ALU.add),
                    wait=[t9, pl.get(f"p0xn_done_{tt - 2}")], inc="v")
                pl.save(f"p0x_done_{tt}", t10)
                pl.save(f"p0st_done_{tt}", t10)
                trs = []
                for c in range(8):
                    bk = f"PS{c % 4}"
                    bank = PS[c % 4]
                    tr = pl.op("tensor", lambda e, bxn=bxn, c=c, bank=bank: e.transpose(
                        out=bank[:, 0:128], in_=bxn[:, c * 128:(c + 1) * 128],
                        identity=CS["eye"][:]),
                        wait=[t10, pl.bwait(bk)] + consts_tok, inc="pe")
                    stg = p0_cm[c % 4]
                    e2 = pl.op("scalar", lambda e, bank=bank, stg=stg: e.copy(
                        out=stg[:], in_=bank[:, 0:128]),
                        wait=[tr, pl.get(f"p0cmd_{tt}_{c - 4}"), pl.get(f"p0cmd_{tt - 1}_{c + 4}")],
                        inc="s")
                    pl.buse(bk, e2)
                    d2 = pl.dma("sync", xn_cm_d[c, :, tt * 128:(tt + 1) * 128], stg[:], "d_xn",
                                wait=[e2], slots=8)
                    pl.save(f"p0cmd_{tt}_{c}", d2)
                    trs.append(tr)
                pl.save(f"p0xn_done_{tt}", trs[-1])
            p0_done = [pl.tok("v"), pl.tok("s"), pl.tok("pe")] + pl.chan_toks("d_xn", 8)

        t_xn = tap("xn_cm", (8, 128, S))
        if t_xn is not None:
            pl.dma("sync", t_xn, xn_cm_d, "d_dbg", wait=p0_done)

        with ExitStack() as stp:
            sbp = mksb(stp)

            slotA = {"cw": sbp("cwA", [128, 6 * 1024], BF16),
                     "rw": sbp("rwA", [128, 8 * 768], BF16),
                     "pw": sbp("pwA", [128, 6 * 1024], BF16)}
            st1 = ExitStack()
            sb1 = mksb(st1, side="right")
            slotB = {"cw": sb1("cwB", [128, 6 * 1024], BF16),
                     "rw": sb1("rwB", [128, 8 * 768], BF16),
                     "pw": sb1("pwB", [128, 6 * 1024], BF16)}

            def load_proj_weights(slot, i, wait):
                tks = []
                for t3 in range(3):
                    for u in range(2):
                        tks.append(pl.dma("gpsimd", slot["cw"][:, (t3 * 2 + u) * 1024:(t3 * 2 + u + 1) * 1024],
                                          din["conv_k"][i, t3, u * 128:(u + 1) * 128, :], "d_w",
                                          wait=wait, slots=6))
                for kc in range(8):
                    for g in range(3):
                        tks.append(pl.dma("gpsimd", slot["rw"][:, kc * 768 + g * 256: kc * 768 + (g + 1) * 256],
                                          din["rff_w"][i, g, kc * 128:(kc + 1) * 128, :], "d_w",
                                          wait=wait, slots=6))
                pwt = []
                for fc in range(6):
                    pwt.append(pl.dma("gpsimd", slot["pw"][:, fc * 1024:(fc + 1) * 1024],
                                      din["proj_w"][i, fc * 128:(fc + 1) * 128, :], "d_w",
                                      wait=wait, slots=6))
                tks += pwt
                tks.append(pl.op("vector", lambda e, slot=slot: e.tensor_scalar(
                    out=slot["pw"][:], in0=slot["pw"][:], scalar1=SQ2M, scalar2=None,
                    op0=ALU.mult), wait=pwt, inc="v"))
                return tks

            wA_tok = load_proj_weights(slotA, 1, p0_done)
            wB_tok = load_proj_weights(slotB, 2, p0_done)

            ombf = sbp("ombf", [128, NRF], BF16)
            om_tok1 = pl.dma("gpsimd", ombf[0:64, :], din["omega"], "d_w", wait=p0_done)
            om_tok2 = pl.dma("gpsimd", ombf[64:128, :], din["omega"], "d_w", wait=p0_done)
            om_tok = [om_tok1, om_tok2]

            def load_colvec(tag, src2d, ncol, wait=()):
                stage = sbp(f"lcv_s_{tag}", [ncol, 128])
                out = sbp(f"lcv_{tag}", [128, ncol])
                d = pl.dma("sync", stage[:], src2d, "d_lcv", wait=list(wait) + p0_done)
                tr = pl.op("tensor", lambda e: e.transpose(
                    out=PS[5][:, 0:ncol], in_=stage[:], identity=CS["eye"][0:ncol, 0:ncol]),
                    wait=[d, pl.bwait("PS5")] + consts_tok, inc="pe")
                ev = pl.op("vector", lambda e: e.tensor_copy(out=out[:], in_=PS[5][:, 0:ncol]),
                           wait=[tr], inc="v")
                pl.buse("PS5", ev)
                return out, ev

            cb_col = {}; pb_col = {}; rb_col = {}
            for i in (0, 1, 2):
                cb_col[i] = load_colvec(f"cb{i}", din["conv_b"][i, :].rearrange("(a p) -> a p", p=128), 8)
                pb_col[i] = load_colvec(f"pb{i}", din["proj_b"][i, :].rearrange("(a p) -> a p", p=128), 8)
                col, tokc = load_colvec(f"rb{i}", din["rff_b"][i].rearrange("g (a p) -> (g a) p", p=128), 6)
                tr = pl.op("vector", lambda e, col=col: e.tensor_scalar(
                    out=col[:], in0=col[:], scalar1=1.0 / TWO_PI, scalar2=0.25,
                    op0=ALU.mult, op1=ALU.add), wait=[tokc], inc="v")
                rb_col[i] = (col, tr)
            outb_col = load_colvec("outb", din["out_b"].rearrange("(a p) -> a p", p=128), 8)

            lam_tok_sb = sbp("lam_tok", [NLM, DH])
            lam2 = sbp("lam2", [128, 128], BF16)
            lamn = sbp("lamn", [128, 1])
            lsq = sbp("lsq", [NLM, DH], BF16)
            lam_d = pl.dma("sync", lam_tok_sb[:], din["landmarks"], "d_lcv", wait=p0_done)
            g0 = pl.op("gpsimd", lambda e: e.memset(lam2[:], 0.0), wait=p0_done, inc="g")
            lt = pl.op("tensor", lambda e: e.transpose(
                out=PS[5][0:64, 0:64], in_=lam_tok_sb[:], identity=CS["eye"][0:64, 0:64]),
                wait=[lam_d, pl.bwait("PS5")] + consts_tok, inc="pe")
            l1 = pl.op("vector", lambda e: e.tensor_scalar(
                out=lam2[0:64, 0:64], in0=PS[5][0:64, 0:64], scalar1=2.0 / 64.0, scalar2=None,
                op0=ALU.mult), wait=[lt, g0], inc="v")
            l2 = pl.op("vector", lambda e: e.tensor_scalar(
                out=lam2[64:128, 64:128], in0=PS[5][0:64, 0:64], scalar1=2.0 / 64.0, scalar2=None,
                op0=ALU.mult), wait=[l1], inc="v")
            pl.buse("PS5", l2)
            l3 = pl.op("scalar", lambda e: e.activation(out=lsq[:], in_=lam_tok_sb[:], func=AF.Square),
                       wait=[lam_d], inc="s")
            l4 = pl.op("vector", lambda e: e.reduce_sum(out=lamn[0:64, 0:1], in_=lsq[:], axis=AX),
                       wait=[l3], inc="v")
            l5 = pl.op("vector", lambda e: e.tensor_scalar(
                out=lamn[0:64, 0:1], in0=lamn[0:64, 0:1], scalar1=-1.0 / 64.0, scalar2=None,
                op0=ALU.mult), wait=[l4], inc="v")
            l6 = pl.op("vector", lambda e: e.tensor_copy(out=lamn[64:128, 0:1], in_=lamn[0:64, 0:1]),
                       wait=[l5], inc="v")
            lam_ready = [l2, l6]

            xn_t = [sbp(f"xn_t{i}", [128, 8 * 514], BF16) for i in range(2)]
            h_bf = sbp("h_bf", [128, 4096], BF16)
            f_bf = sbp("f_bf", [128, 3072], BF16)
            proj_sb = sbp("proj_sb", [128, 4096], BF16)
            r_bf = sbp("r_bf", [128, 4096], BF16)
            sq_bf = sbp("sq_bf", [128, 2 * 512], BF16)
            rr16 = sbp("rr16", [16, 512], BF16)
            n16 = sbp("n16", [16, 512])
            n16q = sbp("n16q", [16, 512], BF16)
            kncols = sbp("kncols", [128, 64])
            kp_bf = sbp("kp_bf", [128, 512], BF16)
            qp_bf = sbp("qp_bf", [128, 2 * 1024], BF16)
            kvT_bf = sbp("kvT_bf", [128, 16 * 130], BF16)
            zt_sb = sbp("zt_sb", [128, 2 * 512])
            zr_sb = sbp("zr_sb", [128, 2 * 512])
            zi_sb = sbp("zi_sb", [128, 2 * 512], I32)
            rt1 = sbp("rt1", [128, 2 * 512], BF16)
            rt2 = sbp("rt2", [128, 2 * 512], BF16)
            tabc = [sbp("tabc0", [128, 512], BF16)]
            tabs = [sbp("tabs0", [128, 512], BF16)]
            pbrow = sbp("pbrow", [1, 1024], BF16)
            pbrow_tok = pl.dma("gpsimd", pbrow[:], din["proj_b"][2:3, :], "d_w", wait=p0_done)

            v_aug = sb1("v_aug", [128, 4160], BF16)
            kv_acc = sb1("kv_acc", [65, 16 * 256])

            va_ones = pl.op("gpsimd", lambda e: e.memset(
                v_aug[:].rearrange("p (h d) -> p h d", d=65)[:, :, 64:65], 1.0), inc="g")

            def emit_precast(n):
                pass

            def load_xn_tile(tile):
                buf = xn_t[tile % 2]
                prev = [pl.get(f"xn_used_{tile - 2}")]
                toks = []
                for u in range(8):
                    lo = tile * 512 - 1
                    ncols = 514
                    dst0 = 514 * u
                    if tile == 0:
                        toks.append(pl.op("gpsimd", lambda e, dst0=dst0, buf=buf: e.memset(
                            buf[:, dst0:dst0 + 1], 0.0), wait=prev, inc="g"))
                        lo = 0; ncols = 513; dst0 += 1
                    if tile == NT - 1:
                        toks.append(pl.op("gpsimd", lambda e, u=u, buf=buf: e.memset(
                            buf[:, 514 * u + 513:514 * u + 514], 0.0), wait=prev, inc="g"))
                        ncols -= 1
                    toks.append(pl.dma("gpsimd", buf[:, dst0:dst0 + ncols],
                                       xn_cm_d[u, :, lo:lo + ncols], "d_xt",
                                       wait=prev + p0_done, slots=4))
                return buf, toks

            def load_tab(tile, cos_name, sin_name):
                s0 = tile * 512
                tc, ts_ = tabc[0], tabs[0]
                tw = [pl.get(f"tab_used_{tile - 1}")] + (p0_done if tile == 0 else [])
                d1 = pl.dma("sync", tc[:], din[cos_name][:, s0:s0 + 512], "d_tab",
                            wait=tw, slots=2)
                d2 = pl.dma("sync", ts_[:], din[sin_name][:, s0:s0 + 512], "d_tab",
                            wait=tw, slots=2)
                return tc, ts_, [d1, d2]

            def conv_rff(tag, slot, i, xnbuf, xn_toks, w_tok):
                hk_toks = []
                for c in range(8):
                    bk = f"PS{c % 2}"
                    bank = PS[c % 2]

                    def mmf(e, c=c, slot=slot, xnbuf=xnbuf, bank=bank):
                        g = c // 2
                        prs = []
                        for t3 in range(3):
                            for uu in range(2):
                                prs.append((slot["cw"][:, (t3 * 2 + uu) * 1024 + c * 128:
                                                       (t3 * 2 + uu) * 1024 + (c + 1) * 128],
                                            xnbuf[:, 514 * (2 * g + uu) + t3:
                                                  514 * (2 * g + uu) + t3 + 512]))
                        return em_mm_group(e, bank[:], prs)
                    mt = pl.op("tensor", mmf, wait=[pl.bwait(bk)] + list(w_tok) + xn_toks, inc="pe")
                    ev = pl.op("vector", lambda e, c=c, bank=bank, i=i: e.tensor_scalar(
                        out=h_bf[:, c * 512:(c + 1) * 512], in0=bank[:], scalar1=1.0,
                        scalar2=cb_col[i][0][:, c:c + 1], op0=ALU.mult, op1=ALU.add),
                        wait=[mt, cb_col[i][1], pl.get("h_user")], inc="v")
                    pl.buse(bk, ev)
                    hk_toks.append(ev)
                f_toks = []
                last_rff_mm = []
                for g3 in range(3):
                    gg = float(np.sqrt(2.0 * GAMMAS[g3]) / TWO_PI)
                    for mc in range(2):
                        u = g3 * 2 + mc
                        bk = f"PS{2 + u % 2}"
                        bank = PS[2 + u % 2]

                        def mmf(e, g3=g3, mc=mc, slot=slot, bank=bank):
                            prs = [(slot["rw"][:, kc * 768 + g3 * 256 + mc * 128:
                                               kc * 768 + g3 * 256 + (mc + 1) * 128],
                                    h_bf[:, kc * 512:(kc + 1) * 512]) for kc in range(8)]
                            return em_mm_group(e, bank[:], prs)
                        mt = pl.op("tensor", mmf, wait=hk_toks + [pl.bwait(bk)], inc="pe")
                        last_rff_mm.append(mt)
                        zp = u % 2
                        zt = zt_sb[:, 512 * zp:512 * (zp + 1)]
                        zr = zr_sb[:, 512 * zp:512 * (zp + 1)]
                        zi = zi_sb[:, 512 * zp:512 * (zp + 1)]
                        c1 = pl.op("vector", lambda e, bank=bank, gg=gg, u=u, i=i, zt=zt:
                                   e.tensor_scalar(
                                       out=zt, in0=bank[:], scalar1=gg,
                                       scalar2=rb_col[i][0][:, u:u + 1],
                                       op0=ALU.mult, op1=ALU.add),
                                   wait=[mt, rb_col[i][1], pl.get(f"zt_user_{zp}")], inc="v")
                        pl.buse(bk, c1)
                        c2 = pl.op("vector", lambda e, zt=zt, zi=zi: e.tensor_copy(
                            out=zi, in_=zt), wait=[c1, pl.get(f"zi_user_{zp}")], inc="v")
                        c3 = pl.op("vector", lambda e, zi=zi, zr=zr: e.tensor_copy(
                            out=zr, in_=zi), wait=[c2, pl.get(f"zr_user_{zp}")], inc="v")
                        pl.save(f"zi_user_{zp}", c3)
                        c4 = pl.op("vector", lambda e, zt=zt, zr=zr: e.tensor_sub(zr, zt, zr),
                                   wait=[c3], inc="v")
                        c4b = pl.op("vector", lambda e, zt=zt, zr=zr: e.scalar_tensor_tensor(
                            out=zt, in0=zr, scalar=0.5, in1=zr,
                            op0=ALU.is_gt, op1=ALU.subtract), wait=[c4], inc="v")
                        c4c = pl.op("vector", lambda e, zt=zt, zr=zr: e.scalar_tensor_tensor(
                            out=zt, in0=zr, scalar=-0.5, in1=zt,
                            op0=ALU.is_lt, op1=ALU.subtract), wait=[c4b], inc="v")
                        pl.save(f"zr_user_{zp}", c4c)
                        c5 = pl.op("scalar", lambda e, u=u, zt=zt: e.activation(
                            out=f_bf[:, u * 512:(u + 1) * 512], in_=zt, func=AF.Sin,
                            scale=TWO_PI), wait=[c4c, pl.get(f"f_user_{u}")], inc="s")
                        pl.save(f"zt_user_{zp}", c5)
                        f_toks.append(c5)
                pl.save("h_user", last_rff_mm[-1])
                return f_toks

            def proj_cm(tag, slot, i, f_toks, dst):
                d_toks = []
                for c in range(8):
                    bk = f"PS{c % 2}"
                    bank = PS[c % 2]

                    def mmf(e, c=c, slot=slot, bank=bank):
                        prs = [(slot["pw"][:, fc * 1024 + c * 128: fc * 1024 + (c + 1) * 128],
                                f_bf[:, fc * 512:(fc + 1) * 512]) for fc in range(6)]
                        return em_mm_group(e, bank[:], prs)
                    mt = pl.op("tensor", mmf, wait=f_toks + [pl.bwait(bk)], inc="pe")
                    ev = pl.op("vector", lambda e, c=c, bank=bank, dst=dst, i=i: e.tensor_scalar(
                        out=dst[:, c * 512:(c + 1) * 512], in0=bank[:], scalar1=1.0,
                        scalar2=pb_col[i][0][:, c:c + 1], op0=ALU.mult, op1=ALU.add),
                        wait=[mt, pb_col[i][1], pl.get(f"proj_user_{c}")], inc="v")
                    pl.buse(bk, ev)
                    d_toks.append(ev)
                for u in range(6):
                    pl.save(f"f_user_{u}", pl.tok("pe"))
                return d_toks

            def rope_cm(tag, tile, src, dst, src_toks, tc, ts_, tab_toks):
                out_toks = []
                for c in range(8):
                    bk = f"PS{4 + c % 2}"
                    bank = PS[4 + c % 2]
                    par = c % 2
                    r1 = rt1[:, 512 * par:512 * (par + 1)]
                    r2 = rt2[:, 512 * par:512 * (par + 1)]
                    pm = pl.op("tensor", lambda e, c=c, src=src, bank=bank: e.matmul(
                        bank[:], CS["shuf"][:], src[:, c * 512:(c + 1) * 512],
                        start=True, stop=True),
                        wait=[src_toks[c], pl.bwait(bk)] + consts_tok, inc="pe")
                    m1 = pl.op("vector", lambda e, c=c, src=src, tc=tc, r1=r1: e.tensor_mul(
                        r1, src[:, c * 512:(c + 1) * 512], tc[:]),
                        wait=[src_toks[c], pl.get(f"rt1_user_{par}")] + tab_toks, inc="v")
                    m2 = pl.op("vector", lambda e, ts_=ts_, bank=bank, r2=r2: e.tensor_mul(
                        r2, bank[:], ts_[:]),
                        wait=[pm, pl.get(f"rt2_user_{par}")] + tab_toks, inc="v")
                    pl.buse(bk, m2)
                    m3 = pl.op("vector", lambda e, c=c, dst=dst, r1=r1, r2=r2: e.tensor_add(
                        dst[:, c * 512:(c + 1) * 512], r1, r2),
                        wait=[m1, m2, pl.get(f"rope_user_{c}")], inc="v")
                    pl.save(f"rt1_user_{par}", m3)
                    pl.save(f"rt2_user_{par}", m3)
                    pl.save(f"{tag}_srcuse_{c}", [pm, m1])
                    out_toks.append(m3)
                return out_toks

            for tile in range(NT):
                emit_precast(18)
                xnbuf, xn_toks = load_xn_tile(tile)
                tc, ts_, tab_toks = load_tab(tile, "t_pr_cos", "t_pr_sin")

                fv_toks = conv_rff("v", slotB, 2, xnbuf, xn_toks, wB_tok[-1:])
                v_toks = []
                for jj in range(4):
                    for n in range(2):
                        bk = f"PS{n}"
                        bank = PS[n]

                        def mmf(e, jj=jj, n=n, bank=bank):
                            prs = [(f_bf[:, fc * 512 + jj * 128: fc * 512 + jj * 128 + 128],
                                    slotB["pw"][:, fc * 1024 + n * 512: fc * 1024 + (n + 1) * 512])
                                   for fc in range(6)]
                            for q, (l, r) in enumerate(prs):
                                e.matmul(bank[:], l, r, start=(q == 0), stop=False)
                            return e.matmul(bank[:], CS["ones1x"][:],
                                            pbrow[:, n * 512:(n + 1) * 512],
                                            start=False, stop=True)
                        mt = pl.op("tensor", mmf,
                                   wait=fv_toks + [pl.bwait(bk), pbrow_tok] + consts_tok, inc="pe")
                        ev = pl.op("scalar", lambda e, jj=jj, n=n, bank=bank: e.copy(
                            out=v_aug[:, 1040 * jj:1040 * (jj + 1)].rearrange(
                                "p (h d) -> p h d", d=65)[:, 8 * n:8 * (n + 1), 0:64],
                            in_=bank[:].rearrange("p (h d) -> p h d", d=64)),
                            wait=[mt, va_ones, pl.get(f"vaug_user_{jj}_{n}")], inc="s")
                        pl.buse(bk, ev)
                        v_toks.append(ev)
                        pl.save(f"v_ready_{jj}_{n}", ev)
                for u in range(6):
                    pl.save(f"f_user_{u}", pl.tok("pe"))

                fk_toks = conv_rff("k", slotA, 1, xnbuf, xn_toks, wA_tok[-1:])
                kd_toks = proj_cm("k", slotA, 1, fk_toks, proj_sb)
                pl.save(f"xn_used_{tile}", [kd_toks[-1], pl.tok("pe")])
                kr_toks = rope_cm("k", tile, proj_sb, r_bf, kd_toks, tc, ts_, tab_toks)
                pl.save(f"tab_used_{tile}", pl.tok("v"))

                nA_mm = []
                for c in range(8):
                    spar = c % 2
                    sq = pl.op("scalar", lambda e, c=c, spar=spar: e.activation(
                        out=sq_bf[:, 512 * spar:512 * (spar + 1)],
                        in_=r_bf[:, c * 512:(c + 1) * 512], func=AF.Square),
                        wait=[kr_toks[c], pl.get(f"sq_user_{spar}")], inc="s")
                    mm = pl.op("tensor", lambda e, c=c, spar=spar: e.matmul(
                        PS[5][0:16, :], CS["bdh16"][:, 16 * c:16 * (c + 1)],
                        sq_bf[:, 512 * spar:512 * (spar + 1)],
                        start=(c == 0), stop=(c == 7)),
                        wait=[sq] + ([pl.bwait("PS5")] if c == 0 else []), inc="pe")
                    pl.save(f"sq_user_{spar}", mm)
                    nA_mm.append(mm)
                rrev = pl.op("scalar", lambda e: e.activation(
                    out=rr16[:], in_=PS[5][0:16, :], func=AF.Copy, scale=-1.0 / 64.0),
                    wait=[nA_mm[-1], pl.get("rr16_user")], inc="s")
                pl.buse("PS5", rrev)

                kny_toks = []
                rep_toks = []
                for c in range(8):
                    bk = f"PS{2 + c % 2}"
                    bank = PS[2 + c % 2]
                    mm1 = pl.op("tensor", lambda e, c=c, bank=bank: e.matmul(
                        bank[:], lam2[:], r_bf[:, c * 512:(c + 1) * 512], start=True, stop=False),
                        wait=[kr_toks[c], pl.bwait(bk)] + lam_ready, inc="pe")
                    mm2 = pl.op("tensor", lambda e, c=c, bank=bank: e.matmul(
                        bank[:], CS["sel2h"][:, 128 * c:128 * (c + 1)], rr16[:],
                        start=False, stop=True), wait=[rrev], inc="pe")
                    rep_toks.append(mm2)
                    ex = pl.op("scalar", lambda e, c=c, bank=bank: e.activation(
                        out=proj_sb[:, c * 512:(c + 1) * 512], in_=bank[:], func=AF.Exp,
                        bias=lamn[:, 0:1]),
                        wait=[mm2] + pl.get(f"k_srcuse_{c}") + [pl.get(f"proj_user_{c}")], inc="s")
                    pl.buse(bk, ex)
                    pl.save(f"proj_user_{c}", ex)
                    kny_toks.append(ex)
                pl.save("rr16_user", rep_toks[-1])

                nB_mm = []
                for c in range(8):
                    spar = c % 2
                    sq = pl.op("scalar", lambda e, c=c, spar=spar: e.activation(
                        out=sq_bf[:, 512 * spar:512 * (spar + 1)],
                        in_=proj_sb[:, c * 512:(c + 1) * 512], func=AF.Square),
                        wait=[kny_toks[c], pl.get(f"sq_user_{spar}")], inc="s")
                    mm = pl.op("tensor", lambda e, c=c, spar=spar: e.matmul(
                        PS[5][0:16, :], CS["bdh16"][:, 16 * c:16 * (c + 1)],
                        sq_bf[:, 512 * spar:512 * (spar + 1)],
                        start=(c == 0), stop=(c == 7)),
                        wait=[sq] + ([pl.bwait("PS5")] if c == 0 else []), inc="pe")
                    pl.save(f"sq_user_{spar}", mm)
                    nB_mm.append(mm)
                nev = pl.op("scalar", lambda e: e.activation(
                    out=n16[:], in_=PS[5][0:16, :], func=AF.Copy, scale=-0.5, bias=-LN16),
                    wait=[nB_mm[-1], pl.get("n16_user")], inc="s")
                pl.buse("PS5", nev)
                kc_toks = []
                for jj in range(4):
                    tr = pl.op("tensor", lambda e, jj=jj: e.transpose(
                        out=PS[5][:, 256 + jj * 16:256 + (jj + 1) * 16],
                        in_=n16[:, jj * 128:(jj + 1) * 128],
                        identity=CS["eye"][0:16, 0:16]),
                        wait=[nev, pl.bwait("PS5")], inc="pe")
                    ev = pl.op("vector", lambda e, jj=jj: e.tensor_copy(
                        out=kncols[:, jj * 16:(jj + 1) * 16],
                        in_=PS[5][:, 256 + jj * 16:256 + (jj + 1) * 16]),
                        wait=[tr, pl.get(f"kncols_user_{jj}")], inc="v")
                    pl.buse("PS5", ev)
                    kc_toks.append(ev)
                pl.save("n16_user", kc_toks[-1])

                for h in range(NH):
                    c, hh = h // 2, h % 2
                    for jp in range(2):
                        bk = f"PS{jp}"
                        bank = PS[jp]
                        mmk = []
                        for jo in range(2):
                            jj = jp * 2 + jo
                            mmk.append(pl.op("tensor", lambda e, c=c, hh=hh, jj=jj, jo=jo, bank=bank: e.matmul(
                                bank[:, jo * 256:(jo + 1) * 256],
                                proj_sb[64 * hh:64 * (hh + 1), c * 512 + jj * 128: c * 512 + (jj + 1) * 128],
                                ombf[64 * hh:64 * (hh + 1), :], start=(jo == 0), stop=(jo == 1)),
                                wait=[kny_toks[c], om_tok] + ([pl.bwait(bk)] if jo == 0 else []),
                                inc="pe"))
                        exs = []
                        for jo in range(2):
                            jj = jp * 2 + jo
                            exs.append(pl.op("scalar", lambda e, h=h, jj=jj, jo=jo, bank=bank: e.activation(
                                out=kp_bf[:, jo * 256:(jo + 1) * 256],
                                in_=bank[:, jo * 256:(jo + 1) * 256], func=AF.Exp,
                                bias=kncols[:, jj * 16 + h:jj * 16 + h + 1]),
                                wait=mmk + [kc_toks[jj], pl.get("kp_user")], inc="s"))
                        pl.buse(bk, exs[-1])
                        bkv = f"PK{h % 2}"
                        bankkv = PK[h % 2]

                        def kvf(e, h=h, jp=jp, tile=tile, bankkv=bankkv):
                            last = None
                            for jo in range(2):
                                jj = jp * 2 + jo
                                last = e.matmul(
                                    bankkv[0:65, 0:256],
                                    v_aug[:, 1040 * jj + 65 * h: 1040 * jj + 65 * (h + 1)],
                                    kp_bf[:, jo * 256:(jo + 1) * 256],
                                    start=(jp == 0 and jo == 0), stop=(jp == 1 and jo == 1))
                            return last
                        kvm = pl.op("tensor", kvf,
                                    wait=exs + [pl.get(f"v_ready_{jp * 2}_{h // 8}"),
                                                pl.get(f"v_ready_{jp * 2 + 1}_{h // 8}")]
                                    + ([pl.bwait(bkv)] if jp == 0 else []), inc="pe")
                        pl.save("kp_user", kvm)
                        if jp == 1:
                            if tile == 0:
                                kva = pl.op("vector", lambda e, h=h, bankkv=bankkv: e.tensor_copy(
                                    out=kv_acc[:, 256 * h:256 * (h + 1)], in_=bankkv[0:65, 0:256]),
                                    wait=[kvm], inc="v")
                            else:
                                kva = pl.op("vector", lambda e, h=h, bankkv=bankkv: e.tensor_add(
                                    kv_acc[:, 256 * h:256 * (h + 1)],
                                    kv_acc[:, 256 * h:256 * (h + 1)], bankkv[0:65, 0:256]),
                                    wait=[kvm], inc="v")
                            pl.buse(bkv, kva)
                petok = pl.tok("pe")
                stok = pl.tok("s")
                for jj in range(4):
                    for n in range(2):
                        pl.save(f"vaug_user_{jj}_{n}", petok)
                for c in range(8):
                    pl.save(f"proj_user_{c}", petok)
                    pl.save(f"rope_user_{c}", [petok, stok])
                for jj in range(4):
                    pl.save(f"kncols_user_{jj}", stok)

            kvT_toks = []
            vdone = pl.tok("v")
            for h in range(NH):
                for ec in range(2):
                    tr = pl.op("tensor", lambda e, h=h, ec=ec: e.transpose(
                        out=PS[5][:, 0:65],
                        in_=kv_acc[0:65, 256 * h + 128 * ec: 256 * h + 128 * (ec + 1)],
                        identity=CS["eye"][0:65, 0:65]),
                        wait=[vdone, pl.bwait("PS5")], inc="pe")
                    ev = pl.op("vector", lambda e, h=h, ec=ec: e.tensor_copy(
                        out=kvT_bf[:, 130 * h + 65 * ec: 130 * h + 65 * (ec + 1)],
                        in_=PS[5][:, 0:65]), wait=[tr], inc="v")
                    pl.buse("PS5", ev)
                    kvT_toks.append(ev)

            t_kv = tap("kv", (65, 16 * 256))
            if t_kv is not None:
                pl.dma("sync", t_kv, kv_acc[:], "d_dbg", wait=[vdone])
            t_kny = tap("kny", (128, 4096))
            if t_kny is not None:
                pl.dma("gpsimd", t_kny, proj_sb[:], "d_dbg2", wait=[pl.tok("s"), pl.tok("pe")])

            st1.close()
            st2 = ExitStack()
            sb2 = mksb(st2, side="right")
            num_sb = sb2("num_sb", [128, 8 * 512], BF16)
            den_one = [sb2(f"den_one{i}", [1, 512]) for i in range(2)]
            den16 = sb2("den16", [16, 512])
            den16b = sb2("den16b", [16, 512], BF16)
            invrep = sb2("invrep", [128, 512])
            attn_bf = sb2("attn_bf", [128, 4096], BF16)
            outw_sb = sb2("outw_sb", [128, 8 * 1024], BF16)
            ao_sb = sb2("ao_sb", [128, 4096])
            x_ld = [sb2("x_ld0", [128, C])]
            accst = [sb2("accst0", [128, C])]
            p1_all = [pl.tok("pe"), pl.tok("v"), pl.tok("s")]
            for key in ["xld_user", "accst_user", "den16_user", "invrep_user",
                        "denone_user_0", "denone_user_1"]:
                pl.save(key, p1_all)
            for c in range(8):
                pl.save(f"attn_user_{c}", p1_all)
            for cw_ in range(8):
                pl.save(f"numsb_user_{cw_}", p1_all)
            wQ_tok = load_proj_weights(slotA, 0, p1_all)
            outw_toks = [pl.dma("gpsimd", outw_sb[:, ic * 1024:(ic + 1) * 1024],
                                din["out_w"][ic * 128:(ic + 1) * 128, :], "d_w",
                                wait=p1_all, slots=6) for ic in range(8)]
            p1_pe = pl.tok("pe")
            for c in range(8):
                pl.save(f"proj_user_{c}", p1_pe)
            pl.save("xn_used_-2", pl.get(f"xn_used_{NT - 2}"))
            pl.save("xn_used_-1", pl.get(f"xn_used_{NT - 1}"))
            pl.save("tab_used_-1", pl.get(f"tab_used_{NT - 1}"))

            for tile in range(NT):
                emit_precast(18)
                xnbuf, xn_toks = load_xn_tile(tile)
                tc, ts_, tab_toks = load_tab(tile, "t_pr_cos", "t_pr_sin")
                fq_toks = conv_rff("q", slotA, 0, xnbuf, xn_toks, wQ_tok[-1:])
                qd_toks = proj_cm("q", slotA, 0, fq_toks, proj_sb)
                pl.save(f"xn_used_{tile}", [qd_toks[-1], pl.tok("pe")])
                qr_toks = rope_cm("q", tile, proj_sb, r_bf, qd_toks, tc, ts_, tab_toks)
                pl.save(f"tab_used_{tile}", pl.tok("v"))
                for c in range(8):
                    pl.save(f"proj_user_{c}", pl.get(f"q_srcuse_{c}"))

                nQ_mm = []
                for c in range(8):
                    spar = c % 2
                    sq = pl.op("scalar", lambda e, c=c, spar=spar: e.activation(
                        out=sq_bf[:, 512 * spar:512 * (spar + 1)],
                        in_=r_bf[:, c * 512:(c + 1) * 512], func=AF.Square),
                        wait=[qr_toks[c], pl.get(f"sq_user_{spar}")], inc="s")
                    mm = pl.op("tensor", lambda e, c=c, spar=spar: e.matmul(
                        PS[5][0:16, :], CS["bdh16"][:, 16 * c:16 * (c + 1)],
                        sq_bf[:, 512 * spar:512 * (spar + 1)],
                        start=(c == 0), stop=(c == 7)),
                        wait=[sq] + ([pl.bwait("PS5")] if c == 0 else []), inc="pe")
                    pl.save(f"sq_user_{spar}", mm)
                    nQ_mm.append(mm)
                nqe = pl.op("scalar", lambda e: e.activation(
                    out=n16q[:], in_=PS[5][0:16, :], func=AF.Copy, scale=-0.5, bias=-LN16),
                    wait=[nQ_mm[-1], pl.get("n16q_user")], inc="s")
                pl.buse("PS5", nqe)

                n16_users = []
                wave_dnd = {0: [], 1: []}

                def emit_fav(wave, h):
                    c, hh = h // 2, h % 2
                    qoff = 1024 * (h % 2)
                    mcs = []
                    for mc in range(2):
                        bk = f"PS{2 + mc}"
                        bank = PS[2 + mc]
                        m1 = pl.op("tensor", lambda e, c=c, hh=hh, mc=mc, bank=bank: e.matmul(
                            bank[:], ombf[64 * hh:64 * (hh + 1), 128 * mc:128 * (mc + 1)],
                            r_bf[64 * hh:64 * (hh + 1), c * 512:(c + 1) * 512],
                            start=True, stop=False),
                            wait=[qr_toks[c], om_tok, pl.bwait(bk)], inc="pe")
                        m2 = pl.op("tensor", lambda e, h=h, bank=bank: e.matmul(
                            bank[:], CS["sel16b"][:, 128 * h:128 * (h + 1)], n16q[:],
                            start=False, stop=True), wait=[nqe], inc="pe")
                        n16_users.append(m2)
                        ex = pl.op("scalar", lambda e, mc=mc, bank=bank, qoff=qoff:
                                   e.activation(
                                       out=qp_bf[:, qoff + mc * 512:qoff + (mc + 1) * 512],
                                       in_=bank[:], func=AF.Exp),
                                   wait=[m2, pl.get(f"qp_user_{h % 2}")], inc="s")
                        pl.buse(bk, ex)
                        mcs.append(ex)
                    fav_toks[h] = mcs

                def emit_num(wave, h):
                    c, hh = h // 2, h % 2
                    hs = h - 8 * wave
                    qoff = 1024 * (h % 2)
                    bkv = f"PK{h % 2}"
                    bankkv = PK[h % 2]

                    def numf(e, h=h, bankkv=bankkv, qoff=qoff):
                        last = None
                        for ec in range(2):
                            last = e.matmul(bankkv[0:65, :],
                                            kvT_bf[:, 130 * h + 65 * ec:130 * h + 65 * (ec + 1)],
                                            qp_bf[:, qoff + ec * 512:qoff + (ec + 1) * 512],
                                            start=(ec == 0), stop=(ec == 1))
                        return last
                    nm = pl.op("tensor", numf,
                               wait=fav_toks[h] + [kvT_toks[-1], pl.bwait(bkv)], inc="pe")
                    pl.save(f"qp_user_{h % 2}", nm)
                    do = den_one[h % 2]
                    dv = pl.op("scalar", lambda e, do=do, bankkv=bankkv: e.copy(
                        out=do[:], in_=bankkv[64:65, :]),
                        wait=[nm, pl.get(f"denone_user_{h % 2}")], inc="s")
                    dnd = pl.dma("sync", bounce[(2 * tile + wave) % 4, 512 * hs:512 * (hs + 1)][None, :],
                                 do[:], "d_bn", wait=[dv], slots=4)
                    pl.save(f"denone_user_{h % 2}", dnd)
                    wave_dnd[wave].append(dnd)
                    nv = pl.op("vector", lambda e, hh=hh, c=c, bankkv=bankkv: e.tensor_copy(
                        out=num_sb[64 * hh:64 * (hh + 1), 512 * c:512 * (c + 1)],
                        in_=bankkv[0:64, :]),
                        wait=[dv, pl.get(f"numsb_user_{c}")], inc="v")
                    pl.buse(bkv, nv)

                def emit_wave_den(wave):
                    brow = (2 * tile + wave) % 4
                    dd2 = pl.dma("sync", den16[0:8, :], bounce[brow, 0:4096].rearrange(
                        "(p f) -> p f", p=8), "d_bn",
                        wait=wave_dnd[wave] + [pl.get("den16_user")], slots=2)
                    iv1 = pl.op("vector", lambda e: e.tensor_scalar(
                        out=den16[0:8, :], in0=den16[0:8, :], scalar1=1.0, scalar2=1e-6,
                        op0=ALU.mult, op1=ALU.add), wait=[dd2], inc="v")
                    iv1b = pl.op("vector", lambda e: e.reciprocal(
                        out=den16[0:8, :], in_=den16[0:8, :]), wait=[iv1], inc="v")
                    iv2 = pl.op("scalar", lambda e: e.activation(
                        out=den16b[0:8, :], in_=den16[0:8, :], func=AF.Copy),
                        wait=[iv1b], inc="s")
                    for c in range(4 * wave, 4 * wave + 4):
                        rm = pl.op("tensor", lambda e, c=c, wave=wave: e.matmul(
                            PS[4][:], CS["sel2h"][0:8, 128 * (c - 4 * wave):128 * (c - 4 * wave + 1)],
                            den16b[0:8, :], start=True, stop=True),
                            wait=[iv2, pl.bwait("PS4")], inc="pe")
                        re_ = pl.op("scalar", lambda e: e.copy(out=invrep[:], in_=PS[4][:]),
                                    wait=[rm, pl.get("invrep_user")], inc="s")
                        pl.buse("PS4", re_)
                        tt = pl.op("vector", lambda e, c=c: e.tensor_mul(
                            attn_bf[:, c * 512:(c + 1) * 512],
                            num_sb[:, 512 * c:512 * (c + 1)],
                            invrep[:]),
                            wait=[re_, pl.get(f"attn_user_{c}")], inc="v")
                        pl.save(f"numsb_user_{c}", tt)
                        pl.save("invrep_user", tt)
                    pl.save("den16_user", pl.tok("v"))

                for wave in range(2):
                    fav_toks = {}
                    hlist = list(range(8 * wave, 8 * wave + 8))
                    for idx in range(9):
                        if idx < 8:
                            emit_fav(wave, hlist[idx])
                        if idx >= 1:
                            emit_num(wave, hlist[idx - 1])
                    if wave == 1:
                        emit_wave_den(0)
                emit_wave_den(1)
                pl.save("n16q_user", n16_users[-1])
                att_toks = [pl.tok("v")]

                ao_toks = []
                for oc in range(8):
                    bk = f"PS{oc % 2}"
                    bank = PS[oc % 2]

                    def mmf(e, oc=oc, bank=bank):
                        prs = [(outw_sb[:, ic * 1024 + oc * 128: ic * 1024 + (oc + 1) * 128],
                                attn_bf[:, ic * 512:(ic + 1) * 512]) for ic in range(8)]
                        return em_mm_group(e, bank[:], prs)
                    mt = pl.op("tensor", mmf, wait=att_toks + outw_toks + [pl.bwait(bk)], inc="pe")
                    ev = pl.op("vector", lambda e, oc=oc, bank=bank: e.tensor_scalar(
                        out=ao_sb[:, oc * 512:(oc + 1) * 512], in0=bank[:], scalar1=1.0,
                        scalar2=outb_col[0][:, oc:oc + 1], op0=ALU.mult, op1=ALU.add),
                        wait=[mt, outb_col[1], pl.get(f"ao_user_{oc}")], inc="v")
                    pl.buse(bk, ev)
                    ao_toks.append(ev)
                tr_last = {}
                for jj in range(4):
                    xl = x_ld[0]
                    ac = accst[0]
                    lx = pl.dma("sync", xl[:], x_h[tile * 512 + jj * 128: tile * 512 + (jj + 1) * 128, :],
                                "d_x2", wait=[pl.get("xld_user")], slots=2)
                    adds = []
                    for oc in range(8):
                        tbk = f"PS{4 + oc % 2}"
                        tbank = PS[4 + oc % 2]
                        tr = pl.op("tensor", lambda e, jj=jj, oc=oc, tbank=tbank: e.transpose(
                            out=tbank[:, 128:256],
                            in_=ao_sb[:, oc * 512 + jj * 128: oc * 512 + (jj + 1) * 128],
                            identity=CS["eye"][:]),
                            wait=[ao_toks[oc], pl.bwait(tbk)], inc="pe")
                        tr_last[oc] = tr
                        ad = pl.op("vector", lambda e, oc=oc, xl=xl, ac=ac, tbank=tbank: e.tensor_add(
                            ac[:, oc * 128:(oc + 1) * 128], tbank[:, 128:256],
                            xl[:, oc * 128:(oc + 1) * 128]),
                            wait=[tr, lx, pl.get("accst_user")], inc="v")
                        pl.buse(tbk, ad)
                        adds.append(ad)
                    pl.save("xld_user", adds[-1])
                    da = pl.dma("sync", acc_d[tile * 512 + jj * 128: tile * 512 + (jj + 1) * 128, :],
                                ac[:], "d_acc", wait=adds, slots=4)
                    pl.save("accst_user", da)
                for oc in range(8):
                    pl.save(f"ao_user_{oc}", tr_last[oc])
                petok = pl.tok("pe")
                stok = pl.tok("s")
                for c in range(8):
                    pl.save(f"attn_user_{c}", petok)
                    pl.save(f"rope_user_{c}", [petok, stok])

            st2.close()
            perf_done = [pl.tok("v"), pl.tok("s"), pl.tok("pe")] + pl.chan_toks("d_acc", 4)

            t_acc = tap("acc", (S, C))
            if t_acc is not None:
                pl.dma("sync", t_acc, acc_d, "d_dbg", wait=perf_done)


        with ExitStack() as stf:
            sbf = mksb(stf)
            xf_in = [sbf(f"xf_in{i}", [128, 8 * 514]) for i in range(2)]
            xs_bf = sbf("xs_bf", [128, 8 * 512], BF16)
            sqf_bf = sbf("sqf_bf", [128, 2 * 512], BF16)
            ftab = [sbf(f"ftab{i}", [128, 512], BF16) for i in range(4)]
            xf_st = [sbf(f"xf_st{i}", [128, 512]) for i in range(2)]
            frt1 = sbf("frt1", [128, 512])
            frt2 = sbf("frt2", [128, 512])
            frp1 = sbf("frp1", [128, 2 * 512], BF16)
            frp2 = sbf("frp2", [128, 2 * 512], BF16)
            eye_bf = sbf("eye_bf", [128, 128], BF16)
            onescol_bf = sbf("onescol_bf", [128, 1], BF16)
            statA = sbf("statA", [1, 512])
            statR = sbf("statR", [1, 512])
            statR_bf = sbf("statR_bf", [1, 512], BF16)
            statB2 = sbf("statB2", [2, 512], BF16)
            growf = sbf("growf", [2, C])
            growf_bf = sbf("growf_bf", [2, C], BF16)
            selstage = sbf("selstage", [16, 1024])
            ec = pl.op("vector", lambda e: e.tensor_copy(out=eye_bf[:], in_=CS["eye"][:]),
                       wait=consts_tok + perf_done, inc="v")
            go1 = pl.op("gpsimd", lambda e: e.memset(onescol_bf[:], 1.0), wait=perf_done, inc="g")
            go2 = pl.op("gpsimd", lambda e: e.memset(statB2[:], -1.0), wait=perf_done, inc="g")
            ffn_setup = [ec, go1, go2]

            def sh0(c):
                return 514 * c + (1 if c < 4 else 0)

            def load_xf_tile(buf, src_d, tile, prev_users):
                toks = []
                for u in range(8):
                    lo = tile * 512 - 1
                    ncols = 514
                    dst0 = 514 * u
                    if tile == 0:
                        toks.append(pl.op("gpsimd", lambda e, dst0=dst0, buf=buf: e.memset(
                            buf[:, dst0:dst0 + 1], 0.0), wait=prev_users, inc="g"))
                        lo = 0; ncols = 513; dst0 += 1
                    if tile == NT - 1:
                        ncols -= 1
                    toks.append(pl.dma("sync", buf[:, dst0:dst0 + ncols],
                                       src_d[u, :, lo:lo + ncols], "d_xf",
                                       wait=prev_users, slots=4))
                return toks

            def colvec_ragged(tag, vec1d, n, wait=(), sba=None):
                sba = sba or sbf
                ncol = (n + 127) // 128
                nfull = n // 128
                tail = n - nfull * 128
                stage = sba(f"fcv_s_{tag}", [ncol, 128])
                out = sba(f"fcv_{tag}", [128, ncol])
                toks = []
                if tail:
                    toks.append(pl.op("gpsimd", lambda e: e.memset(stage[:], 0.0),
                                      wait=wait, inc="g"))
                toks.append(pl.dma("sync", stage[0:nfull, :],
                                   vec1d[0:nfull * 128].rearrange("(a p) -> a p", p=128),
                                   "d_lcv", wait=list(wait) + toks))
                if tail:
                    toks.append(pl.dma("sync", stage[nfull:nfull + 1, 0:tail],
                                       vec1d[nfull * 128:n][None, :], "d_lcv",
                                       wait=list(wait) + toks[:1]))
                tr = pl.op("tensor", lambda e: e.transpose(
                    out=PS[5][:, 0:ncol], in_=stage[:], identity=CS["eye"][0:ncol, 0:ncol]),
                    wait=toks + [pl.bwait("PS5")], inc="pe")
                ev = pl.op("vector", lambda e: e.tensor_copy(out=out[:], in_=PS[5][:, 0:ncol]),
                           wait=[tr], inc="v")
                pl.buse("PS5", ev)
                return out, ev

            prev_done = perf_done + ffn_setup
            src_d = xn_cm_d
            dst_d = xf_a

            import os as _os
            NLAY = int(_os.environ.get("FFN_LAYERS", DEPTH))
            for li in range(NLAY):
                stA = ExitStack()
                sbA = mksb(stA, side="right")
                wqkv = sbA(f"wqkv{li}", [128, 8 * 3072], BF16)
                wo = sbA(f"wo{li}", [128, 8 * 1024], BF16)
                q_bf = sbA(f"q_bf{li}", [128, 8 * 512], BF16)
                kr_cur = sbA(f"fkrc{li}", [128, 8 * 512], BF16)
                kr_prev = sbA(f"fkrp{li}", [128, 8 * 256], BF16)
                va_cur = sbA(f"fvac{li}", [128, 4 * 1040], BF16)
                va_prev = sbA(f"fvap{li}", [128, 2 * 1040], BF16)
                attn_cm = sbA(f"fattn_cm{li}", [128, 8 * 512], BF16)
                probsT = sbA(f"fprobsT{li}", [128, 2 * 384], BF16)
                probs3 = sbA(f"fprobs3{li}", [128, 3 * 384], BF16)
                maxng = sbA(f"fmaxng{li}", [128, 16])
                dencol = sbA(f"fdencol{li}", [128, 16])
                invden = sbA(f"finvden{li}", [128, 16])
                diag_bf = sbA(f"fdiag{li}", [128, 2 * 128], BF16)
                fn16q = sbA(f"ffn16q{li}", [16, 512], BF16)
                fn16k = sbA(f"ffn16k{li}", [16, 512], BF16)
                fnsq = sbA(f"ffnsq{li}", [16, 512])
                qsel_bf = sbA(f"fqselb{li}", [16, 8 * 128], BF16)
                ksel_bf = sbA(f"fkselb{li}", [16, 8 * 128], BF16)

                wq_toks = [pl.dma("gpsimd", wqkv[:, ic * 3072:(ic + 1) * 3072],
                                  din["Wqkv"][li, ic * 128:(ic + 1) * 128, :], "d_w",
                                  wait=prev_done, slots=6) for ic in range(8)]
                wo_toks = [pl.dma("gpsimd", wo[:, ic * 1024:(ic + 1) * 1024],
                                  din["Wo"][li, ic * 128:(ic + 1) * 128, :], "d_w",
                                  wait=prev_done, slots=6) for ic in range(8)]
                va_init = [pl.op("gpsimd", lambda e, vb=vb: e.memset(
                    vb[:].rearrange("p (h d) -> p h d", d=65)[:, :, 64:65], 1.0),
                    wait=prev_done, inc="g") for vb in (va_cur, va_prev)]
                selcast = []
                for scname, dst, scale in ((
                        "q_scale", qsel_bf, QK_SCALE), ("k_scale", ksel_bf, 1.0)):
                    selprev = prev_done + \
                        ([pl.get("selstage_u")] if pl.get("selstage_u") else [])
                    sz = pl.op("gpsimd", lambda e: e.memset(selstage[:], 0.0),
                               wait=selprev, inc="g")
                    selt = []
                    for c in range(8):
                        for r in range(2):
                            selt.append(pl.dma(
                                "sync", selstage[2 * c + r:2 * c + r + 1,
                                                 128 * c + 64 * r:128 * c + 64 * r + 64],
                                din[scname][li:li + 1, :], "d_lcv", wait=[sz]))
                    ct = pl.op("vector", lambda e, dst=dst, scale=scale: e.tensor_scalar(
                        out=dst[:], in0=selstage[:], scalar1=scale, scalar2=None,
                        op0=ALU.mult), wait=selt, inc="v")
                    pl.save("selstage_u", ct)
                    selcast.append(ct)

                pl.save("ftile_prev", prev_done)
                pl.save("ftile_prev2", prev_done)
                pl.save("fsq_user_0", None); pl.save("fsq_user_1", None)
                pl.save("fnu_q", prev_done); pl.save("fnu_k", prev_done)
                for par in range(2):
                    pl.save(f"diag_u{par}", prev_done)
                    pl.save(f"pT_u{par}", prev_done)
                for r3 in range(3):
                    pl.save(f"probs_u{r3}", prev_done)
                for h in range(NH):
                    pl.save(f"mx_u_{h}", prev_done)
                    pl.save(f"dc_u_{h}", prev_done)
                    pl.save(f"iv_u_{h}", prev_done)
                for c in range(8):
                    pl.save(f"fattn_u_{c}", prev_done)

                for tile in range(NT):
                    s0 = tile * 512
                    tprev = pl.get("ftile_prev")
                    tprev2 = pl.get("ftile_prev2")
                    xfb = xf_in[tile % 2]
                    xf_toks = load_xf_tile(xfb, src_d, tile, [tprev2])
                    tb_toks = []
                    for ti, nm in enumerate(["t_fx_cq", "t_fx_sq", "t_fx_ck", "t_fx_sk"]):
                        tb_toks.append(pl.dma("sync", ftab[ti][:], din[nm][:, s0:s0 + 512],
                                              "d_tab", wait=[tprev], slots=4))
                    xs_toks = []
                    for c in range(8):
                        xs_toks.append(pl.op("vector", lambda e, c=c, xfb=xfb: e.tensor_copy(
                            out=xs_bf[:, 512 * c:512 * (c + 1)],
                            in_=xfb[:, sh0(c):sh0(c) + 512]),
                            wait=xf_toks + [pl.get("xsbf_u")], inc="v"))
                    def l2norm(buf, src_toks, nbk, fn16b, tag):
                        nbank = PS[4] if nbk == "PS4" else PS[5]
                        nmm = []
                        for c in range(8):
                            par = c % 2
                            sq = pl.op("scalar", lambda e, c=c, buf=buf, par=par: e.activation(
                                out=sqf_bf[:, 512 * par:512 * (par + 1)],
                                in_=buf[:, 512 * c:512 * (c + 1)], func=AF.Square),
                                wait=[src_toks[c], pl.get(f"fsq_user_{par}")], inc="s")
                            mm = pl.op("tensor", lambda e, c=c, par=par, nbank=nbank: e.matmul(
                                nbank[0:16, :], CS["bdh16"][:, 16 * c:16 * (c + 1)],
                                sqf_bf[:, 512 * par:512 * (par + 1)],
                                start=(c == 0), stop=(c == 7)),
                                wait=[sq] + ([pl.bwait(nbk)] if c == 0 else []), inc="pe")
                            pl.save(f"fsq_user_{par}", mm)
                            nmm.append(mm)
                        e0 = pl.op("scalar", lambda e, nbank=nbank: e.activation(
                            out=fnsq[:], in_=nbank[0:16, :], func=AF.Sqrt),
                            wait=[nmm[-1], pl.get(f"fnu_{tag}"), pl.get("fnsq_u")], inc="s")
                        pl.buse(nbk, e0)
                        e0b = pl.op("vector", lambda e: e.tensor_scalar(
                            out=fnsq[:], in0=fnsq[:], scalar1=1e-12, scalar2=None,
                            op0=ALU.max), wait=[e0], inc="v")
                        e0c = pl.op("vector", lambda e: e.reciprocal(
                            out=fnsq[:], in_=fnsq[:]), wait=[e0b], inc="v")
                        e1 = pl.op("scalar", lambda e, fn16b=fn16b: e.activation(
                            out=fn16b[:], in_=fnsq[:], func=AF.Copy), wait=[e0c], inc="s")
                        pl.save("fnsq_u", e1)
                        return e1

                    def l2fin(buf, selmat_bf, e1, fn16b, tag):
                        outs = []
                        last_rm = None
                        for c in range(8):
                            bbk = f"PK{c % 2}"
                            bbank = PK[c % 2]
                            rm = pl.op("tensor", lambda e, c=c, selmat_bf=selmat_bf,
                                       fn16b=fn16b, bbank=bbank: e.matmul(
                                bbank[:, 0:512], selmat_bf[:, 128 * c:128 * (c + 1)], fn16b[:],
                                start=True, stop=True),
                                wait=[e1, pl.bwait(bbk)] + selcast, inc="pe")
                            last_rm = rm
                            tt = pl.op("vector", lambda e, c=c, buf=buf, bbank=bbank: e.tensor_mul(
                                buf[:, 512 * c:512 * (c + 1)], buf[:, 512 * c:512 * (c + 1)],
                                bbank[:, 0:512]), wait=[rm], inc="v")
                            pl.buse(bbk, tt)
                            outs.append(tt)
                        pl.save(f"fnu_{tag}", last_rm)
                        return outs

                    qs_toks = []; ks_toks = []; vv_toks = []
                    krb = kr_cur
                    vab = va_cur
                    for c in range(8):
                        bk = f"PS{c % 2}"
                        bank = PS[c % 2]

                        def mmq(e, c=c, bank=bank):
                            prs = [(wqkv[:, ic * 3072 + c * 128: ic * 3072 + (c + 1) * 128],
                                    xs_bf[:, 512 * ic:512 * (ic + 1)]) for ic in range(8)]
                            return em_mm_group(e, bank[:], prs)
                        mt = pl.op("tensor", mmq, wait=xs_toks + wq_toks + [pl.bwait(bk)], inc="pe")
                        ev = pl.op("vector", lambda e, c=c, bank=bank: e.tensor_copy(
                            out=q_bf[:, 512 * c:512 * (c + 1)], in_=bank[:]),
                            wait=[mt, tprev], inc="v")
                        pl.buse(bk, ev)
                        qs_toks.append(ev)
                    qn_e1 = l2norm(q_bf, qs_toks, "PS4", fn16q, "q")
                    for c in range(8):
                        bk = f"PS{c % 2}"
                        bank = PS[c % 2]

                        def mmk(e, c=c, bank=bank):
                            prs = [(wqkv[:, ic * 3072 + 1024 + c * 128:
                                         ic * 3072 + 1024 + (c + 1) * 128],
                                    xs_bf[:, 512 * ic:512 * (ic + 1)]) for ic in range(8)]
                            return em_mm_group(e, bank[:], prs)
                        mt = pl.op("tensor", mmk, wait=xs_toks + wq_toks + [pl.bwait(bk)], inc="pe")
                        ev = pl.op("vector", lambda e, c=c, bank=bank, krb=krb: e.tensor_copy(
                            out=krb[:, 512 * c:512 * (c + 1)], in_=bank[:]),
                            wait=[mt, tprev], inc="v")
                        pl.buse(bk, ev)
                        ks_toks.append(ev)
                    kn_e1 = l2norm(krb, ks_toks, "PS5", fn16k, "k")
                    for sl in range(4):
                        for n in range(2):
                            bk = f"PS{2 + n}"
                            bank = PS[2 + n]

                            def mmv(e, sl=sl, n=n, bank=bank):
                                prs = [(xs_bf[:, 512 * ic + 128 * sl: 512 * ic + 128 * (sl + 1)],
                                        wqkv[:, ic * 3072 + 2048 + n * 512:
                                             ic * 3072 + 2048 + (n + 1) * 512])
                                       for ic in range(8)]
                                return em_mm_group(e, bank[:], prs)
                            mt = pl.op("tensor", mmv, wait=xs_toks + wq_toks + [pl.bwait(bk)],
                                       inc="pe")
                            ev = pl.op("scalar", lambda e, sl=sl, n=n, bank=bank, vab=vab: e.copy(
                                out=vab[:, 1040 * sl:1040 * (sl + 1)].rearrange(
                                    "p (h d) -> p h d", d=65)[:, 8 * n:8 * (n + 1), 0:64],
                                in_=bank[:].rearrange("p (h d) -> p h d", d=64)),
                                wait=[mt, tprev] + va_init, inc="s")
                            pl.buse(bk, ev)
                            vv_toks.append(ev)

                    pl.save("xsbf_u", pl.tok("pe"))

                    qn_toks = l2fin(q_bf, qsel_bf, qn_e1, fn16q, "q")
                    kn_toks = l2fin(krb, ksel_bf, kn_e1, fn16k, "k")

                    def ropef(buf, src_toks, tci, tsi):
                        outs = []
                        for c in range(8):
                            par = c % 2
                            bk = f"PS{4 + par}"
                            bank = PS[4 + par]
                            r1 = frp1[:, 512 * par:512 * (par + 1)]
                            r2 = frp2[:, 512 * par:512 * (par + 1)]
                            pm = pl.op("tensor", lambda e, c=c, buf=buf, bank=bank: e.matmul(
                                bank[:], CS["shuf"][:], buf[:, 512 * c:512 * (c + 1)],
                                start=True, stop=True),
                                wait=[src_toks[c], pl.bwait(bk)] + consts_tok, inc="pe")
                            m1 = pl.op("vector", lambda e, c=c, buf=buf, tci=tci, r1=r1: e.tensor_mul(
                                r1, buf[:, 512 * c:512 * (c + 1)], ftab[tci][:]),
                                wait=[src_toks[c], pl.get(f"frt1_u{par}")] + tb_toks, inc="v")
                            m2 = pl.op("vector", lambda e, tsi=tsi, bank=bank, r2=r2: e.tensor_mul(
                                r2, bank[:], ftab[tsi][:]),
                                wait=[pm, pl.get(f"frt2_u{par}")] + tb_toks, inc="v")
                            pl.buse(bk, m2)
                            m3 = pl.op("vector", lambda e, c=c, buf=buf, r1=r1, r2=r2: e.tensor_add(
                                buf[:, 512 * c:512 * (c + 1)], r1, r2),
                                wait=[m1, m2, tprev], inc="v")
                            pl.save(f"frt1_u{par}", m3)
                            pl.save(f"frt2_u{par}", m3)
                            outs.append(m3)
                        return outs
                    qr_toks = ropef(q_bf, qn_toks, 0, 1)
                    krr_toks = ropef(krb, kn_toks, 2, 3)

                    def kloc(gc):
                        if gc // 4 == tile:
                            return kr_cur, 128 * (gc % 4)
                        return kr_prev, 128 * ((gc % 4) - 2)

                    def vloc(gc):
                        if gc // 4 == tile:
                            return va_cur, 1040 * (gc % 4)
                        return va_prev, 1040 * ((gc % 4) - 2)

                    prev_k = pl.get("krr_prev") or []
                    prev_v = pl.get("vv_prev") or []

                    NSLOT = 4 * NH
                    sim_done = [None] * NSLOT
                    tmr_done = [None] * NSLOT
                    exp_done = [None] * NSLOT
                    diag_done = [None] * NSLOT
                    trdiv_done = [None] * NSLOT
                    evac_done = [None] * NSLOT
                    av_done = [None] * NSLOT
                    oc_toks = []

                    def slmeta(i):
                        sl, h = i // NH, i % NH
                        g = 4 * tile + sl
                        wchunks = [gc for gc in (g - 2, g - 1, g) if gc >= 0]
                        return sl, h, g, wchunks

                    def emit_sims(i):
                        sl, h, g, wchunks = slmeta(i)
                        c, hh = h // 2, h % 2
                        bk = f"PS{i % 2}"
                        bank = PS[i % 2]
                        runs = []
                        for gc in wchunks:
                            tb, col = kloc(gc)
                            blk = 256 if tb is kr_prev else 512
                            base = blk * c + col
                            if runs and runs[-1][0] is tb and runs[-1][1] + runs[-1][2] == base:
                                runs[-1] = (tb, runs[-1][1], runs[-1][2] + 128)
                            else:
                                runs.append((tb, base, 128))
                        nw = len(wchunks) * 128
                        toks = []
                        colpos = 0
                        for (tb, base, wdt) in runs:
                            toks.append(pl.op("tensor",
                                lambda e, tb=tb, base=base, wdt=wdt, c=c, hh=hh, sl=sl,
                                       bank=bank, colpos=colpos: e.matmul(
                                    bank[:, colpos:colpos + wdt],
                                    q_bf[64 * hh:64 * (hh + 1),
                                          512 * c + 128 * sl:512 * c + 128 * (sl + 1)],
                                    tb[64 * hh:64 * (hh + 1), base:base + wdt],
                                    start=(colpos == 0), stop=False),
                                wait=[qr_toks[c], krr_toks[c],
                                      pl.bwait(bk) if colpos == 0 else None] + prev_k,
                                inc="pe"))
                            colpos += wdt
                        mname = "mq_full" if g >= 2 else ("mq_s1" if g == 1 else "mq_s0")
                        toks.append(pl.op("tensor",
                            lambda e, bank=bank, mname=mname, nw=nw: e.matmul(
                                bank[:, 0:nw], eye_bf[:], CS[mname][:, 0:nw],
                                start=False, stop=True),
                            wait=[ec] + consts_tok, inc="pe"))
                        sim_done[i] = toks

                    def emit_tmr(i):
                        sl, h, g, wchunks = slmeta(i)
                        nw = len(wchunks) * 128
                        par = i % 2
                        bank = PS[par]
                        t = pl.op("vector", lambda e, bank=bank, nw=nw, h=h: e.reduce_max(
                            out=maxng[:, h:h + 1], in_=bank[:, 0:nw], axis=AX, negate=True),
                            wait=sim_done[i] + [pl.get(f"mx_u_{h}")], inc="v")
                        tmr_done[i] = t

                    def emit_exp(i):
                        sl, h, g, wchunks = slmeta(i)
                        nw = len(wchunks) * 128
                        par = i % 2
                        p3 = i % 3
                        bank = PS[par]
                        t = pl.op("scalar",
                                  lambda e, nw=nw, p3=p3, h=h, bank=bank: e.activation(
                                      out=probs3[:, 384 * p3:384 * p3 + nw],
                                      in_=bank[:, 0:nw],
                                      func=AF.Exp, bias=maxng[:, h:h + 1],
                                      accum_out=dencol[:, h:h + 1]),
                                  wait=[tmr_done[i], pl.get(f"probs_u{p3}"),
                                        pl.get(f"dc_u_{h}")], inc="s")
                        pl.buse(f"PS{par}", t)
                        pl.save(f"mx_u_{h}", t)
                        exp_done[i] = t

                    def emit_recip_diag(i):
                        sl, h, g, wchunks = slmeta(i)
                        par = i % 2
                        r = pl.op("vector", lambda e, h=h: e.reciprocal(
                            out=invden[:, h:h + 1], in_=dencol[:, h:h + 1]),
                            wait=[exp_done[i], pl.get(f"iv_u_{h}")], inc="v")
                        pl.save(f"dc_u_{h}", r)
                        d = pl.op("vector", lambda e, par=par, h=h: e.tensor_scalar(
                            out=diag_bf[:, 128 * par:128 * (par + 1)], in0=eye_bf[:],
                            scalar1=invden[:, h:h + 1], scalar2=None, op0=ALU.mult),
                            wait=[r, pl.get(f"diag_u{par}"), ec], inc="v")
                        pl.save(f"iv_u_{h}", d)
                        diag_done[i] = d

                    def emit_trdiv(i):
                        sl, h, g, wchunks = slmeta(i)
                        nch = len(wchunks)
                        par = i % 2
                        p3 = i % 3
                        bk = f"PS{2 + par}"
                        bank = PS[2 + par]
                        toks = []
                        for q3 in range(nch):
                            toks.append(pl.op("tensor",
                                lambda e, q3=q3, p3=p3, par=par, bank=bank: e.matmul(
                                    bank[:, 128 * q3:128 * (q3 + 1)],
                                    probs3[:, 384 * p3 + 128 * q3:384 * p3 + 128 * (q3 + 1)],
                                    diag_bf[:, 128 * par:128 * (par + 1)],
                                    start=True, stop=True),
                                wait=[exp_done[i], diag_done[i]] +
                                     ([pl.bwait(bk)] if q3 == 0 else []), inc="pe"))
                        pl.save(f"probs_u{p3}", toks)
                        pl.save(f"diag_u{par}", toks[-1])
                        trdiv_done[i] = toks

                    def emit_evac(i):
                        sl, h, g, wchunks = slmeta(i)
                        nw = len(wchunks) * 128
                        par = i % 2
                        t = pl.op("scalar", lambda e, par=par, nw=nw: e.copy(
                            out=probsT[:, 384 * par:384 * par + nw],
                            in_=PS[2 + par][:, 0:nw]),
                            wait=trdiv_done[i] + [pl.get(f"pT_u{par}")], inc="s")
                        pl.buse(f"PS{2 + par}", t)
                        evac_done[i] = t

                    def emit_av(i):
                        sl, h, g, wchunks = slmeta(i)
                        par = i % 2
                        bkv = f"PK{par}"
                        bankkv = PK[par]
                        vlocs = tuple(vloc(gc) for gc in wchunks)

                        def avf(e, h=h, vlocs=vlocs, bankkv=bankkv, par=par):
                            last = None
                            for q3, (vb, voff) in enumerate(vlocs):
                                last = e.matmul(
                                    bankkv[0:65, 0:128],
                                    vb[:, voff + 65 * h: voff + 65 * (h + 1)],
                                    probsT[:, 384 * par + 128 * q3:384 * par + 128 * (q3 + 1)],
                                    start=(q3 == 0), stop=(q3 == len(vlocs) - 1))
                            return last
                        t = pl.op("tensor", avf,
                                  wait=[evac_done[i], pl.bwait(bkv)] + vv_toks + prev_v,
                                  inc="pe")
                        pl.save(f"pT_u{par}", t)
                        av_done[i] = t

                    def emit_outcopy(i):
                        sl, h, g, wchunks = slmeta(i)
                        c, hh = h // 2, h % 2
                        par = i % 2
                        t = pl.op("vector", lambda e, c=c, hh=hh, sl=sl, par=par: e.tensor_copy(
                            out=attn_cm[64 * hh:64 * (hh + 1),
                                        512 * c + 128 * sl:512 * c + 128 * (sl + 1)],
                            in_=PK[par][0:64, 0:128]),
                            wait=[av_done[i], pl.get(f"fattn_u_{c}")], inc="v")
                        pl.buse(f"PK{par}", t)
                        oc_toks.append(t)

                    for i_ in range(NSLOT + 4):
                        if i_ < NSLOT:
                            emit_sims(i_)
                            emit_tmr(i_)
                            emit_exp(i_)
                        if 1 <= i_ <= NSLOT:
                            emit_recip_diag(i_ - 1)
                        if 2 <= i_ < NSLOT + 2:
                            emit_trdiv(i_ - 2)
                            emit_evac(i_ - 2)
                        if 3 <= i_ < NSLOT + 3:
                            emit_av(i_ - 3)
                        if 4 <= i_:
                            emit_outcopy(i_ - 4)

                    slpe = pl.tok("pe")
                    cp_toks = []
                    for c in range(8):
                        cp_toks.append(pl.op("vector", lambda e, c=c: e.tensor_copy(
                            out=kr_prev[:, 256 * c:256 * (c + 1)],
                            in_=kr_cur[:, 512 * c + 256:512 * c + 512]),
                            wait=[slpe], inc="v"))
                    for b2_ in range(2):
                        cp_toks.append(pl.op("vector", lambda e, b2_=b2_: e.tensor_copy(
                            out=va_prev[:, 1040 * b2_:1040 * (b2_ + 1)],
                            in_=va_cur[:, 1040 * (2 + b2_):1040 * (3 + b2_)]),
                            wait=[slpe], inc="v"))
                    pl.save("krr_prev", cp_toks)
                    pl.save("vv_prev", cp_toks)

                    for oc in range(8):
                        bk = f"PS{oc % 2}"
                        bank = PS[oc % 2]

                        def mmo(e, oc=oc, bank=bank):
                            prs = [(wo[:, ic * 1024 + oc * 128: ic * 1024 + (oc + 1) * 128],
                                    attn_cm[:, 512 * ic:512 * (ic + 1)]) for ic in range(8)]
                            return em_mm_group(e, bank[:], prs)
                        mt = pl.op("tensor", mmo, wait=oc_toks + wo_toks + [pl.bwait(bk)], inc="pe")
                        stg = xf_st[oc % 2]
                        ad = pl.op("vector", lambda e, oc=oc, bank=bank, stg=stg, xfb=xfb:
                                   e.tensor_add(stg[:], bank[:],
                                                xfb[:, 514 * oc + 1:514 * oc + 513]),
                                   wait=[mt, pl.get(f"xfst_u{oc % 2}")], inc="v")
                        pl.buse(bk, ad)
                        dw = pl.dma("sync", dst_d[oc, :, s0:s0 + 512], stg[:], "d_xfw",
                                    wait=[ad], slots=4)
                        pl.save(f"xfst_u{oc % 2}", dw)
                    wo_pe = pl.tok("pe")
                    for c in range(8):
                        pl.save(f"fattn_u_{c}", wo_pe)
                    pl.save("ftile_prev2", pl.get("ftile_prev"))
                    pl.save("ftile_prev", [pl.tok("pe"), pl.tok("v"), pl.tok("s")])

                sub_done = [pl.tok("pe"), pl.tok("v"), pl.tok("s")] + \
                    pl.chan_toks("d_xfw", 4)
                stA.close()
                prev_done = sub_done
                src_d = dst_d
                dst_d = xf_b if dst_d is xf_a else xf_a
                t_att = tap(f"fattn{li}", (8, 128, S))
                if t_att is not None:
                    pl.dma("sync", t_att, src_d, "d_dbg", wait=prev_done)

                stF = ExitStack()
                sbF = mksb(stF, side="right")
                w1sb = sbF(f"w1sb{li}", [128, 8 * 2816], BF16)
                w1gp = [sbF(f"w1gp{li}_{i}", [128, 8 * 256], BF16) for i in range(2)]
                w1gs = sbF(f"w1gs{li}", [128, 8 * 128], BF16)
                w1gr = sbF(f"w1gr{li}", [128, 8 * 128], BF16)
                w2p = [sbF(f"w2p{li}_{i}", [128, 2 * 1024], BF16) for i in range(3)]
                w2s = sbF(f"w2s{li}", [128, 1024], BF16)
                w2r = sbF(f"w2r{li}", [128, 1024], BF16)
                t_bf = [sbF(f"t_bf{li}_{i}", [128, 512], BF16) for i in range(NJC)]
                e_sb = sbF(f"e_sb{li}", [128, 512])
                xn2_bf = sbF(f"xn2_bf{li}", [128, 8 * 512], BF16)
                w1g_src = din["W1"][li].rearrange("(a p) n -> p a n", p=128)

                w1_toks = []
                for ic in range(8):
                    base = ic * 2816
                    w1_toks.append(pl.op("gpsimd", lambda e, base=base: e.memset(
                        w1sb[:, base + 2730:base + 2816], 0.0), wait=prev_done, inc="g"))
                    w1_toks.append(pl.dma("gpsimd", w1sb[:, base:base + 2730],
                                          din["W1"][li, ic * 128:(ic + 1) * 128, 0:2730],
                                          "d_w", wait=prev_done, slots=6))
                w1gz = pl.op("gpsimd", lambda e: e.memset(w1gr[:], 0.0),
                            wait=prev_done, inc="g")
                b1a_col, b1a_tok = colvec_ragged(f"b1a{li}", din["b1"][li, 0:DI], DI,
                                                 wait=prev_done, sba=sbF)
                b1g_col, b1g_tok = colvec_ragged(f"b1g{li}", din["b1"][li, DI:2 * DI], DI,
                                                 wait=prev_done, sba=sbF)
                b2_col, b2_tok = colvec_ragged(f"b2_{li}", din["b2"][li, :], C, wait=prev_done, sba=sbF)
                grow_toks = [pl.dma("sync", growf[0:1, :], din["ln_g"][li:li + 1, :], "d_lcv",
                                    wait=prev_done),
                             pl.dma("sync", growf[1:2, :], din["ln_b"][li:li + 1, :], "d_lcv",
                                    wait=prev_done)]
                grow_toks = [pl.op("vector", lambda e: e.tensor_copy(
                    out=growf_bf[:], in_=growf[:]), wait=grow_toks + prev_done, inc="v")]

                w2z = pl.op("gpsimd", lambda e: e.memset(w2r[:], 0.0),
                            wait=prev_done, inc="g")

                pl.save("ftile_prev", prev_done)
                pl.save("ftile_prev2", prev_done)
                pl.save("fsq_user_0", None); pl.save("fsq_user_1", None)
                for i3 in range(3):
                    pl.save(f"w2p_u_{i3}", prev_done)
                pl.save("w2s_u", prev_done)
                pl.save("w2r_u", [prev_done, w2z])
                for i2 in range(2):
                    pl.save(f"w1gp_u_{i2}", prev_done)
                pl.save("w1gs_u", prev_done)
                pl.save("w1gr_u", [prev_done, w1gz])

                for tile in range(NT):
                    s0 = tile * 512
                    tprev = pl.get("ftile_prev")
                    tprev2 = pl.get("ftile_prev2")
                    xfb = xf_in[tile % 2]
                    xf_toks = load_xf_tile(xfb, src_d, tile, [tprev2])
                    xs_toks = []
                    for c in range(8):
                        xs_toks.append(pl.op("vector", lambda e, c=c, xfb=xfb: e.tensor_copy(
                            out=xs_bf[:, 512 * c:512 * (c + 1)],
                            in_=xfb[:, sh0(c):sh0(c) + 512]),
                            wait=xf_toks + [pl.get("xsbf_u")], inc="v"))
                    smm = []
                    for c in range(8):
                        smm.append(pl.op("tensor", lambda e, c=c: e.matmul(
                            PS[5][0:1, :], onescol_bf[:], xs_bf[:, 512 * c:512 * (c + 1)],
                            start=(c == 0), stop=(c == 7)),
                            wait=[xs_toks[c]] + ([pl.bwait("PS5")] if c == 0 else []) +
                                 ffn_setup, inc="pe"))
                    s1 = pl.op("scalar", lambda e: e.copy(out=statA[:], in_=PS[5][0:1, :]),
                               wait=[smm[-1], tprev], inc="s")
                    pl.buse("PS5", s1)
                    sq2 = []
                    for c in range(8):
                        par = c % 2
                        sq = pl.op("scalar", lambda e, c=c, par=par: e.activation(
                            out=sqf_bf[:, 512 * par:512 * (par + 1)],
                            in_=xs_bf[:, 512 * c:512 * (c + 1)], func=AF.Square),
                            wait=[xs_toks[c], pl.get(f"fsq_user_{par}")], inc="s")
                        mm = pl.op("tensor", lambda e, c=c, par=par: e.matmul(
                            PS[5][0:1, :], onescol_bf[:],
                            sqf_bf[:, 512 * par:512 * (par + 1)],
                            start=(c == 0), stop=(c == 7)),
                            wait=[sq] + ([s1] if c == 0 else []), inc="pe")
                        pl.save(f"fsq_user_{par}", mm)
                        sq2.append(mm)
                    s2 = pl.op("scalar", lambda e: e.copy(out=statR[:], in_=PS[5][0:1, :]),
                               wait=[sq2[-1], tprev], inc="s")
                    pl.buse("PS5", s2)
                    pl.save("xsbf_u", pl.tok("pe"))
                    v1 = pl.op("vector", lambda e: e.tensor_scalar(
                        out=statA[:], in0=statA[:], scalar1=1.0 / C, scalar2=None,
                        op0=ALU.mult), wait=[s1, s2, tprev], inc="v")
                    v3b = pl.op("vector", lambda e: e.tensor_mul(
                        frt1[0:1, 0:512], statA[:], statA[:]), wait=[v1], inc="v")
                    v4 = pl.op("vector", lambda e: e.scalar_tensor_tensor(
                        out=statR[:], in0=statR[:], scalar=1.0 / C, in1=frt1[0:1, 0:512],
                        op0=ALU.mult, op1=ALU.subtract), wait=[v3b], inc="v")
                    v5 = pl.op("vector", lambda e: e.tensor_scalar(
                        out=statR[:], in0=statR[:], scalar1=1.0, scalar2=1e-5,
                        op0=ALU.mult, op1=ALU.add), wait=[v4], inc="v")
                    v6 = pl.op("vector", lambda e: e.reciprocal(
                        out=statR[:], in_=statR[:]), wait=[v5], inc="v")
                    v7 = pl.op("scalar", lambda e: e.activation(
                        out=statR_bf[:], in_=statR[:], func=AF.Sqrt),
                        wait=[v6, tprev], inc="s")
                    v8 = pl.op("vector", lambda e: e.tensor_mul(
                        statB2[0:1, :], statA[:], statR_bf[:]), wait=[v7, tprev], inc="v")
                    xn_toks2 = []
                    for c in range(8):
                        mA = pl.op("tensor", lambda e, c=c: e.matmul(
                            PS[2][:], growf_bf[0:1, 128 * c:128 * (c + 1)], statR_bf[:],
                            start=True, stop=True),
                            wait=[v7, pl.bwait("PS2")] + grow_toks, inc="pe")
                        mB = pl.op("tensor", lambda e, c=c: e.matmul(
                            PS[3][:], growf_bf[:, 128 * c:128 * (c + 1)], statB2[:],
                            start=True, stop=True),
                            wait=[v8, pl.bwait("PS3")] + grow_toks + ffn_setup, inc="pe")
                        t1_ = pl.op("vector", lambda e, c=c, xfb=xfb: e.tensor_mul(
                            frt2[:], xfb[:, sh0(c):sh0(c) + 512], PS[2][:]),
                            wait=[mA, pl.get("frt2_u")] + xf_toks, inc="v")
                        pl.buse("PS2", t1_)
                        t2_ = pl.op("vector", lambda e, c=c: e.tensor_sub(
                            xn2_bf[:, 512 * c:512 * (c + 1)], frt2[:], PS[3][:]),
                            wait=[t1_, mB, tprev], inc="v")
                        pl.buse("PS3", t2_)
                        pl.save("frt2_u", t2_)
                        xn_toks2.append(t2_)

                    t_toks = []
                    w1pieces = [("p", pi) for pi in range(10)] + [("s", 0), ("r", 0)]
                    for pc_ in w1pieces:
                        if pc_[0] == "p":
                            pi = pc_[1]
                            gbuf = w1gp[pi % 2]
                            gkey = f"w1gp_u_{pi % 2}"
                            cw = 256
                            gl = pl.dma("gpsimd",
                                        gbuf[:].rearrange("p (a c) -> p a c", c=256),
                                        w1g_src[:, :, DI + 256 * pi: DI + 256 * (pi + 1)],
                                        "d_w2", wait=[pl.get(gkey)], slots=5)
                            chunks = [(2 * pi, 0), (2 * pi + 1, 128)]
                        elif pc_[0] == "s":
                            gbuf = w1gs
                            gkey = "w1gs_u"
                            cw = 128
                            gl = pl.dma("gpsimd",
                                        gbuf[:].rearrange("p (a c) -> p a c", c=128),
                                        w1g_src[:, :, DI + 2560: DI + 2688],
                                        "d_w2", wait=[pl.get(gkey)], slots=5)
                            chunks = [(20, 0)]
                        else:
                            gbuf = w1gr
                            gkey = "w1gr_u"
                            cw = 128
                            gl = pl.dma("gpsimd",
                                        gbuf[:].rearrange("p (a c) -> p a c", c=128)[:, :, 0:42],
                                        w1g_src[:, :, DI + 2688: DI + 2730],
                                        "d_w2", wait=[pl.get(gkey)], slots=5)
                            chunks = [(21, 0)]
                        glast = []
                        for (jc, goff) in chunks:
                            bkA = f"PS{jc % 2}"
                            bankA = PS[jc % 2]

                            def mma(e, jc=jc, bankA=bankA):
                                prs = [(w1sb[:, ic * 2816 + 128 * jc: ic * 2816 + 128 * (jc + 1)],
                                        xn2_bf[:, 512 * ic:512 * (ic + 1)]) for ic in range(8)]
                                return em_mm_group(e, bankA[:], prs)
                            ma = pl.op("tensor", mma, wait=xn_toks2 + w1_toks + [pl.bwait(bkA)],
                                       inc="pe")
                            bkB = f"PS{2 + jc % 2}"
                            bankB = PS[2 + jc % 2]

                            def mmg(e, gbuf=gbuf, goff=goff, cw=cw, bankB=bankB):
                                prs = [(gbuf[:, cw * ic + goff: cw * ic + goff + 128],
                                        xn2_bf[:, 512 * ic:512 * (ic + 1)]) for ic in range(8)]
                                return em_mm_group(e, bankB[:], prs)
                            mg = pl.op("tensor", mmg, wait=xn_toks2 + [gl, pl.bwait(bkB)],
                                       inc="pe")
                            glast.append(mg)
                            ge = pl.op("scalar", lambda e, jc=jc, bankB=bankB: e.activation(
                                out=e_sb[:], in_=bankB[:], func=AF.Gelu,
                                bias=b1g_col[:, jc:jc + 1]),
                                wait=[mg, b1g_tok, pl.get("fe_u")], inc="s")
                            pl.buse(bkB, ge)
                            tb_ = pl.op("vector", lambda e, jc=jc, bankA=bankA:
                                        e.scalar_tensor_tensor(
                                            out=t_bf[jc][:], in0=bankA[:],
                                            scalar=b1a_col[:, jc:jc + 1],
                                            in1=e_sb[:], op0=ALU.add, op1=ALU.mult),
                                        wait=[ma, ge, b1a_tok, tprev], inc="v")
                            pl.buse(bkA, tb_)
                            pl.save("fe_u", tb_)
                            t_toks.append(tb_)
                        pl.save(gkey, glast)

                    accbanks = [("PS4", PS[4]), ("PS5", PS[5]), ("PS2", PS[2]), ("PS3", PS[3])]
                    w2pieces = [("p", pi) for pi in range(10)] + [("s", 0), ("r", 0)]
                    for grp in range(2):
                        grp_mms = {oo: [] for oo in range(4)}
                        for pc_ in w2pieces:
                            if pc_[0] == "p":
                                pi = pc_[1]
                                wt = w2p[pi % 3]
                                wkey = f"w2p_u_{pi % 3}"
                                wl = pl.dma("gpsimd",
                                            wt[:].rearrange("p (a n) -> p a n", n=1024),
                                            din["W2"][li, 256 * pi:256 * (pi + 1), :]
                                            .rearrange("(a p) n -> p a n", p=128),
                                            "d_w2", wait=[pl.get(wkey)], slots=5)
                                chunks = [(2 * pi, 0), (2 * pi + 1, 1024)]
                            elif pc_[0] == "s":
                                wt = w2s
                                wkey = "w2s_u"
                                if grp == 0:
                                    wl = pl.dma("gpsimd", wt[:],
                                                din["W2"][li, 2560:2688, :], "d_w2",
                                                wait=[pl.get(wkey)], slots=5)
                                else:
                                    wl = pl.get(wkey)
                                chunks = [(20, 0)]
                            else:
                                wt = w2r
                                wkey = "w2r_u"
                                if grp == 0:
                                    wl = pl.dma("gpsimd", wt[0:42, :],
                                                din["W2"][li, 2688:2730, :], "d_w2",
                                                wait=[pl.get(wkey)], slots=5)
                                else:
                                    wl = pl.get(wkey)
                                chunks = [(21, 0)]
                            mlast = []
                            for (jc, woff) in chunks:
                                for oo in range(4):
                                    oc = 4 * grp + oo
                                    bkn, bb = accbanks[oo]
                                    mm = pl.op("tensor",
                                               lambda e, jc=jc, oc=oc, wt=wt, woff=woff,
                                               bb=bb: e.matmul(
                                                   bb[:],
                                                   wt[:, woff + 128 * oc:woff + 128 * (oc + 1)],
                                                   t_bf[jc][:],
                                                   start=(jc == 0), stop=(jc == NJC - 1)),
                                               wait=[wl, t_toks[jc]] +
                                                    ([pl.bwait(bkn)] if jc == 0 else []),
                                               inc="pe")
                                    grp_mms[oo].append(mm)
                                    mlast.append(mm)
                            pl.save(wkey, mlast)
                        for oo in range(4):
                            oc = 4 * grp + oo
                            bkn, bb = accbanks[oo]
                            stg = xf_st[oo % 2]
                            ad = pl.op("vector", lambda e, oc=oc, bb=bb, stg=stg, xfb=xfb:
                                       e.scalar_tensor_tensor(
                                           out=stg[:], in0=bb[:], scalar=b2_col[:, oc:oc + 1],
                                           in1=xfb[:, 514 * oc + 1:514 * oc + 513],
                                           op0=ALU.add, op1=ALU.add),
                                       wait=[grp_mms[oo][-1], b2_tok,
                                             pl.get(f"fxfst_u{oo % 2}")], inc="v")
                            pl.buse(bkn, ad)
                            dw = pl.dma("sync", dst_d[oc, :, s0:s0 + 512], stg[:], "d_xfw",
                                        wait=[ad], slots=4)
                            pl.save(f"fxfst_u{oo % 2}", dw)
                    pl.save(f"f2xf_user_{tile}", xs_toks[-1])
                    pl.save("ftile_prev2", pl.get("ftile_prev"))
                    pl.save("ftile_prev", [pl.tok("pe"), pl.tok("v"), pl.tok("s")])

                sub_done = [pl.tok("pe"), pl.tok("v"), pl.tok("s")] + pl.chan_toks("d_xfw", 4)
                stF.close()
                prev_done = sub_done
                src_d = dst_d
                dst_d = xf_b if dst_d is xf_a else xf_a
                t_ffn = tap(f"fffn{li}", (8, 128, S))
                if t_ffn is not None:
                    pl.dma("sync", t_ffn, src_d, "d_dbg", wait=prev_done)

            fin_acc = sbf("fin_acc", [128, C])
            fin_cm = sbf("fin_cm", [128, C])
            fin_out = sbf("fin_out", [128, C])
            for tt in range(S // 128):
                la = pl.dma("sync", fin_acc[:], acc_d[tt * 128:(tt + 1) * 128, :], "d_fx",
                            wait=prev_done + [pl.get("finacc_u")])
                lc = [pl.dma("sync", fin_cm[:, c * 128:(c + 1) * 128],
                             src_d[c, :, tt * 128:(tt + 1) * 128], "d_fx2",
                             wait=prev_done + [pl.get("fincm_u")], slots=4) for c in range(8)]
                adds = []
                for c in range(8):
                    tr = pl.op("tensor", lambda e, c=c: e.transpose(
                        out=PS[4][:, 0:128], in_=fin_cm[:, c * 128:(c + 1) * 128],
                        identity=CS["eye"][:]),
                        wait=lc + [pl.bwait("PS4")] + prev_done, inc="pe")
                    ad = pl.op("vector", lambda e, c=c: e.tensor_add(
                        fin_out[:, c * 128:(c + 1) * 128], PS[4][:, 0:128],
                        fin_acc[:, c * 128:(c + 1) * 128]),
                        wait=[tr, la, pl.get("finout_u")], inc="v")
                    pl.buse("PS4", ad)
                    adds.append(ad)
                pl.save("finacc_u", adds[-1])
                pl.save("fincm_u", adds[-1])
                dy = pl.dma("sync", y_h[tt * 128:(tt + 1) * 128, :], fin_out[:], "d_y",
                            wait=adds, slots=4)
                pl.save("finout_u", dy)

        pl.final_drain()
        block = st.enter_context(nc.Block())
        pl.emit(block)
    return nc, dbg_out


def shard_inputs(inputs, core):
    HC = host_constants()
    HT = host_tables()
    m = {k: np.ascontiguousarray(v) for k, v in {
        "x": inputs["L0_attn_x"][core],
        "conv_k": inputs["L0_attn_conv_k"], "conv_b": inputs["L0_attn_conv_b"],
        "rff_w": inputs["L0_attn_rff_w"], "rff_b": inputs["L0_attn_rff_b"],
        "proj_w": inputs["L0_attn_proj_w"], "proj_b": inputs["L0_attn_proj_b"],
        "omega": inputs["L0_attn_omega"], "landmarks": inputs["L0_attn_landmarks"],
        "out_w": inputs["L0_attn_out_w"], "out_b": inputs["L0_attn_out_b"],
        "Wqkv": inputs["L0_ffn_Wqkv"], "q_scale": inputs["L0_ffn_q_scale"],
        "k_scale": inputs["L0_ffn_k_scale"], "Wo": inputs["L0_ffn_Wo"],
        "ln_g": inputs["L0_ffn_ln_g"], "ln_b": inputs["L0_ffn_ln_b"],
        "W1": inputs["L0_ffn_W1"], "b1": inputs["L0_ffn_b1"],
        "W2": inputs["L0_ffn_W2"], "b2": inputs["L0_ffn_b2"],
    }.items()}
    for k, v in HC.items():
        m["c_" + k] = v
    import ml_dtypes
    for k, v in HT.items():
        m["t_" + k] = v.astype(ml_dtypes.bfloat16)
    return m


def kernel(**inputs):
    nc, _ = build_program()
    in_maps = [shard_inputs(inputs, core) for core in range(8)]
    res = run_bass_kernel_spmd(nc, in_maps, core_ids=list(range(8)))
    return np.stack([np.asarray(r["y"]) for r in res.results], axis=0)
